# version 01
# baseline (speedup 1.0000x reference)
import numpy as np
import ml_dtypes
import concourse.bass as bass
import concourse.mybir as mybir
from concourse.bass_utils import run_bass_kernel_spmd
from contextlib import ExitStack

F32 = mybir.dt.float32
BF16 = mybir.dt.bfloat16
U8 = mybir.dt.uint8
AF = mybir.ActivationFunctionType
ALU = mybir.AluOpType
AX = mybir.AxisListType
BF = ml_dtypes.bfloat16

NCORES = 8
B = 2
T = 2048
D = 1024
DT = D // 128

SB_BYTES = 206 * 1024
BUCKET = 4096


class Reg:
    __slots__ = ("space", "p0", "p1", "b0", "b1")

    def __init__(self, space, p0, p1, b0, b1):
        self.space, self.p0, self.p1, self.b0, self.b1 = space, p0, p1, b0, b1


class Tile:
    __slots__ = ("ap", "space", "p0", "byte0", "dtype", "P", "N", "esz")

    def __init__(self, ap, space, p0, byte0, dtype, P, N):
        self.ap, self.space, self.p0, self.byte0 = ap, space, p0, byte0
        self.dtype, self.P, self.N = dtype, P, N
        self.esz = mybir.dt.size(dtype)

    def __getitem__(self, idx):
        ps, es = idx
        p0, p1 = (ps.start or 0), (ps.stop if ps.stop is not None else self.P)
        e0 = es.start or 0
        e1 = es.stop if es.stop is not None else self.N
        step = es.step or 1
        ap = self.ap[p0:p1, e0:e1:step] if step != 1 else self.ap[p0:p1, e0:e1]
        return ap, self.reg(p0, p1, e0, e1)

    def reg(self, p0=0, p1=None, e0=0, e1=None):
        p1 = self.P if p1 is None else p1
        e1 = self.N if e1 is None else e1
        return Reg(self.space, self.p0 + p0, self.p0 + p1,
                   self.byte0 + e0 * self.esz, self.byte0 + e1 * self.esz)

    @property
    def full(self):
        return self.ap, self.reg()


def dreg(name, lo=0, hi=1 << 62):
    return Reg("D:" + name, 0, 128, lo, hi)


class MK:
    ENGS = ("pe", "act", "dve", "pool", "sp")

    def __init__(self, nc, stack):
        self.nc = nc
        self.S = stack
        self.arena = stack.enter_context(nc.sbuf_tensor("AR", [128, SB_BYTES], U8))
        self.psA = stack.enter_context(nc.psum_tensor("PSA", [128, 3584], F32))
        self.psT = stack.enter_context(nc.psum_tensor("PST", [128, 1024], BF16))
        self.NLANE = 4
        self.sem = {}
        for s in ("pe", "act", "dve", "pool", "sp"):
            self.sem[s] = stack.enter_context(nc.semaphore("s_" + s))
        for q in ("sq", "pq"):
            for l in range(self.NLANE):
                self.sem[q + str(l)] = stack.enter_context(nc.semaphore("s_" + q + str(l)))
        self.count = {s: 0 for s in self.sem}
        self.qidx = {"sq": 0, "pq": 0}
        self.vc = {e: {} for e in self.ENGS}
        self.tasks = {e: [] for e in self.ENGS}
        self.acc = {}
        self.off = 0
        self.ph_off = None
        self.n_waits = 0
        self.n_tasks = 0
        self.serial = set()
        self.same_engine_sync = True

    def _carve(self, P, N, dtype, at):
        esz = mybir.dt.size(dtype)
        o = (at + 31) // 32 * 32
        nb = N * esz
        assert o + nb <= SB_BYTES, f"SBUF arena overflow: {o+nb}"
        ap = self.arena[0:P, o:o + nb].bitcast(dtype)
        return Tile(ap, "S", 0, o, dtype, P, N), o + nb

    def carve(self, P, N, dtype):
        t, self.off = self._carve(P, N, dtype, self.off)
        assert self.ph_off is None, "carve() after phase allocs started"
        return t

    def ph_reset(self):
        self.ph_off = self.off

    def mark(self):
        return self.ph_off

    def reset_to(self, m):
        self.ph_off = m

    def ph(self, P, N, dtype):
        if self.ph_off is None:
            self.ph_off = self.off
        t, self.ph_off = self._carve(P, N, dtype, self.ph_off)
        return t

    def psum(self, bank, nbanks=1):
        assert bank + nbanks <= 7
        ap = self.psA[:, 512 * bank:512 * (bank + nbanks)]
        return Tile(ap, "P", 0, 512 * 4 * bank, F32, 128, 512 * nbanks)

    def psumT(self):
        return Tile(self.psT[:, :], "T", 0, 0, BF16, 128, 1024)

    @staticmethod
    def _buckets(reg):
        if reg.space in ("P", "T"):
            lo = reg.b0 // 2048
            hi = (reg.b1 - 1) // 2048
            return [(reg.space, i) for i in range(lo, hi + 1)]
        if reg.space == "S":
            lo = reg.b0 // BUCKET
            hi = (reg.b1 - 1) // BUCKET
            return [("S", i) for i in range(lo, hi + 1)]
        return [(reg.space, 0)]

    @staticmethod
    def _overlap(a, b):
        if a.space != b.space:
            return False
        if a.space in ("P", "T"):
            return (a.b0 // 2048 <= (b.b1 - 1) // 2048
                    and b.b0 // 2048 <= (a.b1 - 1) // 2048)
        return a.p0 < b.p1 and b.p0 < a.p1 and a.b0 < b.b1 and b.b0 < a.b1

    def _conflicts(self, reg, kind, waits, self_stream):
        for bk in self._buckets(reg):
            for (r2, k2, s2, v2) in self.acc.get(bk, ()):
                if k2 == "r" and kind == "r":
                    continue
                if s2 == self_stream:
                    continue
                if self._overlap(reg, r2):
                    if waits.get(s2, 0) < v2:
                        waits[s2] = v2

    def _record(self, reg, kind, stream, val):
        for bk in self._buckets(reg):
            lst = self.acc.setdefault(bk, [])
            if kind == "w":
                lst[:] = [a for a in lst if not (
                    reg.space == a[0].space and reg.p0 <= a[0].p0 and reg.p1 >= a[0].p1
                    and reg.b0 <= a[0].b0 and reg.b1 >= a[0].b1)]
            lst.append((reg, kind, stream, val))
            if len(lst) > 600:
                mx = {}
                for (r2, k2, s2, v2) in lst:
                    mx[s2] = max(mx.get(s2, 0), v2)
                sp = lst[0][0].space
                lst[:] = [(Reg(sp, 0, 128, 0, 1 << 62), "w", s2, v2)
                          for s2, v2 in mx.items()]

    @staticmethod
    def _regs(lst):
        out = []
        for x in (lst or ()):
            if isinstance(x, Tile):
                out.append(x.reg())
            elif isinstance(x, tuple):
                out.append(x[1])
            else:
                out.append(x)
        return out

    def t(self, eng, emit, r=None, w=None, inc=1, tag="", rs=None):
        rr, ww = self._regs(r), self._regs(w)
        waits = {}
        skip = None if self.same_engine_sync else eng
        for reg in rr:
            self._conflicts(reg, "r", waits, skip)
        for reg in ww:
            self._conflicts(reg, "w", waits, skip)
        strict = self._regs(rs) if rs else []
        for reg in strict:
            self._conflicts(reg, "r", waits, None)
        if strict:
            rr = rr + strict
        if eng in self.serial:
            for s2 in self.count:
                if s2 != eng and self.count[s2] > 0:
                    waits[s2] = self.count[s2]
        need = []
        for s, v in waits.items():
            if self.vc[eng].get(s, 0) < v:
                need.append((s, v))
                self.vc[eng][s] = v
        self.count[eng] += inc
        val = self.count[eng]
        for reg in rr:
            self._record(reg, "r", eng, val)
        for reg in ww:
            self._record(reg, "w", eng, val)
        sem = self.sem
        myname = eng
        self.n_waits += len(need)
        self.n_tasks += 1
        import sys as _sys
        fr = _sys._getframe(1)
        loc = f"{fr.f_code.co_name}:{fr.f_lineno}"
        fr2 = fr.f_back
        if fr2 is not None:
            loc += f"<{fr2.f_code.co_name}:{fr2.f_lineno}"

        def run(engine, need=need, emit=emit, val=val, inc=inc, tag=tag, loc=loc):
            for s, v in need:
                engine.wait_ge(sem[s], v)
            try:
                ins = emit(engine)
            except Exception as ex:
                raise RuntimeError(f"task emit failed at {loc} tag={tag}: {ex}") from ex
            ins.then_inc(sem[myname], inc)
        self.tasks[eng].append(run)
        return val

    def dma(self, q, out, in_, r=None, w=None, tag=""):
        qname = "sq" if q == "sp" else "pq"
        eng = "sp" if q == "sp" else "pool"
        lane = self.qidx[qname] % self.NLANE
        self.qidx[qname] += 1
        qs = qname + str(lane)

        def ap_of(x):
            if isinstance(x, Tile):
                return x.ap
            if isinstance(x, tuple):
                return x[0]
            return x
        out_ap, in_ap = ap_of(out), ap_of(in_)
        rr = self._regs([in_] if not isinstance(in_, list) else in_)
        ww = self._regs([out] if not isinstance(out, list) else out)
        waits = {}
        for reg in rr:
            self._conflicts(reg, "r", waits, None)
        for reg in ww:
            self._conflicts(reg, "w", waits, None)
        if ("dma" in self.serial) or (eng in self.serial):
            for s2 in self.count:
                if s2 != qs and self.count[s2] > 0:
                    waits[s2] = self.count[s2]
        if self.count[qs] > waits.get(qs, 0):
            waits[qs] = self.count[qs]
        need = []
        for s, v in waits.items():
            if self.vc[eng].get(s, 0) < v:
                need.append((s, v))
                self.vc[eng][s] = v
        self.count[qs] += 16
        val = self.count[qs]
        for reg in rr:
            self._record(reg, "r", qs, val)
        for reg in ww:
            self._record(reg, "w", qs, val)
        sem = self.sem
        self.n_tasks += 1
        self.n_waits += len(need)

        def run(engine, need=need, val=val):
            for s, v in need:
                engine.wait_ge(sem[s], v)
            engine.dma_start(out=out_ap, in_=in_ap).then_inc(sem[qs], 16)
        self.tasks[eng].append(run)
        return val

    def finish(self, block):
        fin = [(s, self.count[s]) for s in self.count if self.count[s] > 0]

        def last(engine):
            for s, v in fin:
                if s != "sp":
                    engine.wait_ge(self.sem[s], v)
        self.tasks["sp"].append(last)
        engmap = {"pe": block.tensor, "act": block.scalar, "dve": block.vector,
                  "pool": block.gpsimd, "sp": block.sync}
        for eng in self.ENGS:
            tl = self.tasks[eng]
            if not tl:
                continue

            def body(engine, tl=tl):
                for fn in tl:
                    fn(engine)
            engmap[eng](body)


_VEC_REG = []


def _vec_layout():
    if _VEC_REG:
        return _VEC_REG
    def add(name, ncols):
        _VEC_REG.append((name, ncols))
    add("l0_ba", 8); add("l0_bx", 8); add("l0_lmbd", 8)
    add("l0f_b1", 32); add("l0f_b2", 8); add("l0f_gamma", 8); add("l0f_beta", 8)
    add("l1a_bout", 8)
    for l in range(2):
        add(f"l1f_lng{l}", 8); add(f"l1f_lnb{l}", 8)
        add(f"l1f_b1a{l}", 22); add(f"l1f_b1g{l}", 22); add(f"l1f_b2_{l}", 8)
        add(f"l1f_qs{l}", 1); add(f"l1f_qss{l}", 1)
        add(f"l1f_ks{l}", 1); add(f"l1f_kss{l}", 1)
    add("l2a_bck", 1); add("l2a_bcv", 1); add("l2a_bg", 1)
    add("l2f_gateb", 1)
    for e in range(8):
        add(f"l2f_b1_{e}", 16); add(f"l2f_b2_{e}", 16)
        add(f"l2f_bg_{e}", 16); add(f"l2f_b3_{e}", 8)
    return _VEC_REG


def _vec_cols():
    lay = _vec_layout()
    off, out = 0, {}
    for name, n in lay:
        out[name] = (off, n)
        off += n
    return out, off


def _col128(v):
    v = np.asarray(v, np.float32).reshape(-1)
    assert v.size % 128 == 0 or v.size <= 128, v.size
    if v.size % 128:
        v = np.pad(v, (0, 128 - v.size % 128))
    return v.reshape(-1, 128).T


def _pad_col(v, n=128):
    v = np.asarray(v, np.float32).reshape(-1)
    return np.pad(v, (0, n - v.size))[:, None]


def stage_vecs(I):
    cols, NV = _vec_cols()
    V = np.zeros((128, NV), np.float32)

    def put(name, arr):
        o, n = cols[name]
        arr = np.asarray(arr, np.float32)
        assert arr.shape == (128, n) or (n == 1 and arr.shape == (128, 1)), (name, arr.shape)
        V[:, o:o + n] = arr
    put("l0_ba", _col128(I["L0_attn_ba"]))
    put("l0_bx", _col128(I["L0_attn_bx"]))
    put("l0_lmbd", _col128(I["L0_attn_lmbd"]))
    put("l0f_b1", _col128(I["L0_ffn_b1"][0]))
    put("l0f_b2", _col128(I["L0_ffn_b2"][0]))
    put("l0f_gamma", _col128(I["L0_ffn_gamma"]))
    put("l0f_beta", _col128(I["L0_ffn_beta"]))
    put("l1a_bout", _col128(I["L1_attn_bout"]))
    HID = 2730
    for l in range(2):
        put(f"l1f_lng{l}", _col128(I["L1_ffn_ln_g"][l]))
        put(f"l1f_lnb{l}", _col128(I["L1_ffn_ln_b"][l]))
        b1 = np.asarray(I["L1_ffn_b1"][l], np.float32)
        b1a = np.pad(b1[:HID], (0, 22 * 128 - HID))
        b1g = np.pad(b1[HID:], (0, 22 * 128 - HID))
        put(f"l1f_b1a{l}", b1a.reshape(22, 128).T)
        put(f"l1f_b1g{l}", b1g.reshape(22, 128).T)
        put(f"l1f_b2_{l}", _col128(I["L1_ffn_b2"][l]))
        qs = np.asarray(I["L1_ffn_q_scale"][l], np.float32)
        ks = np.asarray(I["L1_ffn_k_scale"][l], np.float32)
        sig = np.concatenate([qs[32:], qs[:32]])
        sigk = np.concatenate([ks[32:], ks[:32]])
        put(f"l1f_qs{l}", _pad_col(np.tile(qs, 2)))
        put(f"l1f_qss{l}", _pad_col(np.tile(sig, 2)))
        put(f"l1f_ks{l}", _pad_col(np.tile(ks, 2)))
        put(f"l1f_kss{l}", _pad_col(np.tile(sigk, 2)))
    put("l2a_bck", _pad_col(I["L2_attn_bck"]))
    put("l2a_bcv", _pad_col(I["L2_attn_bcv"]))
    put("l2a_bg", _pad_col(I["L2_attn_bg"]))
    put("l2f_gateb", _pad_col(I["L2_ffn_gate_b"]))
    for e in range(8):
        put(f"l2f_b1_{e}", _col128(I["L2_ffn_b1"][e]))
        put(f"l2f_b2_{e}", _col128(I["L2_ffn_b2"][e]))
        put(f"l2f_bg_{e}", _col128(I["L2_ffn_bg"][e]))
        put(f"l2f_b3_{e}", _col128(I["L2_ffn_b3"][e]))
    return np.ascontiguousarray(V)


def stage_toeplitz(filt):
    f = np.asarray(filt, np.float32)
    idx = np.arange(T)[:, None] - np.arange(T)[None, :]
    Tm = np.where((idx >= 0) & (idx < f.size), f[np.clip(idx, 0, f.size - 1)], 0.0)
    return np.ascontiguousarray(Tm.astype(np.float32))


def stage_inputs(I, core):
    m = {}
    x = np.asarray(I["L0_attn_x"], np.float32)[2 * core:2 * core + 2]
    m["x_fm"] = np.ascontiguousarray(x.transpose(0, 2, 1))
    m["vecs"] = stage_vecs(I)
    m["wa_t"] = np.ascontiguousarray(np.asarray(I["L0_attn_Wa"], np.float32).T)
    m["wx_t"] = np.ascontiguousarray(np.asarray(I["L0_attn_Wx"], np.float32).T)
    m["l0f_w1"] = np.ascontiguousarray(np.asarray(I["L0_ffn_w1"], np.float32)[0])
    m["l0f_w2"] = np.ascontiguousarray(np.asarray(I["L0_ffn_w2"], np.float32)[0])
    m["l1a_wq"] = np.asarray(I["L1_attn_Wq"], np.float32)
    m["l1a_wkv"] = np.asarray(I["L1_attn_Wkv"], np.float32)
    m["l1a_wout"] = np.asarray(I["L1_attn_Wout"], np.float32)
    m["l1f_wqkv"] = np.asarray(I["L1_ffn_wqkv"], np.float32)
    m["l1f_wo"] = np.asarray(I["L1_ffn_wo"], np.float32)
    m["l1f_w1"] = np.asarray(I["L1_ffn_w1"], np.float32)
    m["l1f_w2"] = np.asarray(I["L1_ffn_w2"], np.float32)
    m["l2a_wq"] = np.asarray(I["L2_attn_Wq"], np.float32)
    m["l2a_wk"] = np.asarray(I["L2_attn_Wk"], np.float32)
    m["l2a_wv"] = np.asarray(I["L2_attn_Wv"], np.float32)
    m["l2a_wo"] = np.asarray(I["L2_attn_Wo"], np.float32)
    m["l2a_wck"] = np.asarray(I["L2_attn_Wck"], np.float32)
    m["l2a_wcv"] = np.asarray(I["L2_attn_Wcv"], np.float32)
    m["l2a_wg"] = np.ascontiguousarray(
        np.pad(np.asarray(I["L2_attn_Wg"], np.float32), ((0, 0), (0, 1))))
    m["l2f_gw"] = np.asarray(I["L2_ffn_gate_w"], np.float32)
    m["l2f_w1"] = np.asarray(I["L2_ffn_w1"], np.float32)
    m["l2f_w2"] = np.asarray(I["L2_ffn_w2"], np.float32)
    m["l2f_wg"] = np.asarray(I["L2_ffn_wg"], np.float32)
    m["l2f_w3"] = np.asarray(I["L2_ffn_w3"], np.float32)
    m["l3_toep"] = stage_toeplitz(I["L3_attn_filt"])
    m["l3_wgate"] = np.asarray(I["L3_ffn_w_gate"], np.float32)
    m["l3_wup"] = np.asarray(I["L3_ffn_w_up"], np.float32)
    m["l3_wdown"] = np.asarray(I["L3_ffn_w_down"], np.float32)
    return m


def _rot_tables():
    DH, WIN = 64, 512
    half = np.arange(0, DH, 2, dtype=np.float64)
    inv_freq = 1.0 / 10000.0 ** (half / DH)
    t = np.arange(T, dtype=np.float64)
    freqs = t[:, None] * inv_freq[None, :]
    freqs = np.concatenate([freqs, freqs], axis=1)
    base = (half + 0.4 * DH) / (1.4 * DH)
    power = (t - T // 2) / float(WIN // 2)
    scale = base[None, :] ** power[:, None]
    scale = np.concatenate([scale, scale], axis=1)
    cos, sin = np.cos(freqs), np.sin(freqs)
    cs_q = (cos * scale).T.astype(np.float32)
    ss_q = (sin * scale).T.astype(np.float32)
    cs_k = (cos / scale).T.astype(np.float32)
    ss_k = (sin / scale).T.astype(np.float32)
    tile2 = lambda a: np.ascontiguousarray(np.tile(a, (2, 1)))
    return tile2(cs_q), tile2(ss_q), tile2(cs_k), tile2(ss_k)


def _rot_perm():
    R = np.zeros((128, 128), np.float32)
    for h in range(2):
        o = 64 * h
        for i in range(32):
            R[o + 32 + i, o + i] = -1.0
            R[o + i, o + 32 + i] = 1.0
    return np.ascontiguousarray(R)


def _l1_masksT():
    j = np.arange(1024)[:, None]
    i = np.arange(512)[None, :]
    ok = (j >= i) & (j <= i + 512)
    m1 = np.where(ok, 0.0, -1e38).astype(np.float32)
    m0 = np.where(ok & (j >= 512), 0.0, -1e38).astype(np.float32)
    return m0, m1


def _l2_E():
    E = np.zeros((127, T), np.float32)
    for n in range(127):
        for c in range(64):
            p = min(16 * n + c, T - 1)
            E[n, p] += 1.0
    return E


class Prog:
    def __init__(self, nc, S, n_blocks=4):
        self.nc = nc
        self.n_blocks = n_blocks
        self.B = MK(nc, S)
        self.cols, NV = _vec_cols()
        d = nc.dram_tensor
        self.inp = {}
        for name, shape in [
            ("x_fm", [B, D, T]), ("vecs", [128, NV]),
            ("wa_t", [D, D]), ("wx_t", [D, D]),
            ("l0f_w1", [D, 4096]), ("l0f_w2", [4096, D]),
            ("l1a_wq", [D, D]), ("l1a_wkv", [D, D]), ("l1a_wout", [D, D]),
            ("l1f_wqkv", [2, D, 1536]), ("l1f_wo", [2, 512, D]),
            ("l1f_w1", [2, D, 5460]), ("l1f_w2", [2, 2730, D]),
            ("l2a_wq", [D, D]), ("l2a_wk", [D, D]), ("l2a_wv", [D, D]),
            ("l2a_wo", [D, D]), ("l2a_wck", [2048, 64]), ("l2a_wcv", [2048, 64]),
            ("l2a_wg", [64, 4]),
            ("l2f_gw", [D, 8]), ("l2f_w1", [8, D, 2048]), ("l2f_w2", [8, D, 2048]),
            ("l2f_wg", [8, 2048, 2048]), ("l2f_w3", [8, 2048, D]),
            ("l3_toep", [T, T]), ("l3_wgate", [2816, D]), ("l3_wup", [2816, D]),
            ("l3_wdown", [D, 2816]),
        ]:
            self.inp[name] = d(name, shape, F32, kind="ExternalInput").ap()
        self.xout = d("xout", [B, D, T], F32, kind="ExternalOutput").ap()
        self.X = d("Xs", [B, DT, 128, T], F32, kind="Internal").ap()
        self.XNB = d("XNBs", [B, DT, 128, T], BF16, kind="Internal").ap()
        self.XN32 = d("XN32s", [B, DT, 128, T], F32, kind="Internal").ap()
        self.W3Q = d("W3Q", [3, 2816 * D], BF16, kind="Internal").ap()
        self.ZS = d("ZSs", [B, DT, 128, T], F32, kind="Internal").ap()
        it = nc.inline_tensor
        self.c_identb = it(np.eye(128, dtype=BF), "c_identb").ap()
        self.c_identf = it(np.eye(128, dtype=np.float32), "c_identf").ap()
        self.c_ones_col_b = it(np.ones((128, 1), dtype=BF), "c_ones_col_b").ap()
        self.c_ones_row_f = it(np.ones((1, 128), dtype=np.float32), "c_ones_row_f").ap()
        self.c_ones_row_b = it(np.ones((1, 128), dtype=BF), "c_ones_row_b").ap()
        self.c_rperm = it(_rot_perm().astype(BF), "c_rperm").ap()
        self._lazy = {}
        Bld = self.B
        self.vt = Bld.carve(128, NV, F32)
        self.lsg8 = Bld.carve(128, 8, F32)
        self.identb = Bld.carve(128, 128, BF16)
        self.identf = Bld.carve(128, 128, F32)
        self.ones_col_b = Bld.carve(128, 1, BF16)
        self.ones_row_f = Bld.carve(1, 128, F32)
        self.ones_row_b = Bld.carve(1, 128, BF16)
        self.rperm = Bld.carve(128, 128, BF16)
        self.cst = Bld.carve(128, 32, F32)
        self.g64b = Bld.carve(128, 2, BF16)
        self.g64rep = Bld.carve(2, 128, F32)
        self.cmap = {}

    def kf(self, v):
        v = float(v)
        if v not in self.cmap:
            col = len(self.cmap)
            assert col < 32
            self.cmap[v] = col
            ap = self.cst.ap[:, col:col + 1]
            self.nc.const_aps.register(F32, v, ap)
            reg = self.cst.reg(0, 128, col, col + 1)
            self.B.t("dve", lambda e, ap=ap, v=v: e.memset(ap, v), w=[reg])
        col = self.cmap[v]
        return self.cst.reg(0, 128, col, col + 1)

    def lazy_const(self, name, fn):
        if name not in self._lazy:
            self._lazy[name] = self.nc.inline_tensor(fn(), name).ap()
        return self._lazy[name]

    def dbg(self, name, tile, P_=128, N=None, dt=F32):
        if isinstance(tile, Tile):
            ap, reg = tile.full
            P_, N, dt = tile.P, tile.N, tile.dtype
        else:
            ap, reg = tile
        o = self.nc.dram_tensor("dbg_" + name, [P_, N], dt, kind="ExternalOutput").ap()
        self.B.dma("sp", (o, dreg("dbg_" + name)), (ap, reg))

    def vcol(self, name, j0=0, n=1, P=128):
        o, nc_ = self.cols[name]
        assert j0 + n <= nc_, (name, j0, n)
        return self.vt[0:P, o + j0:o + j0 + n]

    @staticmethod
    def _tl(x):
        return x.full if isinstance(x, Tile) else x

    def tt(self, op, out, a, b, eng="dve"):
        out, a, b = self._tl(out), self._tl(a), self._tl(b)
        self.B.t(eng, lambda e: e.tensor_tensor(out[0], a[0], b[0], op=op),
                 r=[a, b], w=[out])

    def ts(self, out, a, s1, s2=None, op0=ALU.mult, op1=ALU.bypass, eng="dve"):
        out, a = self._tl(out), self._tl(a)
        s1, s2 = self._tl(s1), self._tl(s2)
        rr = [a]
        rss = []
        s1v = s1[0] if isinstance(s1, tuple) else s1
        s2v = s2[0] if isinstance(s2, tuple) else s2
        if isinstance(s1, tuple):
            rss.append(s1)
        if isinstance(s2, tuple):
            rss.append(s2)
        if s2 is None:
            self.B.t(eng, lambda e: e.tensor_single_scalar(out[0], a[0], s1v, op=op0),
                     r=rr, w=[out], rs=rss)
        else:
            self.B.t(eng, lambda e: e.tensor_scalar(out[0], a[0], s1v, s2v,
                                                    op0=op0, op1=op1),
                     r=rr, w=[out], rs=rss)

    def stt(self, out, in0, scalar, in1, op0, op1, eng="dve"):
        out, in0, in1, scalar = self._tl(out), self._tl(in0), self._tl(in1), self._tl(scalar)
        rr = [in0, in1]
        rss = []
        sv = scalar[0] if isinstance(scalar, tuple) else scalar
        if isinstance(scalar, tuple):
            rss.append(scalar)
        self.B.t(eng, lambda e: e.scalar_tensor_tensor(out[0], in0[0], sv, in1[0],
                                                       op0=op0, op1=op1),
                 r=rr, w=[out], rs=rss)

    def copy(self, out, a, eng="dve"):
        out, a = self._tl(out), self._tl(a)
        self.B.t(eng, lambda e: e.tensor_copy(out[0], a[0]), r=[a], w=[out])

    def act(self, out, a, func, bias=0.0, scale=1.0, accum=None, eng="act"):
        out, a, accum = self._tl(out), self._tl(a), self._tl(accum)
        bias, scale = self._tl(bias), self._tl(scale)
        rr = [a]
        rss = []
        bv = bias[0] if isinstance(bias, tuple) else bias
        sv = scale[0] if isinstance(scale, tuple) else scale
        if isinstance(bias, tuple):
            rss.append(bias)
        elif isinstance(bias, float) and bias != 0.0:
            rss.append(self.kf(bias))
        if isinstance(scale, tuple):
            rss.append(scale)
        ww = [out]
        kw = {}
        if accum is not None:
            kw["accum_out"] = accum[0]
            ww = [out, accum]
        self.B.t(eng, lambda e: e.activation(out[0], a[0], func, bias=bv, scale=sv, **kw),
                 r=rr, w=ww, rs=rss)

    def red(self, out, a, op=ALU.max, absval=False, eng="dve"):
        out, a = self._tl(out), self._tl(a)
        self.B.t(eng, lambda e: e.tensor_reduce(out[0], a[0], axis=AX.X, op=op,
                                                apply_absolute_value=absval),
                 r=[a], w=[out])

    def memset(self, out, v, eng="dve"):
        out = self._tl(out)
        self.B.t(eng, lambda e: e.memset(out[0], v), w=[out])

    def mm(self, ps, pairs, tag="", start=True, stop=True):
        rr = []
        for lt, rt in pairs:
            rr += [lt, rt]

        def emit(e):
            n = len(pairs)
            for k, (lt, rt) in enumerate(pairs):
                ins = e.matmul(ps[0], lt[0], rt[0], start=(start and k == 0),
                               stop=(stop and k == n - 1))
            return ins
        self.B.t("pe", emit, r=rr, w=[ps], tag=tag)

    def transpose(self, ps_out, a, bf16=True, kp=128, base=0):
        idt = self.identb if bf16 else self.identf
        ident = idt[base:base + kp, base:base + kp]
        self.B.t("pe", lambda e: e.transpose(ps_out[0], a[0], ident[0]),
                 r=[a, ident], w=[ps_out])

    def dsl(self, dr, name, b, dt, c0=0, c1=T, esz=4):
        slab = (b * DT + dt)
        ap = dr[b, dt, :, c0:c1]
        return ap, Reg("D:" + name, 0, 128, slab * 128 * T * esz, (slab + 1) * 128 * T * esz)

    def load_consts(self):
        Bd = self.B
        Bd.dma("sp", self.vt, (self.inp["vecs"], dreg("vecs")))
        Bd.dma("sp", self.identb, (self.c_identb, dreg("c_identb")))
        Bd.dma("sp", self.identf, (self.c_identf, dreg("c_identf")))
        Bd.dma("sp", self.ones_col_b, (self.c_ones_col_b, dreg("c_ones_col_b")))
        Bd.dma("sp", self.ones_row_f, (self.c_ones_row_f, dreg("c_ones_row_f")))
        Bd.dma("sp", self.ones_row_b, (self.c_ones_row_b, dreg("c_ones_row_b")))
        Bd.dma("sp", self.rperm, (self.c_rperm, dreg("c_rperm")))
        g64np = np.zeros((128, 2), np.float32)
        g64np[0:64, 0] = 1.0
        g64np[64:128, 1] = 1.0
        Bd.dma("sp", self.g64b, (self.nc.inline_tensor(g64np.astype(BF), "c_g64b").ap(),
                                 dreg("c_g64b")))
        Bd.dma("sp", self.g64rep, (self.nc.inline_tensor(
            np.ascontiguousarray(g64np.T), "c_g64rep").ap(), dreg("c_g64rep")))
        for b in range(B):
            for dt in range(DT):
                Bd.dma("sp", self.dsl(self.X, "Xs", b, dt),
                       (self.inp["x_fm"][b, 128 * dt:128 * (dt + 1), :], dreg("x_fm")))
        t1 = Bd.carve(128, 8, F32)
        self.act(t1.full, self.vcol("l0_lmbd", 0, 8), AF.Sigmoid)
        self.act(t1.full, t1.full, AF.Ln)
        self.ts(self.lsg8.full, t1.full, 8.0, op0=ALU.mult)

    def ln_entry(self, b):
        P, Bd = self, self.B
        Bd.ph_reset()
        xt = [Bd.ph(128, T, F32) for _ in range(DT)]
        xnb = [Bd.ph(128, T, BF16) for _ in range(DT)]
        sqh = [Bd.ph(128, 1024, BF16) for _ in range(DT)]
        mean = Bd.ph(1, T, F32)
        e2 = Bd.ph(1, T, F32)
        tmp = Bd.ph(1, T, F32)
        inv = Bd.ph(1, T, F32)
        msr = Bd.ph(1, T, F32)
        t1 = Bd.ph(128, 512, F32)
        for dt in range(DT):
            Bd.dma("sp", xt[dt], self.dsl(self.X, "Xs", b, dt))
            self.copy(xnb[dt].full, xt[dt].full)
        onesb = self.ones_col_b.full
        for h in range(2):
            hc = 1024 * h
            for dt in range(DT):
                self.tt(ALU.mult, sqh[dt].full, xnb[dt][:, hc:hc + 1024],
                        xnb[dt][:, hc:hc + 1024])
            for c in range(2):
                c0, cg = hc + 512 * c, 512 * c
                ps_s, ps_q = Bd.psum(c % 2), Bd.psum(2 + c % 2)
                P.mm((ps_s.ap[0:1, :], ps_s.reg()),
                     [(onesb, xnb[dt][:, c0:c0 + 512]) for dt in range(DT)])
                P.mm((ps_q.ap[0:1, :], ps_q.reg()),
                     [(onesb, sqh[dt][:, cg:cg + 512]) for dt in range(DT)])
                self.ts((mean.ap[:, c0:c0 + 512], mean.reg(0, 1, c0, c0 + 512)),
                        (ps_s.ap[0:1, :], ps_s.reg()), 1.0 / D, op0=ALU.mult)
                self.ts((e2.ap[:, c0:c0 + 512], e2.reg(0, 1, c0, c0 + 512)),
                        (ps_q.ap[0:1, :], ps_q.reg()), 1.0 / D, op0=ALU.mult)
        self.tt(ALU.mult, tmp.full, mean.full, mean.full)
        self.tt(ALU.subtract, tmp.full, e2.full, tmp.full)
        self.act(inv.full, tmp.full, AF.Ln, bias=1e-5)
        self.act(inv.full, inv.full, AF.Exp, scale=-0.5)
        self.stt(msr.full, mean.full, -1.0, inv.full, op0=ALU.mult, op1=ALU.mult)
        for c in range(4):
            c0 = 512 * c
            psA_ = Bd.psum(4)
            psB_ = Bd.psum(5)
            P.mm(psA_.full, [((self.ones_row_f.ap, self.ones_row_f.reg()),
                              (inv.ap[:, c0:c0 + 512], inv.reg(0, 1, c0, c0 + 512)))])
            P.mm(psB_.full, [((self.ones_row_f.ap, self.ones_row_f.reg()),
                              (msr.ap[:, c0:c0 + 512], msr.reg(0, 1, c0, c0 + 512)))])
            for dt in range(DT):
                xc = xt[dt][:, c0:c0 + 512]
                self.tt(ALU.mult, t1.full, xc, psA_.full)
                self.tt(ALU.add, xc, t1.full, psB_.full)
                self.act(xnb[dt][:, c0:c0 + 512], xc, AF.Copy)
        for dt in range(DT):
            Bd.dma("sp", self.dsl(self.XN32, "XN32s", b, dt), xt[dt])
            Bd.dma("sp", self.dsl(self.XNB, "XNBs", b, dt, esz=2), xnb[dt])
        if DBG and b == 0:
            self.dbg("ln_mean", mean)
            self.dbg("ln_e2", e2)
            self.dbg("ln_inv", inv)
            self.dbg("ln_msr", msr)

    def l0_attn(self, b):
        P, Bd = self, self.B
        Bd.ph_reset()
        xnb = [Bd.ph(128, T, BF16) for _ in range(DT)]
        xn32 = Bd.ph(128, T, F32)
        wa = Bd.ph(128, 8192, BF16)
        wx = Bd.ph(128, 8192, BF16)
        rbuf = Bd.ph(128, T, F32)
        ibuf = Bd.ph(128, T, F32)
        abuf = Bd.ph(128, T, F32)
        bbuf = Bd.ph(128, T, F32)
        ytil = Bd.ph(128, T, F32)
        xres = Bd.ph(128, T, F32)
        for dt in range(DT):
            Bd.dma("sp", xnb[dt], self.dsl(self.XNB, "XNBs", b, dt, esz=2))
        for k in range(DT):
            Bd.dma("pool", wa[:, 1024 * k:1024 * (k + 1)],
                   (self.inp["wa_t"][128 * k:128 * (k + 1), :], dreg("wa_t")))
            Bd.dma("pool", wx[:, 1024 * k:1024 * (k + 1)],
                   (self.inp["wx_t"][128 * k:128 * (k + 1), :], dreg("wx_t")))
        for dt in range(DT):
            Bd.dma("sp", xn32, self.dsl(self.XN32, "XN32s", b, dt))
            Bd.dma("sp", xres, self.dsl(self.X, "Xs", b, dt))
            for c in range(4):
                c0 = 512 * c
                ps_r, ps_i = Bd.psum(c % 2), Bd.psum(2 + c % 2)
                P.mm(ps_r.full, [(wa[:, 1024 * k + 128 * dt:1024 * k + 128 * (dt + 1)],
                                  xnb[k][:, c0:c0 + 512]) for k in range(DT)])
                P.mm(ps_i.full, [(wx[:, 1024 * k + 128 * dt:1024 * k + 128 * (dt + 1)],
                                  xnb[k][:, c0:c0 + 512]) for k in range(DT)])
                self.act(rbuf[:, c0:c0 + 512], ps_r.full, AF.Sigmoid,
                         bias=self.vcol("l0_ba", dt))
                self.act(ibuf[:, c0:c0 + 512], ps_i.full, AF.Sigmoid,
                         bias=self.vcol("l0_bx", dt))
            self.act(abuf.full, rbuf.full, AF.Exp, scale=self.lsg8[:, dt:dt + 1])
            self.tt(ALU.mult, bbuf.full, abuf.full, abuf.full)
            self.act(bbuf.full, bbuf.full, AF.Ln, bias=1.0, scale=-1.0)
            self.act(bbuf.full, bbuf.full, AF.Exp, scale=0.5)
            self.tt(ALU.mult, ibuf.full, ibuf.full, (xn32.ap, xn32.reg()))
            self.tt(ALU.mult, bbuf.full, bbuf.full, ibuf.full)
            Bd.t("dve", lambda e, o=ytil, a_=abuf, b_=bbuf: e.tensor_tensor_scan(
                o.ap, a_.ap, b_.ap, 0.0, op0=ALU.mult, op1=ALU.add),
                r=[abuf.full, bbuf.full], w=[ytil.full])
            self.tt(ALU.add, xres.full, xres.full, ytil.full)
            Bd.dma("sp", self.dsl(self.X, "Xs", b, dt), xres)
        if DBG and b == 0:
            self.dbg("r7", rbuf)
            self.dbg("a7", abuf)
            self.dbg("b7", bbuf)
            self.dbg("y7", ytil)
            self.dbg("xattn", xres)

    def mm_proj(self, w_sb, src_tiles, out_write, nk=DT, nchunk=4, csz=512):
        pass

    def l1_attn(self, b):
        P, Bd = self, self.B
        Bd.ph_reset()
        qf = [Bd.ph(128, T, BF16) for _ in range(DT)]
        kf = [Bd.ph(128, T, BF16) for _ in range(DT)]
        vt = [Bd.ph(128, 16 * 65, BF16) for _ in range(16)]
        pbt = [Bd.ph(128, 512, BF16) for _ in range(2)]
        mk = Bd.mark()
        xnb = [Bd.ph(128, T, BF16) for _ in range(DT)]
        wq = Bd.ph(128, 8192, BF16)
        wkv = Bd.ph(128, 8192, BF16)
        for dt in range(DT):
            Bd.dma("sp", xnb[dt], self.dsl(self.XNB, "XNBs", b, dt, esz=2))
        for k in range(DT):
            Bd.dma("pool", wq[:, 1024 * k:1024 * (k + 1)],
                   (self.inp["l1a_wq"][128 * k:128 * (k + 1), :], dreg("l1a_wq")))
            Bd.dma("pool", wkv[:, 1024 * k:1024 * (k + 1)],
                   (self.inp["l1a_wkv"][128 * k:128 * (k + 1), :], dreg("l1a_wkv")))
        for dst, w_sb in ((qf, wq), (kf, wkv)):
            for dt in range(DT):
                for c in range(4):
                    c0 = 512 * c
                    ps = Bd.psum(c % 2 + (0 if dst is qf else 2))
                    P.mm(ps.full, [(w_sb[:, 1024 * k + 128 * dt:1024 * k + 128 * (dt + 1)],
                                    xnb[k][:, c0:c0 + 512]) for k in range(DT)])
                    self.act(dst[dt][:, c0:c0 + 512], ps.full, AF.Copy)
        pst = Bd.psumT()
        for h in range(16):
            t2, o = h // 2, 64 * (h % 2)
            for pt in range(16):
                self.transpose((pst.ap[0:128, 0:64], pst.reg(0, 128, 0, 64)),
                               (kf[t2].ap[o:o + 64, 128 * pt:128 * (pt + 1)],
                                kf[t2].reg(o, o + 64, 128 * pt, 128 * (pt + 1))),
                               bf16=True, kp=64, base=o)
                self.copy((vt[pt].ap[:, 65 * h:65 * h + 64],
                           vt[pt].reg(0, 128, 65 * h, 65 * h + 64)),
                          (pst.ap[0:128, 0:64], pst.reg(0, 128, 0, 64)))
        for pt in range(16):
            self.memset((vt[pt].ap[:, 64:16 * 65:65], vt[pt].reg(0, 128, 0, 16 * 65)), 1.0)
        Bd.reset_to(mk)
        ofm = [Bd.ph(128, T, BF16) for _ in range(DT)]
        wo = Bd.ph(128, 8192, BF16)
        den = Bd.ph(1, 512, F32)
        repd = Bd.ph(64, 512, F32)
        xres = Bd.ph(128, T, F32)
        for h in range(16):
            t2, o = h // 2, 64 * (h % 2)
            for ic in range(4):
                i0 = 512 * ic
                ops = Bd.psum(4 + ic % 2)
                for pt in range(16):
                    ps = Bd.psum(pt % 2)
                    P.mm(ps.full, [((kf[t2].ap[o:o + 64, 128 * pt:128 * (pt + 1)],
                                     kf[t2].reg(o, o + 64, 128 * pt, 128 * (pt + 1))),
                                    (qf[t2].ap[o:o + 64, i0:i0 + 512],
                                     qf[t2].reg(o, o + 64, i0, i0 + 512)))])
                    pb = pbt[pt % 2]
                    if pt % 2 == 0:
                        self.act(pb.full, ps.full, AF.Relu, scale=0.125)
                    else:
                        self.ts(pb.full, ps.full, 0.125, 0.0, op0=ALU.mult,
                                op1=ALU.max)
                    P.mm((ops.ap[0:65, :], ops.reg()),
                         [((vt[pt].ap[:, 65 * h:65 * (h + 1)],
                            vt[pt].reg(0, 128, 65 * h, 65 * (h + 1))), pb.full)],
                         start=(pt == 0), stop=(pt == 15))
                self.act((ofm[t2].ap[o:o + 64, i0:i0 + 512],
                          ofm[t2].reg(o, o + 64, i0, i0 + 512)),
                         (ops.ap[0:64, :], ops.reg()), AF.Copy)
        for k in range(DT):
            Bd.dma("pool", wo[:, 1024 * k:1024 * (k + 1)],
                   (self.inp["l1a_wout"][128 * k:128 * (k + 1), :], dreg("l1a_wout")))
        for dt in range(DT):
            Bd.dma("sp", xres, self.dsl(self.X, "Xs", b, dt))
            for c in range(4):
                c0 = 512 * c
                ps = Bd.psum(6)
                P.mm(ps.full, [(wo[:, 1024 * k + 128 * dt:1024 * k + 128 * (dt + 1)],
                                ofm[k][:, c0:c0 + 512]) for k in range(DT)])
                self.stt(xres[:, c0:c0 + 512], ps.full, self.vcol("l1a_bout", dt),
                         xres[:, c0:c0 + 512], op0=ALU.add, op1=ALU.add)
            Bd.dma("sp", self.dsl(self.X, "Xs", b, dt), xres)

    def l1_ffn(self, b):
        P, Bd = self, self.B
        HID = 2730
        rotc = [self.lazy_const("c_csq", lambda: _rot_tables()[0]),
                self.lazy_const("c_ssq", lambda: _rot_tables()[1]),
                self.lazy_const("c_csk", lambda: _rot_tables()[2]),
                self.lazy_const("c_ssk", lambda: _rot_tables()[3])]
        cm0 = self.lazy_const("c_m01_0", lambda: np.ascontiguousarray(
            (_l1_masksT()[0] == 0).astype(BF)))
        cm1 = self.lazy_const("c_m01_1", lambda: np.ascontiguousarray(
            (_l1_masksT()[1] == 0).astype(BF)))
        for l in range(2):
            Bd.ph_reset()
            tabs = [Bd.ph(128, T, F32) for _ in range(4)]
            qf = [Bd.ph(128, T, BF16) for _ in range(4)]
            kpb = [Bd.ph(128, 2560, BF16) for _ in range(4)]
            vt = [Bd.ph(128, 8 * 65, BF16) for _ in range(16)]
            mka = Bd.mark()
            hsh = [Bd.ph(128, T, BF16) for _ in range(DT)]
            wq3 = Bd.ph(128, DT * 1536, BF16)
            zld = Bd.ph(128, T, F32)
            sqb = Bd.ph(128, T, BF16)
            inv2 = Bd.ph(2, 512, F32)
            t1k = Bd.ph(128, 2048, F32)
            for i, cap in enumerate(rotc):
                Bd.dma("sp", tabs[i], (cap, dreg(f"c_rt{i}")))
                sc = [f"l1f_qs{l}", f"l1f_qss{l}", f"l1f_ks{l}", f"l1f_kss{l}"][i]
                self.ts(tabs[i].full, tabs[i].full, self.vcol(sc), op0=ALU.mult)
            zsrc, zname = (self.XN32, "XN32s") if l == 0 else (self.ZS, "ZSs")
            for dt in range(DT):
                Bd.dma("sp", zld, self.dsl(zsrc, zname, b, dt))
                if dt < 4:
                    self.copy(hsh[dt].full, zld.full)
                else:
                    self.memset(hsh[dt][:, 0:1], 0.0)
                    self.copy(hsh[dt][:, 1:T], zld[:, 0:T - 1])
            for k in range(DT):
                Bd.dma("pool", wq3[:, 1536 * k:1536 * (k + 1)],
                       (self.inp["l1f_wqkv"][l, 128 * k:128 * (k + 1), :],
                        dreg("l1f_wqkv")))
            for ft in range(8):
                dst = qf[ft] if ft < 4 else None
                for c in range(4):
                    c0 = 512 * c
                    ps = Bd.psum(c % 2)
                    P.mm(ps.full, [(wq3[:, 1536 * k + 128 * ft:1536 * k + 128 * (ft + 1)],
                                    hsh[k][:, c0:c0 + 512]) for k in range(DT)])
                    if ft < 4:
                        self.act(qf[ft][:, c0:c0 + 512], ps.full, AF.Copy)
                    else:
                        self.act(kpb[ft - 4][:, 512 + c0:512 + c0 + 512], ps.full, AF.Copy)
            for mt in range(16):
                ps = Bd.psum(2 + mt % 2)
                P.mm(ps.full, [((hsh[k].ap[:, 128 * mt:128 * (mt + 1)],
                                 hsh[k].reg(0, 128, 128 * mt, 128 * (mt + 1))),
                                (wq3.ap[:, 1536 * k + 1024:1536 * (k + 1)],
                                 wq3.reg(0, 128, 1536 * k + 1024, 1536 * (k + 1))))
                               for k in range(DT)])
                self.B.t("dve", lambda e, mt=mt, ps=ps: e.tensor_copy(
                    vt[mt].ap.rearrange("p (h c) -> p h c", c=65)[:, :, 0:64],
                    ps.ap.rearrange("p (h c) -> p h c", c=64)),
                    r=[ps.full], w=[vt[mt].full])
                self.memset((vt[mt].ap[:, 64:8 * 65:65], vt[mt].reg()), 1.0)
            for which in range(2):
                ctab, stab = (tabs[0], tabs[1]) if which == 0 else (tabs[2], tabs[3])
                for t2 in range(4):
                    if which == 0:
                        src = qf[t2]
                        off = 0
                    else:
                        src = kpb[t2]
                        off = 512
                    sl = lambda c0, n=512: (src.ap[:, off + c0:off + c0 + n],
                                            src.reg(0, 128, off + c0, off + c0 + n))
                    self.tt(ALU.mult, sqb[:, 0:T], sl(0, T), sl(0, T))
                    for c in range(4):
                        c0 = 512 * c
                        psn = Bd.psum(4)
                        P.mm((psn.ap[0:2, :], psn.reg()),
                             [(self.g64b.full,
                               (sqb.ap[:, c0:c0 + 512], sqb.reg(0, 128, c0, c0 + 512)))])
                        self.ts(inv2.full, (psn.ap[0:2, :], psn.reg()), 1e-24, op0=ALU.max)
                        self.act(inv2.full, inv2.full, AF.Ln)
                        self.act(inv2.full, inv2.full, AF.Exp, scale=-0.5)
                        psr = Bd.psum(5)
                        P.mm(psr.full, [(self.g64rep.full, inv2.full)])
                        self.tt(ALU.mult, sl(c0), sl(c0), psr.full)
                    for hh in range(2):
                        h0 = 1024 * hh
                        pst = Bd.psum(5, 2)
                        P.mm((pst.ap[:, 0:512], pst.reg(0, 128, 0, 512)),
                             [((self.rperm.ap, self.rperm.reg()), sl(h0, 512))])
                        P.mm((pst.ap[:, 512:1024], pst.reg(0, 128, 512, 1024)),
                             [((self.rperm.ap, self.rperm.reg()), sl(h0 + 512, 512))])
                        t1 = t1k
                        self.tt(ALU.mult, (t1.ap[:, 0:1024], t1.reg(0, 128, 0, 1024)),
                                sl(h0, 1024),
                                (ctab.ap[:, h0:h0 + 1024], ctab.reg(0, 128, h0, h0 + 1024)))
                        self.tt(ALU.mult, (t1.ap[:, 1024:2048], t1.reg(0, 128, 1024, 2048)),
                                (pst.ap[:, 0:1024], pst.reg(0, 128, 0, 1024)),
                                (stab.ap[:, h0:h0 + 1024], stab.reg(0, 128, h0, h0 + 1024)))
                        self.tt(ALU.add, sl(h0, 1024),
                                (t1.ap[:, 0:1024], t1.reg(0, 128, 0, 1024)),
                                (t1.ap[:, 1024:2048], t1.reg(0, 128, 1024, 2048)))
            for t2 in range(4):
                self.memset(kpb[t2][:, 0:512], 0.0)
            Bd.reset_to(mka)
            m01 = [Bd.ph(128, 8 * 512, BF16) for _ in range(2)]
            for v_, cap in ((0, cm0), (1, cm1)):
                Bd.dma("sp", (m01[v_].ap.rearrange("p (j n) -> p j n", j=8),
                              m01[v_].reg()),
                       (cap.rearrange("(j p) n -> p j n", p=128),
                        dreg("c_m01_" + str(v_))))
            pbt = [Bd.ph(128, 512, BF16) for _ in range(2)]
            exv = [Bd.ph(128, 512, BF16) for _ in range(2)]
            ofm = [Bd.ph(128, T, BF16) for _ in range(4)]
            wo4 = Bd.ph(128, 4096, BF16)
            denr = Bd.ph(1, 512, F32)
            reps = Bd.ph(64, 512, F32)
            zres = Bd.ph(128, T, F32)
            for k in range(4):
                Bd.dma("pool", wo4[:, 1024 * k:1024 * (k + 1)],
                       (self.inp["l1f_wo"][l, 128 * k:128 * (k + 1), :], dreg("l1f_wo")))
            for h in range(8):
                t2, o = h // 2, 64 * (h % 2)
                for w in range(4):
                    i0 = 512 * w
                    ops = Bd.psum(2 + w % 2)
                    jts = range(4, 8) if w == 0 else range(8)
                    first = True
                    for jt in jts:
                        jc = 512 * w + 128 * jt
                        ps = Bd.psum(jt % 2)
                        P.mm(ps.full, [((kpb[t2].ap[o:o + 64, jc:jc + 128],
                                         kpb[t2].reg(o, o + 64, jc, jc + 128)),
                                        (qf[t2].ap[o:o + 64, i0:i0 + 512],
                                         qf[t2].reg(o, o + 64, i0, i0 + 512)))])
                        ex = exv[jt % 2]
                        pb = pbt[jt % 2]
                        self.act(ex.full, ps.full, AF.Exp, scale=8.0)
                        mv = m01[0 if w == 0 else 1]
                        self.tt(ALU.mult, pb.full, ex.full,
                                (mv.ap[:, 512 * jt:512 * (jt + 1)],
                                 mv.reg(0, 128, 512 * jt, 512 * (jt + 1))), eng="pool")
                        mt = (512 * (w - 1) + 128 * jt) // 128
                        P.mm((ops.ap[0:65, :], ops.reg()),
                             [((vt[mt].ap[:, 65 * h:65 * (h + 1)],
                                vt[mt].reg(0, 128, 65 * h, 65 * (h + 1))), pb.full)],
                             start=first, stop=(jt == max(jts)))
                        first = False
                    self.B.t("dve", lambda e, ops=ops: e.reciprocal(
                        denr.ap, ops.ap[64:65, :]), r=[ops.full], w=[denr.full])
                    psr = Bd.psum(4)
                    P.mm((psr.ap[0:64, :], psr.reg()),
                         [((self.ones_row_f.ap[0:1, 0:64],
                            self.ones_row_f.reg(0, 1, 0, 64)), denr.full)])
                    self.copy(reps.full, (psr.ap[0:64, :], psr.reg()))
                    self.tt(ALU.mult, (ofm[t2].ap[o:o + 64, i0:i0 + 512],
                                       ofm[t2].reg(o, o + 64, i0, i0 + 512)),
                            (ops.ap[0:64, :], ops.reg()), reps.full)
            for dt in range(DT):
                Bd.dma("sp", zres, self.dsl(zsrc, zname, b, dt))
                for c in range(4):
                    c0 = 512 * c
                    ps = Bd.psum(5 + c % 2)
                    P.mm(ps.full, [(wo4[:, 1024 * k + 128 * dt:1024 * k + 128 * (dt + 1)],
                                    ofm[k][:, c0:c0 + 512]) for k in range(4)])
                    self.tt(ALU.add, zres[:, c0:c0 + 512], zres[:, c0:c0 + 512], ps.full)
                Bd.dma("sp", self.dsl(self.ZS, "ZSs", b, dt), zres)
            Bd.ph_reset()
            h2b = [Bd.ph(128, T, BF16) for _ in range(DT)]
            hnb = [Bd.ph(128, T, BF16) for _ in range(DT)]
            zld2 = Bd.ph(128, T, F32)
            sq2 = Bd.ph(128, 512, BF16)
            mean = Bd.ph(1, T, F32)
            e2 = Bd.ph(1, T, F32)
            inv = Bd.ph(1, T, F32)
            msr = Bd.ph(1, T, F32)
            t1c = Bd.ph(128, 512, F32)
            for dt in range(DT):
                Bd.dma("sp", zld2, self.dsl(self.ZS, "ZSs", b, dt))
                if dt < 4:
                    self.copy(h2b[dt].full, zld2.full)
                else:
                    self.memset(h2b[dt][:, 0:1], 0.0)
                    self.copy(h2b[dt][:, 1:T], zld2[:, 0:T - 1])
            onesb = self.ones_col_b.full
            for c in range(4):
                c0 = 512 * c
                ps_s, ps_q = Bd.psum(c % 2), Bd.psum(2 + c % 2)
                P.mm((ps_s.ap[0:1, :], ps_s.reg()),
                     [(onesb, h2b[dt][:, c0:c0 + 512]) for dt in range(DT)])
                for dt in range(DT):
                    self.tt(ALU.mult, sq2.full, h2b[dt][:, c0:c0 + 512],
                            h2b[dt][:, c0:c0 + 512])
                    P.mm((ps_q.ap[0:1, :], ps_q.reg()), [(onesb, sq2.full)],
                         start=(dt == 0), stop=(dt == DT - 1))
                self.ts((mean.ap[:, c0:c0 + 512], mean.reg(0, 1, c0, c0 + 512)),
                        (ps_s.ap[0:1, :], ps_s.reg()), 1.0 / D, op0=ALU.mult)
                self.ts((e2.ap[:, c0:c0 + 512], e2.reg(0, 1, c0, c0 + 512)),
                        (ps_q.ap[0:1, :], ps_q.reg()), 1.0 / D, op0=ALU.mult)
            self.tt(ALU.mult, inv.full, mean.full, mean.full)
            self.tt(ALU.subtract, inv.full, e2.full, inv.full)
            self.act(inv.full, inv.full, AF.Ln, bias=1e-5)
            self.act(inv.full, inv.full, AF.Exp, scale=-0.5)
            self.stt(msr.full, mean.full, -1.0, inv.full, op0=ALU.mult, op1=ALU.mult)
            for c in range(4):
                c0 = 512 * c
                psA_, psB_ = Bd.psum(4), Bd.psum(5)
                P.mm(psA_.full, [((self.ones_row_f.ap, self.ones_row_f.reg()),
                                  (inv.ap[:, c0:c0 + 512], inv.reg(0, 1, c0, c0 + 512)))])
                P.mm(psB_.full, [((self.ones_row_f.ap, self.ones_row_f.reg()),
                                  (msr.ap[:, c0:c0 + 512], msr.reg(0, 1, c0, c0 + 512)))])
                for dt in range(DT):
                    self.tt(ALU.mult, t1c.full, h2b[dt][:, c0:c0 + 512], psA_.full)
                    self.tt(ALU.add, t1c.full, t1c.full, psB_.full)
                    self.ts(hnb[dt][:, c0:c0 + 512], t1c.full,
                            self.vcol(f"l1f_lng{l}", dt), self.vcol(f"l1f_lnb{l}", dt),
                            op0=ALU.mult, op1=ALU.add)
            mkc = Bd.mark()
            w1b = [Bd.ph(128, 1024, BF16) for _ in range(4)]
            w2s = Bd.ph(128, 22 * 1024, BF16)
            hgg = [Bd.ph(128, 512, BF16) for _ in range(22)]
            gg = Bd.ph(128, 512, F32)
            widths = [128] * 21 + [42]
            for fi in range(22):
                Bd.dma("pool", (w2s.ap[0:widths[fi], 1024 * fi:1024 * (fi + 1)],
                                w2s.reg(0, 128, 1024 * fi, 1024 * (fi + 1))),
                       (self.inp["l1f_w2"][l, 128 * fi:128 * fi + widths[fi], :],
                        dreg("l1f_w2")))
            for c in range(4):
                c0 = 512 * c
                for fi in range(22):
                    wdt = widths[fi]
                    wa_ = w1b[2 * (fi % 2)]
                    wg_ = w1b[2 * (fi % 2) + 1]
                    Bd.dma("pool", (wa_.ap[:, 0:8 * wdt], wa_.reg(0, 128, 0, 8 * wdt)),
                           (self.inp["l1f_w1"].rearrange("l (kt p) m -> l p kt m", p=128)
                            [l, :, :, 128 * fi:128 * fi + wdt], dreg("l1f_w1")))
                    Bd.dma("pool", (wg_.ap[:, 0:8 * wdt], wg_.reg(0, 128, 0, 8 * wdt)),
                           (self.inp["l1f_w1"].rearrange("l (kt p) m -> l p kt m", p=128)
                            [l, :, :, HID + 128 * fi:HID + 128 * fi + wdt], dreg("l1f_w1")))
                    psa = Bd.psum(fi % 2)
                    psg = Bd.psum(2 + fi % 2)
                    P.mm((psa.ap[0:wdt, :], psa.reg()),
                         [((wa_.ap[:, wdt * k:wdt * (k + 1)],
                            wa_.reg(0, 128, wdt * k, wdt * (k + 1))),
                           hnb[k][:, c0:c0 + 512]) for k in range(DT)])
                    P.mm((psg.ap[0:wdt, :], psg.reg()),
                         [((wg_.ap[:, wdt * k:wdt * (k + 1)],
                            wg_.reg(0, 128, wdt * k, wdt * (k + 1))),
                           hnb[k][:, c0:c0 + 512]) for k in range(DT)])
                    self.act((gg.ap[0:wdt, :], gg.reg(0, wdt, 0, 512)),
                             (psg.ap[0:wdt, :], psg.reg()), AF.Gelu,
                             bias=self.vcol(f"l1f_b1g{l}", fi, P=wdt))
                    self.stt(hgg[fi][0:wdt, 0:512],
                             (psa.ap[0:wdt, :], psa.reg()),
                             self.vcol(f"l1f_b1a{l}", fi, P=wdt),
                             (gg.ap[0:wdt, :], gg.reg(0, wdt, 0, 512)),
                             op0=ALU.add, op1=ALU.mult)
                for dt in range(DT):
                    ps = Bd.psum(4 + dt % 2)
                    P.mm(ps.full, [((w2s.ap[0:widths[fi], 1024 * fi + 128 * dt:1024 * fi + 128 * (dt + 1)],
                                     w2s.reg(0, 128, 1024 * fi, 1024 * (fi + 1))),
                                    hgg[fi][0:widths[fi], 0:512]) for fi in range(22)])
                    Bd.dma("sp", (zld2.ap[:, c0:c0 + 512],
                                  zld2.reg(0, 128, c0, c0 + 512)),
                           self.dsl(self.ZS, "ZSs", b, dt, c0, c0 + 512))
                    self.stt(zld2[:, c0:c0 + 512], ps.full, self.vcol(f"l1f_b2_{l}", dt),
                             zld2[:, c0:c0 + 512], op0=ALU.add, op1=ALU.add)
                    Bd.dma("sp", self.dsl(self.ZS, "ZSs", b, dt, c0, c0 + 512),
                           (zld2.ap[:, c0:c0 + 512], zld2.reg(0, 128, c0, c0 + 512)))
        Bd.ph_reset()
        xr = Bd.ph(128, T, F32)
        zr = Bd.ph(128, T, F32)
        for dt in range(DT):
            Bd.dma("sp", xr, self.dsl(self.X, "Xs", b, dt))
            Bd.dma("sp", zr, self.dsl(self.ZS, "ZSs", b, dt))
            self.tt(ALU.add, xr.full, xr.full, zr.full)
            Bd.dma("sp", self.dsl(self.X, "Xs", b, dt), xr)

    def quant_fp4(self, dst, src, tmpa, tmpb, tmpr, tmps, N, P_=128,
                  eng1="dve", eng2="pool"):
        nb = N // 16
        bc = lambda t: t[0].unsqueeze(2).to_broadcast([P_, nb, 16])
        v3 = lambda t: t[0].rearrange("p (b s) -> p b s", s=16)
        self.B.t(eng1, lambda e: e.tensor_reduce(
            tmps[0], v3(src), axis=AX.X, op=ALU.max, apply_absolute_value=True),
            r=[src], w=[tmps])
        self.ts(tmps, tmps, 1.0 / 6.0, 1e-12, op0=ALU.mult, op1=ALU.max, eng=eng1)
        self.B.t("dve", lambda e: e.reciprocal(tmpr[0], tmps[0]), r=[tmps], w=[tmpr])
        self.act(tmpa, src, AF.Abs)
        self.B.t(eng1, lambda e: e.tensor_tensor(v3(tmpa), v3(tmpa), bc(tmpr),
                                                 op=ALU.mult), r=[tmpa, tmpr], w=[tmpa])
        self.ts(tmpb, tmpa, 0.25, None, op0=ALU.is_gt, eng=eng1)
        for k, m in enumerate((0.75, 1.25, 1.75, 2.5, 3.5, 5.0)):
            self.stt(tmpb, tmpa, m, tmpb, op0=ALU.is_gt, op1=ALU.add, eng=eng1)
        self.ts(tmpa, tmpb, -4.0, 0.0, op0=ALU.add, op1=ALU.max, eng=eng1)
        self.tt(ALU.add, tmpa, tmpa, tmpb, eng=eng2)
        self.ts(tmpb, tmpb, -6.0, 0.0, op0=ALU.add, op1=ALU.max, eng=eng1)
        self.stt(tmpb, tmpa, 0.5, tmpb, op0=ALU.mult, op1=ALU.add, eng=eng1)
        self.B.t(eng2, lambda e: e.tensor_tensor(v3(tmpb), v3(tmpb), bc(tmps),
                                                 op=ALU.mult), r=[tmpb, tmps], w=[tmpb])
        self.ts(tmpa, src, 0.0, 2.0, op0=ALU.is_ge, op1=ALU.mult, eng=eng1)
        self.stt(dst, tmpa, -1.0, tmpb, op0=ALU.add, op1=ALU.mult, eng=eng1)

    def l3_prep_weights(self):
        P, Bd = self, self.B
        Bd.ph_reset()
        src = Bd.ph(128, 2816, F32)
        tmpa = Bd.ph(128, 2816, F32)
        tmpb = Bd.ph(128, 2816, F32)
        tmpr = Bd.ph(128, 176, F32)
        tmps = Bd.ph(128, 176, F32)
        dstb = Bd.ph(128, 2816, BF16)
        ev = Bd.ph(128, 128, BF16)
        w3q = lambda wi, a: self.W3Q[wi, :].rearrange("(a p n) -> a p n", a=a, p=128)
        for wi, wname in ((0, "l3_wgate"), (1, "l3_wup")):
            for sl_ in range(22):
                r0 = 128 * sl_
                Bd.dma("sp", (src.ap[:, 0:1024], src.reg(0, 128, 0, 1024)),
                       (self.inp[wname][r0:r0 + 128, :], dreg(wname)))
                self.quant_fp4((dstb.ap[:, 0:1024], dstb.reg(0, 128, 0, 1024)),
                               (src.ap[:, 0:1024], src.reg(0, 128, 0, 1024)),
                               (tmpa.ap[:, 0:1024], tmpa.reg(0, 128, 0, 1024)),
                               (tmpb.ap[:, 0:1024], tmpb.reg(0, 128, 0, 1024)),
                               (tmpr.ap[:, 0:64], tmpr.reg(0, 128, 0, 64)),
                               (tmps.ap[:, 0:64], tmps.reg(0, 128, 0, 64)), 1024)
                for ds in range(8):
                    pst = Bd.psumT()
                    self.transpose((pst.ap[:, 0:128], pst.reg(0, 128, 0, 128)),
                                   (dstb.ap[:, 128 * ds:128 * (ds + 1)],
                                    dstb.reg(0, 128, 128 * ds, 128 * (ds + 1))))
                    self.copy(ev.full, (pst.ap[:, 0:128], pst.reg(0, 128, 0, 128)))
                    Bd.dma("sp", (w3q(wi, 8)[ds, :, r0:r0 + 128], dreg("W3Q")), ev)
        for sl_ in range(8):
            r0 = 128 * sl_
            Bd.dma("sp", src, (self.inp["l3_wdown"][r0:r0 + 128, :], dreg("l3_wdown")))
            self.quant_fp4(dstb.full, src.full, tmpa.full, tmpb.full, tmpr.full, tmps.full, 2816)
            for isl in range(22):
                pst = Bd.psumT()
                self.transpose((pst.ap[:, 0:128], pst.reg(0, 128, 0, 128)),
                               (dstb.ap[:, 128 * isl:128 * (isl + 1)],
                                dstb.reg(0, 128, 128 * isl, 128 * (isl + 1))))
                self.copy(ev.full, (pst.ap[:, 0:128], pst.reg(0, 128, 0, 128)))
                Bd.dma("sp", (w3q(2, 22)[isl, :, r0:r0 + 128], dreg("W3Q")), ev)

    def l3_attn(self, b):
        P, Bd = self, self.B
        Bd.ph_reset()
        xnb = [Bd.ph(128, T, BF16) for _ in range(DT)]
        xtm = [Bd.ph(128, 1024, BF16) for _ in range(16)]
        tb = [Bd.ph(128, 512, BF16) for _ in range(16)]
        xres = Bd.ph(128, 512, F32)
        for dt in range(DT):
            Bd.dma("sp", xnb[dt], self.dsl(self.XNB, "XNBs", b, dt, esz=2))
        for mt in range(16):
            for dt in range(DT):
                pst = Bd.psumT()
                self.transpose((pst.ap[:, 0:128], pst.reg(0, 128, 0, 128)),
                               (xnb[dt].ap[:, 128 * mt:128 * (mt + 1)],
                                xnb[dt].reg(0, 128, 128 * mt, 128 * (mt + 1))))
                self.copy((xtm[mt].ap[:, 128 * dt:128 * (dt + 1)],
                           xtm[mt].reg(0, 128, 128 * dt, 128 * (dt + 1))),
                          (pst.ap[:, 0:128], pst.reg(0, 128, 0, 128)))
        for c in range(4):
            c0 = 512 * c
            for mt in range(16):
                Bd.dma("pool", tb[mt], (self.inp["l3_toep"][128 * mt:128 * (mt + 1),
                                                            c0:c0 + 512], dreg("l3_toep")))
            for dt in range(DT):
                ps = Bd.psum(dt % 4)
                P.mm(ps.full, [((xtm[mt].ap[:, 128 * dt:128 * (dt + 1)],
                                 xtm[mt].reg(0, 128, 128 * dt, 128 * (dt + 1))),
                                tb[mt].full) for mt in range(16)])
                Bd.dma("sp", xres, self.dsl(self.X, "Xs", b, dt, c0, c0 + 512))
                self.tt(ALU.add, xres.full, xres.full, ps.full)
                Bd.dma("sp", self.dsl(self.X, "Xs", b, dt, c0, c0 + 512), xres)

    def l3_ffn(self, b):
        P, Bd = self, self.B
        Bd.ph_reset()
        wgq = Bd.ph(128, 8 * 2816, BF16)
        wuq = Bd.ph(128, 8 * 2816, BF16)
        wdq = Bd.ph(128, 22 * 1024, BF16)
        xq = [Bd.ph(128, T, BF16) for _ in range(DT)]
        xc = Bd.ph(128, 128, F32)
        ttm = Bd.ph(128, 1024, F32)
        tmpa = Bd.ph(128, 1024, F32)
        tmpb = Bd.ph(128, 1024, F32)
        tmpr = Bd.ph(128, 64, F32)
        tmps = Bd.ph(128, 64, F32)
        qtm = Bd.ph(128, 1024, BF16)
        hq = [Bd.ph(128, 128, BF16) for _ in range(22)]
        sg = Bd.ph(128, 512, F32)
        h32 = Bd.ph(128, 512, F32)
        hqb = Bd.ph(128, 512, BF16)
        xres = Bd.ph(128, 128, F32)
        w3q = lambda wi, a: self.W3Q[wi, :].rearrange("(a p n) -> a p n", a=a, p=128)
        for ds in range(8):
            Bd.dma("sp", (wgq.ap[:, 2816 * ds:2816 * (ds + 1)],
                          wgq.reg(0, 128, 2816 * ds, 2816 * (ds + 1))),
                   (w3q(0, 8)[ds], dreg("W3Q")))
            Bd.dma("sp", (wuq.ap[:, 2816 * ds:2816 * (ds + 1)],
                          wuq.reg(0, 128, 2816 * ds, 2816 * (ds + 1))),
                   (w3q(1, 8)[ds], dreg("W3Q")))
        for isl in range(22):
            Bd.dma("sp", (wdq.ap[:, 1024 * isl:1024 * (isl + 1)],
                          wdq.reg(0, 128, 1024 * isl, 1024 * (isl + 1))),
                   (w3q(2, 22)[isl], dreg("W3Q")))
        for mt in range(16):
            for dt in range(DT):
                Bd.dma("sp", xc, self.dsl(self.XN32, "XN32s", b, dt,
                                          128 * mt, 128 * (mt + 1)))
                psf = Bd.psum(6)
                self.transpose((psf.ap[:, 0:128], psf.reg(0, 128, 0, 128)),
                               xc.full, bf16=False)
                self.copy((ttm.ap[:, 128 * dt:128 * (dt + 1)],
                           ttm.reg(0, 128, 128 * dt, 128 * (dt + 1))),
                          (psf.ap[:, 0:128], psf.reg(0, 128, 0, 128)))
            self.quant_fp4(qtm.full, ttm.full, tmpa.full, tmpb.full, tmpr.full, tmps.full, 1024)
            for dt in range(DT):
                pst = Bd.psumT()
                self.transpose((pst.ap[:, 0:128], pst.reg(0, 128, 0, 128)),
                               (qtm.ap[:, 128 * dt:128 * (dt + 1)],
                                qtm.reg(0, 128, 128 * dt, 128 * (dt + 1))))
                self.copy((xq[dt].ap[:, 128 * mt:128 * (mt + 1)],
                           xq[dt].reg(0, 128, 128 * mt, 128 * (mt + 1))),
                          (pst.ap[:, 0:128], pst.reg(0, 128, 0, 128)))
        ics = [(0, 512), (512, 512), (1024, 512), (1536, 512), (2048, 512), (2560, 256)]
        for mt in range(16):
            for (i0, iw) in ics:
                psg = Bd.psum(0)
                psu = Bd.psum(1)
                P.mm((psg.ap[:, 0:iw], psg.reg()),
                     [((xq[ds].ap[:, 128 * mt:128 * (mt + 1)],
                        xq[ds].reg(0, 128, 128 * mt, 128 * (mt + 1))),
                       (wgq.ap[:, 2816 * ds + i0:2816 * ds + i0 + iw],
                        wgq.reg(0, 128, 2816 * ds + i0, 2816 * ds + i0 + iw)))
                      for ds in range(8)])
                P.mm((psu.ap[:, 0:iw], psu.reg()),
                     [((xq[ds].ap[:, 128 * mt:128 * (mt + 1)],
                        xq[ds].reg(0, 128, 128 * mt, 128 * (mt + 1))),
                       (wuq.ap[:, 2816 * ds + i0:2816 * ds + i0 + iw],
                        wuq.reg(0, 128, 2816 * ds + i0, 2816 * ds + i0 + iw)))
                      for ds in range(8)])
                self.act((sg.ap[:, 0:iw], sg.reg(0, 128, 0, iw)),
                         (psg.ap[:, 0:iw], psg.reg()), AF.Silu)
                self.tt(ALU.mult, (h32.ap[:, 0:iw], h32.reg(0, 128, 0, iw)),
                        (sg.ap[:, 0:iw], sg.reg(0, 128, 0, iw)),
                        (psu.ap[:, 0:iw], psu.reg()))
                self.quant_fp4((hqb.ap[:, 0:iw], hqb.reg(0, 128, 0, iw)),
                               (h32.ap[:, 0:iw], h32.reg(0, 128, 0, iw)),
                               (tmpa.ap[:, 0:iw], tmpa.reg(0, 128, 0, iw)),
                               (tmpb.ap[:, 0:iw], tmpb.reg(0, 128, 0, iw)),
                               (tmpr.ap[:, 0:iw // 16], tmpr.reg(0, 128, 0, iw // 16)),
                               (tmps.ap[:, 0:iw // 16], tmps.reg(0, 128, 0, iw // 16)), iw)
                for j in range(iw // 128):
                    isl = i0 // 128 + j
                    pst = Bd.psumT()
                    self.transpose((pst.ap[:, 0:128], pst.reg(0, 128, 0, 128)),
                                   (hqb.ap[:, 128 * j:128 * (j + 1)],
                                    hqb.reg(0, 128, 128 * j, 128 * (j + 1))))
                    self.copy(hq[isl].full, (pst.ap[:, 0:128], pst.reg(0, 128, 0, 128)))
            for dt in range(DT):
                psd = Bd.psum(2 + dt % 2)
                P.mm((psd.ap[:, 0:128], psd.reg()),
                     [((wdq.ap[:, 1024 * isl + 128 * dt:1024 * isl + 128 * (dt + 1)],
                        wdq.reg(0, 128, 1024 * isl, 1024 * (isl + 1))),
                       hq[isl].full) for isl in range(22)])
                Bd.dma("sp", xres, self.dsl(self.X, "Xs", b, dt,
                                            128 * mt, 128 * (mt + 1)))
                self.tt(ALU.add, xres.full, xres.full, (psd.ap[:, 0:128], psd.reg()))
                Bd.dma("sp", self.dsl(self.X, "Xs", b, dt, 128 * mt, 128 * (mt + 1)),
                       xres)

    def l2_attn(self, b):
        P, Bd = self, self.B
        Bd.ph_reset()
        qf = [Bd.ph(128, T, BF16) for _ in range(DT)]
        kf = [Bd.ph(128, T, BF16) for _ in range(DT)]
        vt = [Bd.ph(128, 16 * 65, BF16) for _ in range(16)]
        ocb = [Bd.ph(128, T, BF16) for _ in range(DT)]
        gcols = Bd.ph(64, 48, F32)
        lnw = Bd.ph(128, 16, F32)
        mk0 = Bd.mark()
        xnb = [Bd.ph(128, T, BF16) for _ in range(DT)]
        wbig = Bd.ph(128, 8192, BF16)
        vtmp = Bd.ph(128, 512, BF16)
        for dt in range(DT):
            Bd.dma("sp", xnb[dt], self.dsl(self.XNB, "XNBs", b, dt, esz=2))
        for dst, wname in ((qf, "l2a_wq"), (kf, "l2a_wk"), (None, "l2a_wv")):
            for k in range(DT):
                Bd.dma("pool", wbig[:, 1024 * k:1024 * (k + 1)],
                       (self.inp[wname][128 * k:128 * (k + 1), :], dreg(wname)))
            for dt in range(DT):
                for c in range(4):
                    c0 = 512 * c
                    ps = Bd.psum(c % 4)
                    P.mm(ps.full, [(wbig[:, 1024 * k + 128 * dt:1024 * k + 128 * (dt + 1)],
                                    xnb[k][:, c0:c0 + 512]) for k in range(DT)])
                    if dst is not None:
                        self.act(dst[dt][:, c0:c0 + 512], ps.full, AF.Copy)
                    else:
                        self.act(vtmp.full, ps.full, AF.Copy)
                        pstt = Bd.psumT()
                        for hh in range(2):
                            h = 2 * dt + hh
                            o2 = 64 * hh
                            for j in range(4):
                                pt = 4 * c + j
                                self.transpose(
                                    (pstt.ap[0:128, 0:64], pstt.reg(0, 128, 0, 64)),
                                    (vtmp.ap[o2:o2 + 64, 128 * j:128 * (j + 1)],
                                     vtmp.reg(o2, o2 + 64, 128 * j, 128 * (j + 1))),
                                    bf16=True, kp=64, base=o2)
                                self.copy((vt[pt].ap[:, 65 * h:65 * h + 64],
                                           vt[pt].reg(0, 128, 65 * h, 65 * h + 64)),
                                          (pstt.ap[0:128, 0:64], pstt.reg(0, 128, 0, 64)))
        for pt in range(16):
            self.memset((vt[pt].ap[:, 64:16 * 65:65], vt[pt].reg(0, 128, 0, 16 * 65)), 1.0)
        Bd.reset_to(mk0)
        vf = [Bd.ph(128, T, BF16) for _ in range(DT)]
        wck = Bd.ph(64, 32 * 64, BF16)
        wcv = Bd.ph(64, 32 * 64, BF16)
        khc = Bd.ph(64, T, BF16)
        wgt = Bd.ph(64, 4, BF16)
        ckb = [Bd.ph(128, 127, BF16) for _ in range(8)]
        cvt = [Bd.ph(128, 65, BF16) for _ in range(16)]
        pcn = Bd.ph(128, 127, BF16)
        pcT = Bd.ph(128, 2048, BF16)
        den = Bd.ph(128, 1, F32)
        qm = Bd.ph(64, 1, F32)
        qmb = Bd.ph(64, 1, BF16)
        grow = Bd.ph(1, 4, F32)
        g4 = Bd.ph(4, 1, F32)
        Bd.dma("pool", (wck.ap.rearrange("q (j n) -> q j n", j=32), wck.reg()),
               (self.inp["l2a_wck"].rearrange("(j q) n -> q j n", q=64),
                dreg("l2a_wck")))
        Bd.dma("pool", (wcv.ap.rearrange("q (j n) -> q j n", j=32), wcv.reg()),
               (self.inp["l2a_wcv"].rearrange("(j q) n -> q j n", q=64),
                dreg("l2a_wcv")))
        Bd.dma("pool", wgt, (self.inp["l2a_wg"], dreg("l2a_wg")))
        for h in range(16):
            t2, o = h // 2, 64 * (h % 2)
            for pt in range(16):
                pstt = Bd.psumT()
                self.transpose((pstt.ap[0:64, 0:128], pstt.reg(0, 64, 0, 128)),
                               (vt[pt].ap[:, 65 * h:65 * h + 64],
                                vt[pt].reg(0, 128, 65 * h, 65 * h + 64)),
                               bf16=True, kp=128)
                self.copy((vf[t2].ap[o:o + 64, 128 * pt:128 * (pt + 1)],
                           vf[t2].reg(o, o + 64, 128 * pt, 128 * (pt + 1))),
                          (pstt.ap[0:64, 0:128], pstt.reg(0, 64, 0, 128)))
        impb = Bd.psum(6)
        first_imp = True
        for h in range(16):
            t2, o = h // 2, 64 * (h % 2)
            self.B.t("dve", lambda e, t2=t2, o=o: e.reduce_sum(
                qm.ap, qf[t2].ap[o:o + 64, :], axis=AX.X),
                r=[qf[t2].full], w=[qm.full])
            self.copy(qmb.full, qm.full)
            psg = Bd.psum(4)
            P.mm((psg.ap[0:4, 0:1], psg.reg()), [((wgt.ap, wgt.reg()), qmb.full)])
            self.act(g4.full, (psg.ap[0:4, 0:1], psg.reg()),
                     AF.Sigmoid, scale=1.0 / T, bias=self.vcol("l2a_bg", 0, P=4))
            psr = Bd.psum(5)
            P.mm((psr.ap[0:1, 0:4], psr.reg()),
                 [(g4.full, (self.identf.ap[0:4, 0:4], self.identf.reg(0, 4, 0, 4)))])
            self.copy(grow.full, (psr.ap[0:1, 0:4], psr.reg()))
            for j in range(3):
                psr2 = Bd.psum(4)
                P.mm((psr2.ap[0:64, 0:1], psr2.reg()),
                     [((self.ones_row_f.ap[0:1, 0:64], self.ones_row_f.reg(0, 1, 0, 64)),
                       (grow.ap[0:1, j:j + 1], grow.reg(0, 1, j, j + 1)))])
                self.copy((gcols.ap[:, 3 * h + j:3 * h + j + 1],
                           gcols.reg(0, 64, 3 * h + j, 3 * h + j + 1)),
                          (psr2.ap[0:64, 0:1], psr2.reg()))
            for which, wc, dstt in ((0, wck, ckb), (1, wcv, None)):
                psc = Bd.psum(4)
                kvsrc = kf if which == 0 else vf
                self.copy(khc.full, (kvsrc[t2].ap[o:o + 64, :],
                                     kvsrc[t2].reg(o, o + 64, 0, T)))
                pairs = []
                for j in range(32):
                    lhs = (wc.ap[0:64, 64 * j:64 * (j + 1)],
                           wc.reg(0, 64, 64 * j, 64 * (j + 1)))
                    rhs = (khc.ap[0:64, j:j + 16 * 126 + 1:16],
                           khc.reg(0, 64, 0, T))
                    pairs.append((lhs, rhs))
                P.mm((psc.ap[0:64, 0:127], psc.reg()), pairs)
                if which == 0:
                    self.ts((ckb[t2].ap[o:o + 64, 0:127],
                             ckb[t2].reg(o, o + 64, 0, 127)),
                            (psc.ap[0:64, 0:127], psc.reg()),
                            self.vcol("l2a_bck", 0, P=64), op0=ALU.add)
                else:
                    self.ts(pcn[0:64, 0:127], (psc.ap[0:64, 0:127], psc.reg()),
                            self.vcol("l2a_bcv", 0, P=64), op0=ALU.add)
                    pstc = Bd.psumT()
                    self.transpose((pstc.ap[0:127, 0:64], pstc.reg(0, 127, 0, 64)),
                                   pcn[0:64, 0:127], bf16=True, kp=64)
                    self.copy((cvt[h].ap[0:127, 0:64], cvt[h].reg(0, 127, 0, 64)),
                              (pstc.ap[0:127, 0:64], pstc.reg(0, 127, 0, 64)))
                    self.memset((cvt[h].ap[0:127, 64:65], cvt[h].reg(0, 127, 64, 65)), 1.0)
            for it in range(16):
                psS = Bd.psum(it % 2)
                P.mm((psS.ap[:, 0:127], psS.reg()),
                     [((qf[t2].ap[o:o + 64, 128 * it:128 * (it + 1)],
                        qf[t2].reg(o, o + 64, 128 * it, 128 * (it + 1))),
                       (ckb[t2].ap[o:o + 64, 0:127], ckb[t2].reg(o, o + 64, 0, 127)))])
                self.act(pcn[0:128, 0:127], (psS.ap[:, 0:127], psS.reg()),
                         AF.Exp, scale=0.125, accum=den.full)
                self.B.t("dve", lambda e: e.reciprocal(den.ap, den.ap),
                         r=[den.full], w=[den.full])
                self.ts(pcn[0:128, 0:127], pcn[0:128, 0:127], den.full, op0=ALU.mult)
                if DBG and b == 0 and h == 15 and it == 15:
                    self.dbg("l2_pcn_last", pcn)
                    self.dbg("l2_den_last", den)
                pstp = Bd.psumT()
                self.transpose((pstp.ap[0:127, 0:128], pstp.reg(0, 127, 0, 128)),
                               pcn[0:128, 0:127], bf16=True, kp=128)
                self.copy((pcT.ap[0:127, 128 * it:128 * (it + 1)],
                           pcT.reg(0, 127, 128 * it, 128 * (it + 1))),
                          (pstp.ap[0:127, 0:128], pstp.reg(0, 127, 0, 128)))
                P.mm((impb.ap[0:1, 0:127], impb.reg()),
                     [((self.ones_col_b.ap, self.ones_col_b.reg()),
                       pcn[0:128, 0:127])], start=first_imp, stop=(h == 15 and it == 15))
                first_imp = False
            for c in range(4):
                c0 = 512 * c
                pso = Bd.psum(2 + c % 2)
                P.mm((pso.ap[0:64, :], pso.reg()),
                     [((cvt[h].ap[0:127, 0:64], cvt[h].reg(0, 127, 0, 64)),
                       (pcT.ap[0:127, c0:c0 + 512], pcT.reg(0, 127, c0, c0 + 512)))])
                self.act((ocb[t2].ap[o:o + 64, c0:c0 + 512],
                          ocb[t2].reg(o, o + 64, c0, c0 + 512)),
                         (pso.ap[0:64, :], pso.reg()), AF.Copy)
        bi = Bd.ph(1, 127, F32)
        ohall = Bd.ph(1, 127, F32)
        onesr = Bd.ph(1, 127, F32)
        cum = Bd.ph(1, 127, F32)
        fo = Bd.ph(1, 127, F32)
        mrow = Bd.ph(1, 1, F32)
        ohc = Bd.ph(127, 1, BF16)
        wcol = Bd.ph(128, 1, F32)
        ecl = Bd.ph(127, 16 * 128, BF16)
        cE = self.lazy_const("c_Eb", lambda: np.ascontiguousarray(_l2_E().astype(BF)))
        Bd.dma("sp", ecl, (cE, dreg("c_Eb")))
        self.copy(bi.full, (impb.ap[0:1, 0:127], impb.reg()))
        self.memset(onesr.full, 1.0)
        self.memset(ohall.full, 0.0)
        for it16 in range(16):
            self.B.t("dve", lambda e: e.reduce_max(mrow.ap, bi.ap, axis=AX.X),
                     r=[bi.full], w=[mrow.full])
            self.ts(fo, bi, (mrow.ap, mrow.reg()), None, op0=ALU.is_ge, eng="dve")
            self.B.t("dve", lambda e: e.tensor_tensor_scan(
                cum.ap, onesr.ap, fo.ap, 0.0, op0=ALU.mult, op1=ALU.max),
                r=[onesr.full, fo.full], w=[cum.full])
            self.copy((fo.ap[0:1, 0:1], fo.reg(0, 1, 0, 1)),
                      (cum.ap[0:1, 0:1], cum.reg(0, 1, 0, 1)))
            self.tt(ALU.subtract, (fo.ap[0:1, 1:127], fo.reg(0, 1, 1, 127)),
                    (cum.ap[0:1, 1:127], cum.reg(0, 1, 1, 127)),
                    (cum.ap[0:1, 0:126], cum.reg(0, 1, 0, 126)))
            self.tt(ALU.add, ohall.full, ohall.full, fo.full)
            self.stt(bi, fo, -1e38, bi, op0=ALU.mult, op1=ALU.add, eng="dve")
        psoc = Bd.psum(4)
        P.mm((psoc.ap[0:127, 0:1], psoc.reg()),
             [((ohall.ap, ohall.reg()), (self.ones_row_f.ap[0:1, 0:1],
                                         self.ones_row_f.reg(0, 1, 0, 1)))])
        self.copy(ohc.full, (psoc.ap[0:127, 0:1], psoc.reg()))
        for pt in range(16):
            psw = Bd.psum(5)
            P.mm((psw.ap[:, 0:1], psw.reg()),
                 [((ecl.ap[0:127, 128 * pt:128 * (pt + 1)],
                    ecl.reg(0, 127, 128 * pt, 128 * (pt + 1))), ohc.full)])
            self.copy(wcol.full, (psw.ap[:, 0:1], psw.reg()))
            if DBG and b == 0 and pt == 0:
                self.dbg("l2_ohc", ohc)
                self.dbg("l2_wcol0", wcol)
            self.ts((lnw.ap[:, pt:pt + 1], lnw.reg(0, 128, pt, pt + 1)),
                    wcol.full, 1e-30, None, op0=ALU.max, eng="dve")
            self.act((lnw.ap[:, pt:pt + 1], lnw.reg(0, 128, pt, pt + 1)),
                     (lnw.ap[:, pt:pt + 1], lnw.reg(0, 128, pt, pt + 1)), AF.Ln)
            self.ts(wcol.full, wcol.full, 0.5, -1e38, op0=ALU.is_le, op1=ALU.mult)
            self.tt(ALU.add, (lnw.ap[:, pt:pt + 1], lnw.reg(0, 128, pt, pt + 1)),
                    (lnw.ap[:, pt:pt + 1], lnw.reg(0, 128, pt, pt + 1)), wcol.full)
        if DBG and b == 0:
            self.dbg("l2_ckb7", ckb[7])
            self.dbg("l2_cvt15", cvt[15])
            self.dbg("l2_pcT", pcT)
            self.dbg("l2_gcols", gcols)
            self.dbg("l2_ohall", ohall)
            self.dbg("l2_lnw", lnw)
            self.dbg("l2_ocb0", ocb[0])
        Bd.reset_to(mk0)
        ofin = [Bd.ph(128, T, BF16) for _ in range(DT)]
        wo8 = Bd.ph(128, 8192, BF16)
        pw = [Bd.ph(128, 512, BF16) for _ in range(2)]
        pw2 = [Bd.ph(128, 512, BF16) for _ in range(2)]
        denr = Bd.ph(1, 512, F32)
        reps = Bd.ph(64, 512, F32)
        repw = Bd.ph(64, 512, F32)
        osl = Bd.ph(64, 512, F32)
        owl = Bd.ph(64, 512, F32)
        xres = Bd.ph(128, T, F32)
        for k in range(DT):
            Bd.dma("pool", wo8[:, 1024 * k:1024 * (k + 1)],
                   (self.inp["l2a_wo"][128 * k:128 * (k + 1), :], dreg("l2a_wo")))
        for h in range(16):
            t2, o = h // 2, 64 * (h % 2)
            for ic in range(4):
                i0 = 512 * ic
                psel = Bd.psum(2)
                pwin = Bd.psum(3)
                for pt in range(16):
                    psS = Bd.psum(pt % 2)
                    P.mm(psS.full,
                         [((kf[t2].ap[o:o + 64, 128 * pt:128 * (pt + 1)],
                            kf[t2].reg(o, o + 64, 128 * pt, 128 * (pt + 1))),
                           (qf[t2].ap[o:o + 64, i0:i0 + 512],
                            qf[t2].reg(o, o + 64, i0, i0 + 512)))])
                    pb = pw[pt % 2]
                    self.act(pb.full, psS.full, AF.Exp, scale=0.125,
                             bias=(lnw.ap[:, pt:pt + 1], lnw.reg(0, 128, pt, pt + 1)))
                    P.mm((psel.ap[0:65, :], psel.reg()),
                         [((vt[pt].ap[:, 65 * h:65 * (h + 1)],
                            vt[pt].reg(0, 128, 65 * h, 65 * (h + 1))), pb.full)],
                         start=(pt == 0), stop=(pt == 15))
                    if pt >= 12:
                        pb2 = pw2[pt % 2]
                        self.act(pb2.full, psS.full, AF.Exp, scale=0.125)
                        P.mm((pwin.ap[0:65, :], pwin.reg()),
                             [((vt[pt].ap[:, 65 * h:65 * (h + 1)],
                                vt[pt].reg(0, 128, 65 * h, 65 * (h + 1))), pb2.full)],
                             start=(pt == 12), stop=(pt == 15))
                for which, pso, dst in ((0, psel, osl), (1, pwin, owl)):
                    self.B.t("dve", lambda e, pso=pso: e.reciprocal(
                        denr.ap, pso.ap[64:65, :]), r=[pso.full], w=[denr.full])
                    psr = Bd.psum(4 + which)
                    P.mm((psr.ap[0:64, :], psr.reg()),
                         [((self.ones_row_f.ap[0:1, 0:64],
                            self.ones_row_f.reg(0, 1, 0, 64)), denr.full)])
                    self.copy(reps.full, (psr.ap[0:64, :], psr.reg()))
                    self.tt(ALU.mult, dst.full, (pso.ap[0:64, :], pso.reg()), reps.full)
                if DBG and b == 0 and h == 0 and ic == 0:
                    self.dbg("l2_osl00", osl)
                    self.dbg("l2_owl00", owl)
                self.ts(repw.full, (ocb[t2].ap[o:o + 64, i0:i0 + 512],
                                    ocb[t2].reg(o, o + 64, i0, i0 + 512)),
                        (gcols.ap[:, 3 * h:3 * h + 1], gcols.reg(0, 64, 3 * h, 3 * h + 1)),
                        None, op0=ALU.mult)
                self.stt(repw, osl.full, (gcols.ap[:, 3 * h + 1:3 * h + 2],
                                          gcols.reg(0, 64, 3 * h + 1, 3 * h + 2)),
                         repw.full, op0=ALU.mult, op1=ALU.add)
                self.stt((ofin[t2].ap[o:o + 64, i0:i0 + 512],
                          ofin[t2].reg(o, o + 64, i0, i0 + 512)),
                         owl.full, (gcols.ap[:, 3 * h + 2:3 * h + 3],
                                    gcols.reg(0, 64, 3 * h + 2, 3 * h + 3)),
                         repw.full, op0=ALU.mult, op1=ALU.add)
        if DBG and b == 0:
            self.dbg("l2_ofin0", ofin[0])
        for dt in range(DT):
            Bd.dma("sp", xres, self.dsl(self.X, "Xs", b, dt))
            for c in range(4):
                c0 = 512 * c
                ps = Bd.psum(6)
                P.mm(ps.full, [(wo8[:, 1024 * k + 128 * dt:1024 * (k) + 128 * (dt + 1)],
                                ofin[k][:, c0:c0 + 512]) for k in range(DT)])
                self.tt(ALU.add, xres[:, c0:c0 + 512], xres[:, c0:c0 + 512], ps.full)
            Bd.dma("sp", self.dsl(self.X, "Xs", b, dt), xres)

    def l2_ffn(self, b):
        P, Bd = self, self.B
        Bd.ph_reset()
        xnb = [Bd.ph(128, T, BF16) for _ in range(DT)]
        gw = Bd.ph(128, 64, BF16)
        lg = Bd.ph(8, T, F32)
        Wsb = Bd.ph(8, T, F32)
        oh8 = Bd.ph(8, 1024, F32)
        Lt = Bd.ph(128, 8, F32)
        m1 = Bd.ph(128, 1, F32)
        nm1 = Bd.ph(128, 1, F32)
        m2 = Bd.ph(128, 1, F32)
        rd = Bd.ph(128, 1, F32)
        msk = Bd.ph(128, 8, F32)
        W8 = Bd.ph(128, 8, F32)
        acc = [Bd.ph(128, 1024, F32) for _ in range(DT)]
        zb = [Bd.ph(128, 1024, BF16) for _ in range(16)]
        sb = [Bd.ph(128, 1024, BF16) for _ in range(16)]
        w1b = [Bd.ph(128, 1024, BF16) for _ in range(2)]
        w2b = [Bd.ph(128, 1024, BF16) for _ in range(2)]
        wgb = [Bd.ph(128, 2048, BF16) for _ in range(2)]
        w3b = [Bd.ph(128, 2048, BF16) for _ in range(2)]
        wrp = Bd.ph(128, 1024, F32)
        t1f = Bd.ph(128, 512, F32)
        ysb = Bd.ph(128, 512, F32)
        yt = Bd.ph(128, 512, F32)
        xres = Bd.ph(128, 1024, F32)
        for dt in range(DT):
            Bd.dma("sp", xnb[dt], self.dsl(self.XNB, "XNBs", b, dt, esz=2))
        Bd.dma("pool", (gw.ap.rearrange("p (k n) -> p k n", k=8), gw.reg()),
               (self.inp["l2f_gw"].rearrange("(k p) n -> p k n", p=128),
                dreg("l2f_gw")))
        for c in range(4):
            c0 = 512 * c
            psl = Bd.psum(4)
            P.mm((psl.ap[0:8, :], psl.reg()),
                 [((gw.ap[:, 8 * k:8 * (k + 1)], gw.reg(0, 128, 8 * k, 8 * (k + 1))),
                   xnb[k][:, c0:c0 + 512]) for k in range(DT)])
            self.ts((lg.ap[0:8, c0:c0 + 512], lg.reg(0, 8, c0, c0 + 512)),
                    (psl.ap[0:8, :], psl.reg()),
                    self.vcol("l2f_gateb", 0, P=8), op0=ALU.add)
        for mt in range(16):
            psf = Bd.psum(5)
            self.transpose((psf.ap[0:128, 0:8], psf.reg()),
                           (lg.ap[0:8, 128 * mt:128 * (mt + 1)],
                            lg.reg(0, 8, 128 * mt, 128 * (mt + 1))), bf16=False, kp=8)
            self.copy(Lt.full, (psf.ap[0:128, 0:8], psf.reg()))
            self.red(m1, Lt, op=ALU.max)
            self.ts(nm1, m1, -1.0, None, op0=ALU.mult)
            self.ts(msk, Lt, m1.full, -1e38, op0=ALU.is_ge, op1=ALU.mult)
            self.tt(ALU.add, msk, msk, Lt)
            self.red(m2, msk, op=ALU.max)
            self.tt(ALU.subtract, rd, m2, m1)
            self.act(rd, rd, AF.Exp)
            self.ts(rd, rd, 1.0, None, op0=ALU.add)
            self.B.t("dve", lambda e: e.reciprocal(rd.ap, rd.ap),
                     r=[rd.full], w=[rd.full])
            self.act(W8, Lt, AF.Exp, bias=nm1.full)
            self.ts(msk, Lt, m2.full, 1.0, op0=ALU.is_ge, op1=ALU.mult)
            self.tt(ALU.mult, W8, W8, msk)
            self.ts(W8, W8, rd.full, 0.0, op0=ALU.mult, op1=ALU.add)
            psf2 = Bd.psum(6)
            self.transpose((psf2.ap[0:8, 0:128], psf2.reg()), W8.full,
                           bf16=False, kp=128)
            self.copy((Wsb.ap[0:8, 128 * mt:128 * (mt + 1)],
                       Wsb.reg(0, 8, 128 * mt, 128 * (mt + 1))),
                      (psf2.ap[0:8, 0:128], psf2.reg()))
        oh8np = np.zeros((8, 1024), np.float32)
        for e in range(8):
            oh8np[e, 128 * e:128 * (e + 1)] = 1.0
        Bd.dma("sp", oh8, (self.lazy_const("c_oh8", lambda: oh8np), dreg("c_oh8")))
        for c2 in range(2):
            q0 = 1024 * c2
            for dt in range(DT):
                self.memset(acc[dt], 0.0)
            for e in range(8):
                psw = Bd.psum(5, 2)
                for hh in range(2):
                    P.mm((psw.ap[:, 512 * hh:512 * (hh + 1)],
                          psw.reg(0, 128, 512 * hh, 512 * (hh + 1))),
                         [((oh8.ap[:, 128 * e:128 * (e + 1)],
                            oh8.reg(0, 8, 128 * e, 128 * (e + 1))),
                           (Wsb.ap[0:8, q0 + 512 * hh:q0 + 512 * (hh + 1)],
                            Wsb.reg(0, 8, q0 + 512 * hh, q0 + 512 * (hh + 1))))])
                self.copy(wrp, psw)
                for ft in range(16):
                    wa_, wb_ = w1b[ft % 2], w2b[ft % 2]
                    Bd.dma("pool", (wa_.ap, wa_.reg()),
                           (self.inp["l2f_w1"].rearrange("e (kt p) m -> e p kt m", p=128)
                            [e, :, :, 128 * ft:128 * (ft + 1)], dreg("l2f_w1")))
                    Bd.dma("pool", (wb_.ap, wb_.reg()),
                           (self.inp["l2f_w2"].rearrange("e (kt p) m -> e p kt m", p=128)
                            [e, :, :, 128 * ft:128 * (ft + 1)], dreg("l2f_w2")))
                    for hh in range(2):
                        h0 = q0 + 512 * hh
                        psa = Bd.psum(hh % 2)
                        psb_ = Bd.psum(2 + hh % 2)
                        P.mm(psa.full, [((wa_.ap[:, 128 * k:128 * (k + 1)],
                                          wa_.reg(0, 128, 128 * k, 128 * (k + 1))),
                                         xnb[k][:, h0:h0 + 512]) for k in range(DT)])
                        P.mm(psb_.full, [((wb_.ap[:, 128 * k:128 * (k + 1)],
                                           wb_.reg(0, 128, 128 * k, 128 * (k + 1))),
                                          xnb[k][:, h0:h0 + 512]) for k in range(DT)])
                        self.ts(t1f, psa.full, self.vcol(f"l2f_b1_{e}", ft),
                                op0=ALU.add)
                        wait = self.stt((zb[ft].ap[:, 512 * hh:512 * (hh + 1)],
                                         zb[ft].reg(0, 128, 1024 * hh // 2,
                                                    1024 * (hh + 1) // 2)),
                                        psb_.full, self.vcol(f"l2f_b2_{e}", ft),
                                        t1f.full, op0=ALU.add, op1=ALU.mult)
                for gt in range(16):
                    wg_ = wgb[gt % 2]
                    Bd.dma("pool", (wg_.ap, wg_.reg()),
                           (self.inp["l2f_wg"].rearrange("e (kt p) m -> e p kt m", p=128)
                            [e, :, :, 128 * gt:128 * (gt + 1)], dreg("l2f_wg")))
                    for hh in range(2):
                        psg = Bd.psum(4 + hh % 2)
                        P.mm(psg.full, [((wg_.ap[:, 128 * k:128 * (k + 1)],
                                          wg_.reg(0, 128, 128 * k, 128 * (k + 1))),
                                         (zb[k].ap[:, 512 * hh:512 * (hh + 1)],
                                          zb[k].reg(0, 128, 1024 * hh // 2,
                                                    1024 * (hh + 1) // 2)))
                                        for k in range(16)])
                        self.act((sb[gt].ap[:, 512 * hh:512 * (hh + 1)],
                                  sb[gt].reg(0, 128, 1024 * hh // 2,
                                             1024 * (hh + 1) // 2)),
                                 psg.full, AF.Silu, bias=self.vcol(f"l2f_bg_{e}", gt))
                for dt in range(DT):
                    w3_ = w3b[dt % 2]
                    Bd.dma("pool", (w3_.ap, w3_.reg()),
                           (self.inp["l2f_w3"].rearrange("e (kt p) m -> e p kt m", p=128)
                            [e, :, :, 128 * dt:128 * (dt + 1)], dreg("l2f_w3")))
                    for hh in range(2):
                        psy = Bd.psum(hh % 2)
                        P.mm(psy.full, [((w3_.ap[:, 128 * k:128 * (k + 1)],
                                          w3_.reg(0, 128, 128 * k, 128 * (k + 1))),
                                         (sb[k].ap[:, 512 * hh:512 * (hh + 1)],
                                          sb[k].reg(0, 128, 1024 * hh // 2,
                                                    1024 * (hh + 1) // 2)))
                                        for k in range(16)])
                        self.ts(ysb, psy.full, self.vcol(f"l2f_b3_{e}", dt),
                                op0=ALU.add)
                        self.tt(ALU.mult, yt, ysb,
                                (wrp.ap[:, 512 * hh:512 * (hh + 1)],
                                 wrp.reg(0, 128, 512 * hh, 512 * (hh + 1))), eng="pool")
                        self.tt(ALU.add, (acc[dt].ap[:, 512 * hh:512 * (hh + 1)],
                                          acc[dt].reg(0, 128, 512 * hh, 512 * (hh + 1))),
                                (acc[dt].ap[:, 512 * hh:512 * (hh + 1)],
                                 acc[dt].reg(0, 128, 512 * hh, 512 * (hh + 1))),
                                yt.full, eng="pool")
            for dt in range(DT):
                Bd.dma("sp", xres, self.dsl(self.X, "Xs", b, dt, q0, q0 + 1024))
                self.tt(ALU.add, xres.full, xres.full, acc[dt].full)
                Bd.dma("sp", self.dsl(self.X, "Xs", b, dt, q0, q0 + 1024), xres)

    def l0_ffn(self, b):
        P, Bd = self, self.B
        Bd.ph_reset()
        xnb = [Bd.ph(128, T, BF16) for _ in range(DT)]
        w2 = Bd.ph(128, 32 * 1024, BF16)
        w1b = [Bd.ph(128, 1024, BF16) for _ in range(2)]
        hsb = Bd.ph(128, 32 * 512, BF16)
        ssb = [Bd.ph(128, 512, F32) for _ in range(DT)]
        sb16 = [Bd.ph(128, 512, BF16) for _ in range(DT)]
        sq16 = [Bd.ph(128, 512, BF16) for _ in range(DT)]
        mean = Bd.ph(1, 512, F32)
        e2 = Bd.ph(1, 512, F32)
        inv = Bd.ph(1, 512, F32)
        msr = Bd.ph(1, 512, F32)
        t1 = Bd.ph(128, 512, F32)
        xres = Bd.ph(128, 512, F32)
        xn32c = Bd.ph(128, 512, F32)
        for dt in range(DT):
            Bd.dma("sp", xnb[dt], self.dsl(self.XNB, "XNBs", b, dt, esz=2))
        for f in range(32):
            Bd.dma("pool", w2[:, 1024 * f:1024 * (f + 1)],
                   (self.inp["l0f_w2"][128 * f:128 * (f + 1), :], dreg("l0f_w2")))
        onesb = self.ones_col_b.full
        for c in range(4):
            c0 = 512 * c
            for f in range(32):
                wb = w1b[f % 2]
                Bd.dma("pool", wb,
                       (self.inp["l0f_w1"].rearrange("(kt p) m -> p kt m", p=128)[:, :, 128 * f:128 * (f + 1)],
                        dreg("l0f_w1")))
                ps_h = Bd.psum(f % 2)
                P.mm(ps_h.full, [(wb[:, 128 * k:128 * (k + 1)],
                                  xnb[k][:, c0:c0 + 512]) for k in range(DT)])
                self.act(hsb[:, 512 * f:512 * (f + 1)], ps_h.full, AF.Relu,
                         bias=self.vcol("l0f_b1", f))
            for dt in range(DT):
                ps_o = Bd.psum(2 + dt % 2)
                P.mm(ps_o.full, [(w2[:, 1024 * f + 128 * dt:1024 * f + 128 * (dt + 1)],
                                  hsb[:, 512 * f:512 * (f + 1)])
                                 for f in range(32)])
                Bd.dma("sp", xn32c, self.dsl(self.XN32, "XN32s", b, dt, c0, c0 + 512))
                self.stt(ssb[dt].full, ps_o.full, self.vcol("l0f_b2", dt),
                         (xn32c.ap, xn32c.reg()), op0=ALU.add, op1=ALU.add)
                self.copy(sb16[dt].full, ssb[dt].full)
                self.tt(ALU.mult, sq16[dt].full, sb16[dt].full, sb16[dt].full)
            ps_s, ps_q = Bd.psum(4), Bd.psum(5)
            P.mm((ps_s.ap[0:1, :], ps_s.reg()), [(onesb, sb16[dt].full) for dt in range(DT)])
            P.mm((ps_q.ap[0:1, :], ps_q.reg()), [(onesb, sq16[dt].full) for dt in range(DT)])
            self.ts(mean.full, (ps_s.ap[0:1, :], ps_s.reg()), 1.0 / D, op0=ALU.mult)
            self.ts(e2.full, (ps_q.ap[0:1, :], ps_q.reg()), 1.0 / D, op0=ALU.mult)
            self.tt(ALU.mult, inv.full, mean.full, mean.full)
            self.tt(ALU.subtract, inv.full, e2.full, inv.full)
            self.act(inv.full, inv.full, AF.Ln, bias=1e-5)
            self.act(inv.full, inv.full, AF.Exp, scale=-0.5)
            self.stt(msr.full, mean.full, -1.0, inv.full, op0=ALU.mult, op1=ALU.mult)
            psA_, psB_ = Bd.psum(4), Bd.psum(5)
            P.mm(psA_.full, [((self.ones_row_f.ap, self.ones_row_f.reg()), inv.full)])
            P.mm(psB_.full, [((self.ones_row_f.ap, self.ones_row_f.reg()), msr.full)])
            for dt in range(DT):
                self.tt(ALU.mult, t1.full, ssb[dt].full, psA_.full)
                self.tt(ALU.add, t1.full, t1.full, psB_.full)
                self.ts(t1.full, t1.full, self.vcol("l0f_gamma", dt),
                        self.vcol("l0f_beta", dt), op0=ALU.mult, op1=ALU.add)
                if DBG and b == 0 and c == 0 and dt == 0:
                    self.dbg("ffn_out00", t1)
                Bd.dma("sp", xres, self.dsl(self.X, "Xs", b, dt, c0, c0 + 512))
                self.tt(ALU.add, xres.full, xres.full, t1.full)
                Bd.dma("sp", self.dsl(self.X, "Xs", b, dt, c0, c0 + 512), xres)
            if DBG and b == 0 and c == 0:
                self.dbg("ffn_h0", Tile(hsb.ap[:, 0:512], "S", 0, hsb.byte0, BF16, 128, 512))
                self.dbg("ffn_s0", ssb[0])
                self.dbg("ffn_mean", mean)
                self.dbg("ffn_inv", inv)


DBG = []


def build_custom(stage_fns):
    nc = bass.Bass("TRN2", target_bir_lowering=False, debug=False)
    with ExitStack() as S:
        blk = S.enter_context(nc.Block())
        P = Prog(nc, S)
        P.load_consts()
        for fn in stage_fns:
            fn(P)
        P.B.finish(blk)
    return nc


def build_program(n_blocks=4, dump=None):
    nc = bass.Bass("TRN2", target_bir_lowering=False, debug=False)
    with ExitStack() as S:
        blk = S.enter_context(nc.Block())
        P = Prog(nc, S, n_blocks)
        P.load_consts()
        if n_blocks >= 4:
            P.l3_prep_weights()
        for i in range(n_blocks):
            for b in range(B):
                P.ln_entry(b)
                if i == 0:
                    P.l0_attn(b)
                    P.l0_ffn(b)
                elif i == 1:
                    P.l1_attn(b)
                    P.l1_ffn(b)
                elif i == 2:
                    P.l2_attn(b)
                    P.l2_ffn(b)
                elif i == 3:
                    P.l3_attn(b)
                    P.l3_ffn(b)
        for b in range(B):
            for dt in range(DT):
                P.B.dma("sp", (P.xout[b, 128 * dt:128 * (dt + 1), :], dreg("xout")),
                        P.dsl(P.X, "Xs", b, dt))
        if dump:
            for j, (dr, name, shape_esz) in enumerate(dump):
                pass
        P.B.finish(blk)
    return nc


_PROG_CACHE = {}


def kernel(**inputs):
    nc = build_program(4)
    in_maps = [stage_inputs(inputs, c) for c in range(NCORES)]
    res = run_bass_kernel_spmd(nc, in_maps, core_ids=list(range(NCORES)))
    outs = []
    for c in range(NCORES):
        o = np.asarray(res.results[c]["xout"], np.float32)
        outs.append(o.transpose(0, 2, 1))
    return np.ascontiguousarray(np.concatenate(outs, axis=0))


# version 02
# speedup vs baseline: 1.0084x; 1.0084x over previous
import numpy as np
import ml_dtypes
import concourse.bass as bass
import concourse.mybir as mybir
from concourse.bass_utils import run_bass_kernel_spmd
from contextlib import ExitStack

F32 = mybir.dt.float32
BF16 = mybir.dt.bfloat16
U8 = mybir.dt.uint8
AF = mybir.ActivationFunctionType
ALU = mybir.AluOpType
AX = mybir.AxisListType
BF = ml_dtypes.bfloat16

NCORES = 8
B = 2
T = 2048
D = 1024
DT = D // 128

SB_BYTES = 206 * 1024
BUCKET = 4096


class Reg:
    __slots__ = ("space", "p0", "p1", "b0", "b1")

    def __init__(self, space, p0, p1, b0, b1):
        self.space, self.p0, self.p1, self.b0, self.b1 = space, p0, p1, b0, b1


class Tile:
    __slots__ = ("ap", "space", "p0", "byte0", "dtype", "P", "N", "esz")

    def __init__(self, ap, space, p0, byte0, dtype, P, N):
        self.ap, self.space, self.p0, self.byte0 = ap, space, p0, byte0
        self.dtype, self.P, self.N = dtype, P, N
        self.esz = mybir.dt.size(dtype)

    def __getitem__(self, idx):
        ps, es = idx
        p0, p1 = (ps.start or 0), (ps.stop if ps.stop is not None else self.P)
        e0 = es.start or 0
        e1 = es.stop if es.stop is not None else self.N
        step = es.step or 1
        ap = self.ap[p0:p1, e0:e1:step] if step != 1 else self.ap[p0:p1, e0:e1]
        return ap, self.reg(p0, p1, e0, e1)

    def reg(self, p0=0, p1=None, e0=0, e1=None):
        p1 = self.P if p1 is None else p1
        e1 = self.N if e1 is None else e1
        return Reg(self.space, self.p0 + p0, self.p0 + p1,
                   self.byte0 + e0 * self.esz, self.byte0 + e1 * self.esz)

    @property
    def full(self):
        return self.ap, self.reg()


def dreg(name, lo=0, hi=1 << 62):
    return Reg("D:" + name, 0, 128, lo, hi)


class MK:
    ENGS = ("pe", "act", "dve", "pool", "sp")

    def __init__(self, nc, stack):
        self.nc = nc
        self.S = stack
        self.arena = stack.enter_context(nc.sbuf_tensor("AR", [128, SB_BYTES], U8))
        self.psA = stack.enter_context(nc.psum_tensor("PSA", [128, 3584], F32))
        self.psT = stack.enter_context(nc.psum_tensor("PST", [128, 1024], BF16))
        self.NLANE = 4
        self.sem = {}
        for s in ("pe", "act", "dve", "pool", "sp"):
            self.sem[s] = stack.enter_context(nc.semaphore("s_" + s))
        for q in ("sq", "pq"):
            for l in range(self.NLANE):
                self.sem[q + str(l)] = stack.enter_context(nc.semaphore("s_" + q + str(l)))
        self.count = {s: 0 for s in self.sem}
        self.qidx = {"sq": 0, "pq": 0}
        self.vc = {e: {} for e in self.ENGS}
        self.tasks = {e: [] for e in self.ENGS}
        self.acc = {}
        self.off = 0
        self.ph_off = None
        self.n_waits = 0
        self.n_tasks = 0
        self.serial = set()
        self.same_engine_sync = True

    def _carve(self, P, N, dtype, at):
        esz = mybir.dt.size(dtype)
        o = (at + 31) // 32 * 32
        nb = N * esz
        assert o + nb <= SB_BYTES, f"SBUF arena overflow: {o+nb}"
        ap = self.arena[0:P, o:o + nb].bitcast(dtype)
        return Tile(ap, "S", 0, o, dtype, P, N), o + nb

    def carve(self, P, N, dtype):
        t, self.off = self._carve(P, N, dtype, self.off)
        assert self.ph_off is None, "carve() after phase allocs started"
        return t

    def ph_reset(self):
        self.ph_off = self.off

    def mark(self):
        return self.ph_off

    def reset_to(self, m):
        self.ph_off = m

    def ph(self, P, N, dtype):
        if self.ph_off is None:
            self.ph_off = self.off
        t, self.ph_off = self._carve(P, N, dtype, self.ph_off)
        return t

    def psum(self, bank, nbanks=1):
        assert bank + nbanks <= 7
        ap = self.psA[:, 512 * bank:512 * (bank + nbanks)]
        return Tile(ap, "P", 0, 512 * 4 * bank, F32, 128, 512 * nbanks)

    def psumT(self):
        return Tile(self.psT[:, :], "T", 0, 0, BF16, 128, 1024)

    @staticmethod
    def _buckets(reg):
        if reg.space in ("P", "T"):
            lo = reg.b0 // 2048
            hi = (reg.b1 - 1) // 2048
            return [(reg.space, i) for i in range(lo, hi + 1)]
        if reg.space == "S":
            lo = reg.b0 // BUCKET
            hi = (reg.b1 - 1) // BUCKET
            return [("S", i) for i in range(lo, hi + 1)]
        return [(reg.space, 0)]

    @staticmethod
    def _overlap(a, b):
        if a.space != b.space:
            return False
        if a.space in ("P", "T"):
            return (a.b0 // 2048 <= (b.b1 - 1) // 2048
                    and b.b0 // 2048 <= (a.b1 - 1) // 2048)
        return a.p0 < b.p1 and b.p0 < a.p1 and a.b0 < b.b1 and b.b0 < a.b1

    def _conflicts(self, reg, kind, waits, self_stream):
        for bk in self._buckets(reg):
            for (r2, k2, s2, v2) in self.acc.get(bk, ()):
                if k2 == "r" and kind == "r":
                    continue
                if s2 == self_stream:
                    continue
                if self._overlap(reg, r2):
                    if waits.get(s2, 0) < v2:
                        waits[s2] = v2

    def _record(self, reg, kind, stream, val):
        for bk in self._buckets(reg):
            lst = self.acc.setdefault(bk, [])
            if kind == "w":
                lst[:] = [a for a in lst if not (
                    reg.space == a[0].space and reg.p0 <= a[0].p0 and reg.p1 >= a[0].p1
                    and reg.b0 <= a[0].b0 and reg.b1 >= a[0].b1)]
            lst.append((reg, kind, stream, val))
            if len(lst) > 600:
                mx = {}
                for (r2, k2, s2, v2) in lst:
                    mx[s2] = max(mx.get(s2, 0), v2)
                sp = lst[0][0].space
                lst[:] = [(Reg(sp, 0, 128, 0, 1 << 62), "w", s2, v2)
                          for s2, v2 in mx.items()]

    @staticmethod
    def _regs(lst):
        out = []
        for x in (lst or ()):
            if isinstance(x, Tile):
                out.append(x.reg())
            elif isinstance(x, tuple):
                out.append(x[1])
            else:
                out.append(x)
        return out

    def t(self, eng, emit, r=None, w=None, inc=1, tag="", rs=None):
        rr, ww = self._regs(r), self._regs(w)
        waits = {}
        skip = None if self.same_engine_sync else eng
        for reg in rr:
            self._conflicts(reg, "r", waits, skip)
        for reg in ww:
            self._conflicts(reg, "w", waits, skip)
        strict = self._regs(rs) if rs else []
        for reg in strict:
            self._conflicts(reg, "r", waits, None)
        if strict:
            rr = rr + strict
        if eng in self.serial:
            for s2 in self.count:
                if s2 != eng and self.count[s2] > 0:
                    waits[s2] = self.count[s2]
        need = []
        for s, v in waits.items():
            if self.vc[eng].get(s, 0) < v:
                need.append((s, v))
                self.vc[eng][s] = v
        self.count[eng] += inc
        val = self.count[eng]
        for reg in rr:
            self._record(reg, "r", eng, val)
        for reg in ww:
            self._record(reg, "w", eng, val)
        sem = self.sem
        myname = eng
        self.n_waits += len(need)
        self.n_tasks += 1
        import sys as _sys
        fr = _sys._getframe(1)
        loc = f"{fr.f_code.co_name}:{fr.f_lineno}"
        fr2 = fr.f_back
        if fr2 is not None:
            loc += f"<{fr2.f_code.co_name}:{fr2.f_lineno}"

        def run(engine, need=need, emit=emit, val=val, inc=inc, tag=tag, loc=loc):
            for s, v in need:
                engine.wait_ge(sem[s], v)
            try:
                ins = emit(engine)
            except Exception as ex:
                raise RuntimeError(f"task emit failed at {loc} tag={tag}: {ex}") from ex
            ins.then_inc(sem[myname], inc)
        self.tasks[eng].append(run)
        return val

    def dma(self, q, out, in_, r=None, w=None, tag=""):
        qname = "sq" if q == "sp" else "pq"
        eng = "sp" if q == "sp" else "pool"
        lane = self.qidx[qname] % self.NLANE
        self.qidx[qname] += 1
        qs = qname + str(lane)

        def ap_of(x):
            if isinstance(x, Tile):
                return x.ap
            if isinstance(x, tuple):
                return x[0]
            return x
        out_ap, in_ap = ap_of(out), ap_of(in_)
        rr = self._regs([in_] if not isinstance(in_, list) else in_)
        ww = self._regs([out] if not isinstance(out, list) else out)
        waits = {}
        for reg in rr:
            self._conflicts(reg, "r", waits, None)
        for reg in ww:
            self._conflicts(reg, "w", waits, None)
        if ("dma" in self.serial) or (eng in self.serial):
            for s2 in self.count:
                if s2 != qs and self.count[s2] > 0:
                    waits[s2] = self.count[s2]
        if self.count[qs] > waits.get(qs, 0):
            waits[qs] = self.count[qs]
        need = []
        for s, v in waits.items():
            if self.vc[eng].get(s, 0) < v:
                need.append((s, v))
                self.vc[eng][s] = v
        self.count[qs] += 16
        val = self.count[qs]
        for reg in rr:
            self._record(reg, "r", qs, val)
        for reg in ww:
            self._record(reg, "w", qs, val)
        sem = self.sem
        self.n_tasks += 1
        self.n_waits += len(need)

        def run(engine, need=need, val=val):
            for s, v in need:
                engine.wait_ge(sem[s], v)
            engine.dma_start(out=out_ap, in_=in_ap).then_inc(sem[qs], 16)
        self.tasks[eng].append(run)
        return val

    def finish(self, block):
        fin = [(s, self.count[s]) for s in self.count if self.count[s] > 0]

        def last(engine):
            for s, v in fin:
                if s != "sp":
                    engine.wait_ge(self.sem[s], v)
        self.tasks["sp"].append(last)
        engmap = {"pe": block.tensor, "act": block.scalar, "dve": block.vector,
                  "pool": block.gpsimd, "sp": block.sync}
        for eng in self.ENGS:
            tl = self.tasks[eng]
            if not tl:
                continue

            def body(engine, tl=tl):
                for fn in tl:
                    fn(engine)
            engmap[eng](body)


_VEC_REG = []


def _vec_layout():
    if _VEC_REG:
        return _VEC_REG
    def add(name, ncols):
        _VEC_REG.append((name, ncols))
    add("l0_ba", 8); add("l0_bx", 8); add("l0_lmbd", 8)
    add("l0f_b1", 32); add("l0f_b2", 8); add("l0f_gamma", 8); add("l0f_beta", 8)
    add("l1a_bout", 8)
    for l in range(2):
        add(f"l1f_lng{l}", 8); add(f"l1f_lnb{l}", 8)
        add(f"l1f_b1a{l}", 22); add(f"l1f_b1g{l}", 22); add(f"l1f_b2_{l}", 8)
        add(f"l1f_qs{l}", 1); add(f"l1f_qss{l}", 1)
        add(f"l1f_ks{l}", 1); add(f"l1f_kss{l}", 1)
    add("l2a_bck", 1); add("l2a_bcv", 1); add("l2a_bg", 1)
    add("l2f_gateb", 1)
    for e in range(8):
        add(f"l2f_b1_{e}", 16); add(f"l2f_b2_{e}", 16)
        add(f"l2f_bg_{e}", 16); add(f"l2f_b3_{e}", 8)
    return _VEC_REG


def _vec_cols():
    lay = _vec_layout()
    off, out = 0, {}
    for name, n in lay:
        out[name] = (off, n)
        off += n
    return out, off


def _col128(v):
    v = np.asarray(v, np.float32).reshape(-1)
    assert v.size % 128 == 0 or v.size <= 128, v.size
    if v.size % 128:
        v = np.pad(v, (0, 128 - v.size % 128))
    return v.reshape(-1, 128).T


def _pad_col(v, n=128):
    v = np.asarray(v, np.float32).reshape(-1)
    return np.pad(v, (0, n - v.size))[:, None]


def stage_vecs(I):
    cols, NV = _vec_cols()
    V = np.zeros((128, NV), np.float32)

    def put(name, arr):
        o, n = cols[name]
        arr = np.asarray(arr, np.float32)
        assert arr.shape == (128, n) or (n == 1 and arr.shape == (128, 1)), (name, arr.shape)
        V[:, o:o + n] = arr
    put("l0_ba", _col128(I["L0_attn_ba"]))
    put("l0_bx", _col128(I["L0_attn_bx"]))
    put("l0_lmbd", _col128(I["L0_attn_lmbd"]))
    put("l0f_b1", _col128(I["L0_ffn_b1"][0]))
    put("l0f_b2", _col128(I["L0_ffn_b2"][0]))
    put("l0f_gamma", _col128(I["L0_ffn_gamma"]))
    put("l0f_beta", _col128(I["L0_ffn_beta"]))
    put("l1a_bout", _col128(I["L1_attn_bout"]))
    HID = 2730
    for l in range(2):
        put(f"l1f_lng{l}", _col128(I["L1_ffn_ln_g"][l]))
        put(f"l1f_lnb{l}", _col128(I["L1_ffn_ln_b"][l]))
        b1 = np.asarray(I["L1_ffn_b1"][l], np.float32)
        b1a = np.pad(b1[:HID], (0, 22 * 128 - HID))
        b1g = np.pad(b1[HID:], (0, 22 * 128 - HID))
        put(f"l1f_b1a{l}", b1a.reshape(22, 128).T)
        put(f"l1f_b1g{l}", b1g.reshape(22, 128).T)
        put(f"l1f_b2_{l}", _col128(I["L1_ffn_b2"][l]))
        qs = np.asarray(I["L1_ffn_q_scale"][l], np.float32)
        ks = np.asarray(I["L1_ffn_k_scale"][l], np.float32)
        sig = np.concatenate([qs[32:], qs[:32]])
        sigk = np.concatenate([ks[32:], ks[:32]])
        put(f"l1f_qs{l}", _pad_col(np.tile(qs, 2)))
        put(f"l1f_qss{l}", _pad_col(np.tile(sig, 2)))
        put(f"l1f_ks{l}", _pad_col(np.tile(ks, 2)))
        put(f"l1f_kss{l}", _pad_col(np.tile(sigk, 2)))
    put("l2a_bck", _pad_col(I["L2_attn_bck"]))
    put("l2a_bcv", _pad_col(I["L2_attn_bcv"]))
    put("l2a_bg", _pad_col(I["L2_attn_bg"]))
    put("l2f_gateb", _pad_col(I["L2_ffn_gate_b"]))
    for e in range(8):
        put(f"l2f_b1_{e}", _col128(I["L2_ffn_b1"][e]))
        put(f"l2f_b2_{e}", _col128(I["L2_ffn_b2"][e]))
        put(f"l2f_bg_{e}", _col128(I["L2_ffn_bg"][e]))
        put(f"l2f_b3_{e}", _col128(I["L2_ffn_b3"][e]))
    return np.ascontiguousarray(V)


def stage_toeplitz(filt):
    f = np.asarray(filt, np.float32)
    idx = np.arange(T)[:, None] - np.arange(T)[None, :]
    Tm = np.where((idx >= 0) & (idx < f.size), f[np.clip(idx, 0, f.size - 1)], 0.0)
    return np.ascontiguousarray(Tm.astype(np.float32))


def stage_inputs(I, core):
    m = {}
    x = np.asarray(I["L0_attn_x"], np.float32)[2 * core:2 * core + 2]
    m["x_fm"] = np.ascontiguousarray(x.transpose(0, 2, 1))
    m["vecs"] = stage_vecs(I)
    m["wa_t"] = np.ascontiguousarray(np.asarray(I["L0_attn_Wa"], np.float32).T)
    m["wx_t"] = np.ascontiguousarray(np.asarray(I["L0_attn_Wx"], np.float32).T)
    m["l0f_w1"] = np.ascontiguousarray(np.asarray(I["L0_ffn_w1"], np.float32)[0])
    m["l0f_w2"] = np.ascontiguousarray(np.asarray(I["L0_ffn_w2"], np.float32)[0])
    m["l1a_wq"] = np.asarray(I["L1_attn_Wq"], np.float32)
    m["l1a_wkv"] = np.asarray(I["L1_attn_Wkv"], np.float32)
    m["l1a_wout"] = np.asarray(I["L1_attn_Wout"], np.float32)
    m["l1f_wqkv"] = np.asarray(I["L1_ffn_wqkv"], np.float32)
    m["l1f_wo"] = np.asarray(I["L1_ffn_wo"], np.float32)
    m["l1f_w1"] = np.asarray(I["L1_ffn_w1"], np.float32)
    m["l1f_w2"] = np.asarray(I["L1_ffn_w2"], np.float32)
    m["l2a_wq"] = np.asarray(I["L2_attn_Wq"], np.float32)
    m["l2a_wk"] = np.asarray(I["L2_attn_Wk"], np.float32)
    m["l2a_wv"] = np.asarray(I["L2_attn_Wv"], np.float32)
    m["l2a_wo"] = np.asarray(I["L2_attn_Wo"], np.float32)
    m["l2a_wck"] = np.asarray(I["L2_attn_Wck"], np.float32)
    m["l2a_wcv"] = np.asarray(I["L2_attn_Wcv"], np.float32)
    m["l2a_wg"] = np.ascontiguousarray(
        np.pad(np.asarray(I["L2_attn_Wg"], np.float32), ((0, 0), (0, 1))))
    m["l2f_gw"] = np.asarray(I["L2_ffn_gate_w"], np.float32)
    m["l2f_w1"] = np.asarray(I["L2_ffn_w1"], np.float32)
    m["l2f_w2"] = np.asarray(I["L2_ffn_w2"], np.float32)
    m["l2f_wg"] = np.asarray(I["L2_ffn_wg"], np.float32)
    m["l2f_w3"] = np.asarray(I["L2_ffn_w3"], np.float32)
    m["l3_toep"] = stage_toeplitz(I["L3_attn_filt"])
    m["l3_wgate"] = np.asarray(I["L3_ffn_w_gate"], np.float32)
    m["l3_wup"] = np.asarray(I["L3_ffn_w_up"], np.float32)
    m["l3_wdown"] = np.asarray(I["L3_ffn_w_down"], np.float32)
    return m


def _rot_tables():
    DH, WIN = 64, 512
    half = np.arange(0, DH, 2, dtype=np.float64)
    inv_freq = 1.0 / 10000.0 ** (half / DH)
    t = np.arange(T, dtype=np.float64)
    freqs = t[:, None] * inv_freq[None, :]
    freqs = np.concatenate([freqs, freqs], axis=1)
    base = (half + 0.4 * DH) / (1.4 * DH)
    power = (t - T // 2) / float(WIN // 2)
    scale = base[None, :] ** power[:, None]
    scale = np.concatenate([scale, scale], axis=1)
    cos, sin = np.cos(freqs), np.sin(freqs)
    cs_q = (cos * scale).T.astype(np.float32)
    ss_q = (sin * scale).T.astype(np.float32)
    cs_k = (cos / scale).T.astype(np.float32)
    ss_k = (sin / scale).T.astype(np.float32)
    tile2 = lambda a: np.ascontiguousarray(np.tile(a, (2, 1)))
    return tile2(cs_q), tile2(ss_q), tile2(cs_k), tile2(ss_k)


def _rot_perm():
    R = np.zeros((128, 128), np.float32)
    for h in range(2):
        o = 64 * h
        for i in range(32):
            R[o + 32 + i, o + i] = -1.0
            R[o + i, o + 32 + i] = 1.0
    return np.ascontiguousarray(R)


def _l1_masksT():
    j = np.arange(1024)[:, None]
    i = np.arange(512)[None, :]
    ok = (j >= i) & (j <= i + 512)
    m1 = np.where(ok, 0.0, -1e38).astype(np.float32)
    m0 = np.where(ok & (j >= 512), 0.0, -1e38).astype(np.float32)
    return m0, m1


def _l2_E():
    E = np.zeros((127, T), np.float32)
    for n in range(127):
        for c in range(64):
            p = min(16 * n + c, T - 1)
            E[n, p] += 1.0
    return E


class Prog:
    def __init__(self, nc, S, n_blocks=4):
        self.nc = nc
        self.n_blocks = n_blocks
        self.B = MK(nc, S)
        self.cols, NV = _vec_cols()
        d = nc.dram_tensor
        self.inp = {}
        for name, shape in [
            ("x_fm", [B, D, T]), ("vecs", [128, NV]),
            ("wa_t", [D, D]), ("wx_t", [D, D]),
            ("l0f_w1", [D, 4096]), ("l0f_w2", [4096, D]),
            ("l1a_wq", [D, D]), ("l1a_wkv", [D, D]), ("l1a_wout", [D, D]),
            ("l1f_wqkv", [2, D, 1536]), ("l1f_wo", [2, 512, D]),
            ("l1f_w1", [2, D, 5460]), ("l1f_w2", [2, 2730, D]),
            ("l2a_wq", [D, D]), ("l2a_wk", [D, D]), ("l2a_wv", [D, D]),
            ("l2a_wo", [D, D]), ("l2a_wck", [2048, 64]), ("l2a_wcv", [2048, 64]),
            ("l2a_wg", [64, 4]),
            ("l2f_gw", [D, 8]), ("l2f_w1", [8, D, 2048]), ("l2f_w2", [8, D, 2048]),
            ("l2f_wg", [8, 2048, 2048]), ("l2f_w3", [8, 2048, D]),
            ("l3_toep", [T, T]), ("l3_wgate", [2816, D]), ("l3_wup", [2816, D]),
            ("l3_wdown", [D, 2816]),
        ]:
            self.inp[name] = d(name, shape, F32, kind="ExternalInput").ap()
        self.xout = d("xout", [B, D, T], F32, kind="ExternalOutput").ap()
        self.X = d("Xs", [B, DT, 128, T], F32, kind="Internal").ap()
        self.XNB = d("XNBs", [B, DT, 128, T], BF16, kind="Internal").ap()
        self.XN32 = d("XN32s", [B, DT, 128, T], F32, kind="Internal").ap()
        self.W3Q = d("W3Q", [3, 2816 * D], BF16, kind="Internal").ap()
        self.ZS = d("ZSs", [B, DT, 128, T], F32, kind="Internal").ap()
        it = nc.inline_tensor
        self.c_identb = it(np.eye(128, dtype=BF), "c_identb").ap()
        self.c_identf = it(np.eye(128, dtype=np.float32), "c_identf").ap()
        self.c_ones_col_b = it(np.ones((128, 1), dtype=BF), "c_ones_col_b").ap()
        self.c_ones_row_f = it(np.ones((1, 128), dtype=np.float32), "c_ones_row_f").ap()
        self.c_ones_row_b = it(np.ones((1, 128), dtype=BF), "c_ones_row_b").ap()
        self.c_rperm = it(_rot_perm().astype(BF), "c_rperm").ap()
        self._lazy = {}
        Bld = self.B
        self.vt = Bld.carve(128, NV, F32)
        self.lsg8 = Bld.carve(128, 8, F32)
        self.identb = Bld.carve(128, 128, BF16)
        self.identf = Bld.carve(128, 128, F32)
        self.ones_col_b = Bld.carve(128, 1, BF16)
        self.ones_row_f = Bld.carve(1, 128, F32)
        self.ones_row_b = Bld.carve(1, 128, BF16)
        self.rperm = Bld.carve(128, 128, BF16)
        self.cst = Bld.carve(128, 32, F32)
        self.g64b = Bld.carve(128, 2, BF16)
        self.g64rep = Bld.carve(2, 128, F32)
        self.cmap = {}

    def kf(self, v):
        v = float(v)
        if v not in self.cmap:
            col = len(self.cmap)
            assert col < 32
            self.cmap[v] = col
            ap = self.cst.ap[:, col:col + 1]
            self.nc.const_aps.register(F32, v, ap)
            reg = self.cst.reg(0, 128, col, col + 1)
            self.B.t("dve", lambda e, ap=ap, v=v: e.memset(ap, v), w=[reg])
        col = self.cmap[v]
        return self.cst.reg(0, 128, col, col + 1)

    def lazy_const(self, name, fn):
        if name not in self._lazy:
            self._lazy[name] = self.nc.inline_tensor(fn(), name).ap()
        return self._lazy[name]

    def dbg(self, name, tile, P_=128, N=None, dt=F32):
        if isinstance(tile, Tile):
            ap, reg = tile.full
            P_, N, dt = tile.P, tile.N, tile.dtype
        else:
            ap, reg = tile
        o = self.nc.dram_tensor("dbg_" + name, [P_, N], dt, kind="ExternalOutput").ap()
        self.B.dma("sp", (o, dreg("dbg_" + name)), (ap, reg))

    def vcol(self, name, j0=0, n=1, P=128):
        o, nc_ = self.cols[name]
        assert j0 + n <= nc_, (name, j0, n)
        return self.vt[0:P, o + j0:o + j0 + n]

    @staticmethod
    def _tl(x):
        return x.full if isinstance(x, Tile) else x

    def tt(self, op, out, a, b, eng="dve"):
        out, a, b = self._tl(out), self._tl(a), self._tl(b)
        self.B.t(eng, lambda e: e.tensor_tensor(out[0], a[0], b[0], op=op),
                 r=[a, b], w=[out])

    def ts(self, out, a, s1, s2=None, op0=ALU.mult, op1=ALU.bypass, eng="dve"):
        out, a = self._tl(out), self._tl(a)
        s1, s2 = self._tl(s1), self._tl(s2)
        rr = [a]
        rss = []
        s1v = s1[0] if isinstance(s1, tuple) else s1
        s2v = s2[0] if isinstance(s2, tuple) else s2
        if isinstance(s1, tuple):
            rss.append(s1)
        if isinstance(s2, tuple):
            rss.append(s2)
        if s2 is None:
            self.B.t(eng, lambda e: e.tensor_single_scalar(out[0], a[0], s1v, op=op0),
                     r=rr, w=[out], rs=rss)
        else:
            self.B.t(eng, lambda e: e.tensor_scalar(out[0], a[0], s1v, s2v,
                                                    op0=op0, op1=op1),
                     r=rr, w=[out], rs=rss)

    def stt(self, out, in0, scalar, in1, op0, op1, eng="dve"):
        out, in0, in1, scalar = self._tl(out), self._tl(in0), self._tl(in1), self._tl(scalar)
        rr = [in0, in1]
        rss = []
        sv = scalar[0] if isinstance(scalar, tuple) else scalar
        if isinstance(scalar, tuple):
            rss.append(scalar)
        self.B.t(eng, lambda e: e.scalar_tensor_tensor(out[0], in0[0], sv, in1[0],
                                                       op0=op0, op1=op1),
                 r=rr, w=[out], rs=rss)

    def copy(self, out, a, eng="dve"):
        out, a = self._tl(out), self._tl(a)
        self.B.t(eng, lambda e: e.tensor_copy(out[0], a[0]), r=[a], w=[out])

    def act(self, out, a, func, bias=0.0, scale=1.0, accum=None, eng="act"):
        out, a, accum = self._tl(out), self._tl(a), self._tl(accum)
        bias, scale = self._tl(bias), self._tl(scale)
        rr = [a]
        rss = []
        bv = bias[0] if isinstance(bias, tuple) else bias
        sv = scale[0] if isinstance(scale, tuple) else scale
        if isinstance(bias, tuple):
            rss.append(bias)
        elif isinstance(bias, float) and bias != 0.0:
            rss.append(self.kf(bias))
        if isinstance(scale, tuple):
            rss.append(scale)
        ww = [out]
        kw = {}
        if accum is not None:
            kw["accum_out"] = accum[0]
            ww = [out, accum]
        self.B.t(eng, lambda e: e.activation(out[0], a[0], func, bias=bv, scale=sv, **kw),
                 r=rr, w=ww, rs=rss)

    def red(self, out, a, op=ALU.max, absval=False, eng="dve"):
        out, a = self._tl(out), self._tl(a)
        self.B.t(eng, lambda e: e.tensor_reduce(out[0], a[0], axis=AX.X, op=op,
                                                apply_absolute_value=absval),
                 r=[a], w=[out])

    def memset(self, out, v, eng="dve"):
        out = self._tl(out)
        self.B.t(eng, lambda e: e.memset(out[0], v), w=[out])

    def mm(self, ps, pairs, tag="", start=True, stop=True):
        rr = []
        for lt, rt in pairs:
            rr += [lt, rt]

        def emit(e):
            n = len(pairs)
            for k, (lt, rt) in enumerate(pairs):
                ins = e.matmul(ps[0], lt[0], rt[0], start=(start and k == 0),
                               stop=(stop and k == n - 1))
            return ins
        self.B.t("pe", emit, r=rr, w=[ps], tag=tag)

    def transpose(self, ps_out, a, bf16=True, kp=128, base=0):
        idt = self.identb if bf16 else self.identf
        ident = idt[base:base + kp, base:base + kp]
        self.B.t("pe", lambda e: e.transpose(ps_out[0], a[0], ident[0]),
                 r=[a, ident], w=[ps_out])

    def dsl(self, dr, name, b, dt, c0=0, c1=T, esz=4):
        slab = (b * DT + dt)
        ap = dr[b, dt, :, c0:c1]
        return ap, Reg("D:" + name, 0, 128, slab * 128 * T * esz, (slab + 1) * 128 * T * esz)

    def load_consts(self):
        Bd = self.B
        Bd.dma("sp", self.vt, (self.inp["vecs"], dreg("vecs")))
        Bd.dma("sp", self.identb, (self.c_identb, dreg("c_identb")))
        Bd.dma("sp", self.identf, (self.c_identf, dreg("c_identf")))
        Bd.dma("sp", self.ones_col_b, (self.c_ones_col_b, dreg("c_ones_col_b")))
        Bd.dma("sp", self.ones_row_f, (self.c_ones_row_f, dreg("c_ones_row_f")))
        Bd.dma("sp", self.ones_row_b, (self.c_ones_row_b, dreg("c_ones_row_b")))
        Bd.dma("sp", self.rperm, (self.c_rperm, dreg("c_rperm")))
        g64np = np.zeros((128, 2), np.float32)
        g64np[0:64, 0] = 1.0
        g64np[64:128, 1] = 1.0
        Bd.dma("sp", self.g64b, (self.nc.inline_tensor(g64np.astype(BF), "c_g64b").ap(),
                                 dreg("c_g64b")))
        Bd.dma("sp", self.g64rep, (self.nc.inline_tensor(
            np.ascontiguousarray(g64np.T), "c_g64rep").ap(), dreg("c_g64rep")))
        for b in range(B):
            for dt in range(DT):
                Bd.dma("sp", self.dsl(self.X, "Xs", b, dt),
                       (self.inp["x_fm"][b, 128 * dt:128 * (dt + 1), :], dreg("x_fm")))
        t1 = Bd.carve(128, 8, F32)
        self.act(t1.full, self.vcol("l0_lmbd", 0, 8), AF.Sigmoid)
        self.act(t1.full, t1.full, AF.Ln)
        self.ts(self.lsg8.full, t1.full, 8.0, op0=ALU.mult)

    def ln_entry(self, b):
        P, Bd = self, self.B
        Bd.ph_reset()
        xt = [Bd.ph(128, T, F32) for _ in range(DT)]
        xnb = [Bd.ph(128, T, BF16) for _ in range(DT)]
        sqh = [Bd.ph(128, 1024, BF16) for _ in range(DT)]
        mean = Bd.ph(1, T, F32)
        e2 = Bd.ph(1, T, F32)
        tmp = Bd.ph(1, T, F32)
        inv = Bd.ph(1, T, F32)
        msr = Bd.ph(1, T, F32)
        t1 = Bd.ph(128, 512, F32)
        for dt in range(DT):
            Bd.dma("sp", xt[dt], self.dsl(self.X, "Xs", b, dt))
            self.copy(xnb[dt].full, xt[dt].full)
        onesb = self.ones_col_b.full
        for h in range(2):
            hc = 1024 * h
            for dt in range(DT):
                self.tt(ALU.mult, sqh[dt].full, xnb[dt][:, hc:hc + 1024],
                        xnb[dt][:, hc:hc + 1024])
            for c in range(2):
                c0, cg = hc + 512 * c, 512 * c
                ps_s, ps_q = Bd.psum(c % 2), Bd.psum(2 + c % 2)
                P.mm((ps_s.ap[0:1, :], ps_s.reg()),
                     [(onesb, xnb[dt][:, c0:c0 + 512]) for dt in range(DT)])
                P.mm((ps_q.ap[0:1, :], ps_q.reg()),
                     [(onesb, sqh[dt][:, cg:cg + 512]) for dt in range(DT)])
                self.ts((mean.ap[:, c0:c0 + 512], mean.reg(0, 1, c0, c0 + 512)),
                        (ps_s.ap[0:1, :], ps_s.reg()), 1.0 / D, op0=ALU.mult)
                self.ts((e2.ap[:, c0:c0 + 512], e2.reg(0, 1, c0, c0 + 512)),
                        (ps_q.ap[0:1, :], ps_q.reg()), 1.0 / D, op0=ALU.mult)
        self.tt(ALU.mult, tmp.full, mean.full, mean.full)
        self.tt(ALU.subtract, tmp.full, e2.full, tmp.full)
        self.act(inv.full, tmp.full, AF.Ln, bias=1e-5)
        self.act(inv.full, inv.full, AF.Exp, scale=-0.5)
        self.stt(msr.full, mean.full, -1.0, inv.full, op0=ALU.mult, op1=ALU.mult)
        for c in range(4):
            c0 = 512 * c
            psA_ = Bd.psum(4)
            psB_ = Bd.psum(5)
            P.mm(psA_.full, [((self.ones_row_f.ap, self.ones_row_f.reg()),
                              (inv.ap[:, c0:c0 + 512], inv.reg(0, 1, c0, c0 + 512)))])
            P.mm(psB_.full, [((self.ones_row_f.ap, self.ones_row_f.reg()),
                              (msr.ap[:, c0:c0 + 512], msr.reg(0, 1, c0, c0 + 512)))])
            for dt in range(DT):
                xc = xt[dt][:, c0:c0 + 512]
                self.tt(ALU.mult, t1.full, xc, psA_.full)
                self.tt(ALU.add, xc, t1.full, psB_.full)
                self.act(xnb[dt][:, c0:c0 + 512], xc, AF.Copy)
        for dt in range(DT):
            Bd.dma("sp", self.dsl(self.XN32, "XN32s", b, dt), xt[dt])
            Bd.dma("sp", self.dsl(self.XNB, "XNBs", b, dt, esz=2), xnb[dt])
        if DBG and b == 0:
            self.dbg("ln_mean", mean)
            self.dbg("ln_e2", e2)
            self.dbg("ln_inv", inv)
            self.dbg("ln_msr", msr)

    def l0_attn(self, b):
        P, Bd = self, self.B
        Bd.ph_reset()
        xnb = [Bd.ph(128, T, BF16) for _ in range(DT)]
        xn32 = Bd.ph(128, T, F32)
        wa = Bd.ph(128, 8192, BF16)
        wx = Bd.ph(128, 8192, BF16)
        rbuf = Bd.ph(128, T, F32)
        ibuf = Bd.ph(128, T, F32)
        abuf = Bd.ph(128, T, F32)
        bbuf = Bd.ph(128, T, F32)
        ytil = Bd.ph(128, T, F32)
        xres = Bd.ph(128, T, F32)
        for dt in range(DT):
            Bd.dma("sp", xnb[dt], self.dsl(self.XNB, "XNBs", b, dt, esz=2))
        for k in range(DT):
            Bd.dma("pool", wa[:, 1024 * k:1024 * (k + 1)],
                   (self.inp["wa_t"][128 * k:128 * (k + 1), :], dreg("wa_t")))
            Bd.dma("pool", wx[:, 1024 * k:1024 * (k + 1)],
                   (self.inp["wx_t"][128 * k:128 * (k + 1), :], dreg("wx_t")))
        for dt in range(DT):
            Bd.dma("sp", xn32, self.dsl(self.XN32, "XN32s", b, dt))
            Bd.dma("sp", xres, self.dsl(self.X, "Xs", b, dt))
            for c in range(4):
                c0 = 512 * c
                ps_r, ps_i = Bd.psum(c % 2), Bd.psum(2 + c % 2)
                P.mm(ps_r.full, [(wa[:, 1024 * k + 128 * dt:1024 * k + 128 * (dt + 1)],
                                  xnb[k][:, c0:c0 + 512]) for k in range(DT)])
                P.mm(ps_i.full, [(wx[:, 1024 * k + 128 * dt:1024 * k + 128 * (dt + 1)],
                                  xnb[k][:, c0:c0 + 512]) for k in range(DT)])
                self.act(rbuf[:, c0:c0 + 512], ps_r.full, AF.Sigmoid,
                         bias=self.vcol("l0_ba", dt))
                self.act(ibuf[:, c0:c0 + 512], ps_i.full, AF.Sigmoid,
                         bias=self.vcol("l0_bx", dt))
            self.act(abuf.full, rbuf.full, AF.Exp, scale=self.lsg8[:, dt:dt + 1])
            self.tt(ALU.mult, bbuf.full, abuf.full, abuf.full)
            self.act(bbuf.full, bbuf.full, AF.Ln, bias=1.0, scale=-1.0)
            self.act(bbuf.full, bbuf.full, AF.Exp, scale=0.5)
            self.tt(ALU.mult, ibuf.full, ibuf.full, (xn32.ap, xn32.reg()))
            self.tt(ALU.mult, bbuf.full, bbuf.full, ibuf.full)
            Bd.t("dve", lambda e, o=ytil, a_=abuf, b_=bbuf: e.tensor_tensor_scan(
                o.ap, a_.ap, b_.ap, 0.0, op0=ALU.mult, op1=ALU.add),
                r=[abuf.full, bbuf.full], w=[ytil.full])
            self.tt(ALU.add, xres.full, xres.full, ytil.full)
            Bd.dma("sp", self.dsl(self.X, "Xs", b, dt), xres)
        if DBG and b == 0:
            self.dbg("r7", rbuf)
            self.dbg("a7", abuf)
            self.dbg("b7", bbuf)
            self.dbg("y7", ytil)
            self.dbg("xattn", xres)

    def mm_proj(self, w_sb, src_tiles, out_write, nk=DT, nchunk=4, csz=512):
        pass

    def l1_attn(self, b):
        P, Bd = self, self.B
        Bd.ph_reset()
        qf = [Bd.ph(128, T, BF16) for _ in range(DT)]
        kf = [Bd.ph(128, T, BF16) for _ in range(DT)]
        vt = [Bd.ph(128, 16 * 65, BF16) for _ in range(16)]
        pbt = [Bd.ph(128, 512, BF16) for _ in range(2)]
        mk = Bd.mark()
        xnb = [Bd.ph(128, T, BF16) for _ in range(DT)]
        wq = Bd.ph(128, 8192, BF16)
        wkv = Bd.ph(128, 8192, BF16)
        for dt in range(DT):
            Bd.dma("sp", xnb[dt], self.dsl(self.XNB, "XNBs", b, dt, esz=2))
        for k in range(DT):
            Bd.dma("pool", wq[:, 1024 * k:1024 * (k + 1)],
                   (self.inp["l1a_wq"][128 * k:128 * (k + 1), :], dreg("l1a_wq")))
            Bd.dma("pool", wkv[:, 1024 * k:1024 * (k + 1)],
                   (self.inp["l1a_wkv"][128 * k:128 * (k + 1), :], dreg("l1a_wkv")))
        for dst, w_sb in ((qf, wq), (kf, wkv)):
            for dt in range(DT):
                for c in range(4):
                    c0 = 512 * c
                    ps = Bd.psum(c % 2 + (0 if dst is qf else 2))
                    P.mm(ps.full, [(w_sb[:, 1024 * k + 128 * dt:1024 * k + 128 * (dt + 1)],
                                    xnb[k][:, c0:c0 + 512]) for k in range(DT)])
                    self.act(dst[dt][:, c0:c0 + 512], ps.full, AF.Copy)
        pst = Bd.psumT()
        for h in range(16):
            t2, o = h // 2, 64 * (h % 2)
            for pt in range(16):
                self.transpose((pst.ap[0:128, 0:64], pst.reg(0, 128, 0, 64)),
                               (kf[t2].ap[o:o + 64, 128 * pt:128 * (pt + 1)],
                                kf[t2].reg(o, o + 64, 128 * pt, 128 * (pt + 1))),
                               bf16=True, kp=64, base=o)
                self.copy((vt[pt].ap[:, 65 * h:65 * h + 64],
                           vt[pt].reg(0, 128, 65 * h, 65 * h + 64)),
                          (pst.ap[0:128, 0:64], pst.reg(0, 128, 0, 64)))
        for pt in range(16):
            self.memset((vt[pt].ap[:, 64:16 * 65:65], vt[pt].reg(0, 128, 0, 16 * 65)), 1.0)
        Bd.reset_to(mk)
        ofm = [Bd.ph(128, T, BF16) for _ in range(DT)]
        wo = Bd.ph(128, 8192, BF16)
        den = Bd.ph(1, 512, F32)
        repd = Bd.ph(64, 512, F32)
        xres = Bd.ph(128, T, F32)
        for h in range(16):
            t2, o = h // 2, 64 * (h % 2)
            for ic in range(4):
                i0 = 512 * ic
                ops = Bd.psum(4 + ic % 2)
                for pt in range(16):
                    ps = Bd.psum(pt % 2)
                    P.mm(ps.full, [((kf[t2].ap[o:o + 64, 128 * pt:128 * (pt + 1)],
                                     kf[t2].reg(o, o + 64, 128 * pt, 128 * (pt + 1))),
                                    (qf[t2].ap[o:o + 64, i0:i0 + 512],
                                     qf[t2].reg(o, o + 64, i0, i0 + 512)))])
                    pb = pbt[pt % 2]
                    if pt % 2 == 0:
                        self.act(pb.full, ps.full, AF.Relu, scale=0.125)
                    else:
                        self.ts(pb.full, ps.full, 0.125, 0.0, op0=ALU.mult,
                                op1=ALU.max)
                    P.mm((ops.ap[0:65, :], ops.reg()),
                         [((vt[pt].ap[:, 65 * h:65 * (h + 1)],
                            vt[pt].reg(0, 128, 65 * h, 65 * (h + 1))), pb.full)],
                         start=(pt == 0), stop=(pt == 15))
                self.act((ofm[t2].ap[o:o + 64, i0:i0 + 512],
                          ofm[t2].reg(o, o + 64, i0, i0 + 512)),
                         (ops.ap[0:64, :], ops.reg()), AF.Copy)
        for k in range(DT):
            Bd.dma("pool", wo[:, 1024 * k:1024 * (k + 1)],
                   (self.inp["l1a_wout"][128 * k:128 * (k + 1), :], dreg("l1a_wout")))
        for dt in range(DT):
            Bd.dma("sp", xres, self.dsl(self.X, "Xs", b, dt))
            for c in range(4):
                c0 = 512 * c
                ps = Bd.psum(6)
                P.mm(ps.full, [(wo[:, 1024 * k + 128 * dt:1024 * k + 128 * (dt + 1)],
                                ofm[k][:, c0:c0 + 512]) for k in range(DT)])
                self.stt(xres[:, c0:c0 + 512], ps.full, self.vcol("l1a_bout", dt),
                         xres[:, c0:c0 + 512], op0=ALU.add, op1=ALU.add)
            Bd.dma("sp", self.dsl(self.X, "Xs", b, dt), xres)

    def l1_ffn(self, b):
        P, Bd = self, self.B
        HID = 2730
        rotc = [self.lazy_const("c_csq", lambda: _rot_tables()[0]),
                self.lazy_const("c_ssq", lambda: _rot_tables()[1]),
                self.lazy_const("c_csk", lambda: _rot_tables()[2]),
                self.lazy_const("c_ssk", lambda: _rot_tables()[3])]
        cm0 = self.lazy_const("c_m01_0", lambda: np.ascontiguousarray(
            (_l1_masksT()[0] == 0).astype(BF)))
        cm1 = self.lazy_const("c_m01_1", lambda: np.ascontiguousarray(
            (_l1_masksT()[1] == 0).astype(BF)))
        for l in range(2):
            Bd.ph_reset()
            tabs = [Bd.ph(128, T, F32) for _ in range(4)]
            qf = [Bd.ph(128, T, BF16) for _ in range(4)]
            kpb = [Bd.ph(128, 2560, BF16) for _ in range(4)]
            vt = [Bd.ph(128, 8 * 65, BF16) for _ in range(16)]
            mka = Bd.mark()
            hsh = [Bd.ph(128, T, BF16) for _ in range(DT)]
            wq3 = Bd.ph(128, DT * 1536, BF16)
            zld = Bd.ph(128, T, F32)
            sqb = Bd.ph(128, T, BF16)
            inv2 = Bd.ph(2, 512, F32)
            t1k = Bd.ph(128, 2048, F32)
            for i, cap in enumerate(rotc):
                Bd.dma("sp", tabs[i], (cap, dreg(f"c_rt{i}")))
                sc = [f"l1f_qs{l}", f"l1f_qss{l}", f"l1f_ks{l}", f"l1f_kss{l}"][i]
                self.ts(tabs[i].full, tabs[i].full, self.vcol(sc), op0=ALU.mult)
            zsrc, zname = (self.XN32, "XN32s") if l == 0 else (self.ZS, "ZSs")
            for dt in range(DT):
                Bd.dma("sp", zld, self.dsl(zsrc, zname, b, dt))
                if dt < 4:
                    self.copy(hsh[dt].full, zld.full)
                else:
                    self.memset(hsh[dt][:, 0:1], 0.0)
                    self.copy(hsh[dt][:, 1:T], zld[:, 0:T - 1])
            for k in range(DT):
                Bd.dma("pool", wq3[:, 1536 * k:1536 * (k + 1)],
                       (self.inp["l1f_wqkv"][l, 128 * k:128 * (k + 1), :],
                        dreg("l1f_wqkv")))
            for ft in range(8):
                dst = qf[ft] if ft < 4 else None
                for c in range(4):
                    c0 = 512 * c
                    ps = Bd.psum(c % 2)
                    P.mm(ps.full, [(wq3[:, 1536 * k + 128 * ft:1536 * k + 128 * (ft + 1)],
                                    hsh[k][:, c0:c0 + 512]) for k in range(DT)])
                    if ft < 4:
                        self.act(qf[ft][:, c0:c0 + 512], ps.full, AF.Copy)
                    else:
                        self.act(kpb[ft - 4][:, 512 + c0:512 + c0 + 512], ps.full, AF.Copy)
            for mt in range(16):
                ps = Bd.psum(2 + mt % 2)
                P.mm(ps.full, [((hsh[k].ap[:, 128 * mt:128 * (mt + 1)],
                                 hsh[k].reg(0, 128, 128 * mt, 128 * (mt + 1))),
                                (wq3.ap[:, 1536 * k + 1024:1536 * (k + 1)],
                                 wq3.reg(0, 128, 1536 * k + 1024, 1536 * (k + 1))))
                               for k in range(DT)])
                self.B.t("dve", lambda e, mt=mt, ps=ps: e.tensor_copy(
                    vt[mt].ap.rearrange("p (h c) -> p h c", c=65)[:, :, 0:64],
                    ps.ap.rearrange("p (h c) -> p h c", c=64)),
                    r=[ps.full], w=[vt[mt].full])
                self.memset((vt[mt].ap[:, 64:8 * 65:65], vt[mt].reg()), 1.0)
            for which in range(2):
                ctab, stab = (tabs[0], tabs[1]) if which == 0 else (tabs[2], tabs[3])
                for t2 in range(4):
                    if which == 0:
                        src = qf[t2]
                        off = 0
                    else:
                        src = kpb[t2]
                        off = 512
                    sl = lambda c0, n=512: (src.ap[:, off + c0:off + c0 + n],
                                            src.reg(0, 128, off + c0, off + c0 + n))
                    self.tt(ALU.mult, sqb[:, 0:T], sl(0, T), sl(0, T))
                    for c in range(4):
                        c0 = 512 * c
                        psn = Bd.psum(4)
                        P.mm((psn.ap[0:2, :], psn.reg()),
                             [(self.g64b.full,
                               (sqb.ap[:, c0:c0 + 512], sqb.reg(0, 128, c0, c0 + 512)))])
                        self.ts(inv2.full, (psn.ap[0:2, :], psn.reg()), 1e-24, op0=ALU.max)
                        self.act(inv2.full, inv2.full, AF.Ln)
                        self.act(inv2.full, inv2.full, AF.Exp, scale=-0.5)
                        psr = Bd.psum(5)
                        P.mm(psr.full, [(self.g64rep.full, inv2.full)])
                        self.tt(ALU.mult, sl(c0), sl(c0), psr.full)
                    for hh in range(2):
                        h0 = 1024 * hh
                        pst = Bd.psum(5, 2)
                        P.mm((pst.ap[:, 0:512], pst.reg(0, 128, 0, 512)),
                             [((self.rperm.ap, self.rperm.reg()), sl(h0, 512))])
                        P.mm((pst.ap[:, 512:1024], pst.reg(0, 128, 512, 1024)),
                             [((self.rperm.ap, self.rperm.reg()), sl(h0 + 512, 512))])
                        t1 = t1k
                        self.tt(ALU.mult, (t1.ap[:, 0:1024], t1.reg(0, 128, 0, 1024)),
                                sl(h0, 1024),
                                (ctab.ap[:, h0:h0 + 1024], ctab.reg(0, 128, h0, h0 + 1024)))
                        self.tt(ALU.mult, (t1.ap[:, 1024:2048], t1.reg(0, 128, 1024, 2048)),
                                (pst.ap[:, 0:1024], pst.reg(0, 128, 0, 1024)),
                                (stab.ap[:, h0:h0 + 1024], stab.reg(0, 128, h0, h0 + 1024)))
                        self.tt(ALU.add, sl(h0, 1024),
                                (t1.ap[:, 0:1024], t1.reg(0, 128, 0, 1024)),
                                (t1.ap[:, 1024:2048], t1.reg(0, 128, 1024, 2048)))
            for t2 in range(4):
                self.memset(kpb[t2][:, 0:512], 0.0)
            Bd.reset_to(mka)
            m01 = [Bd.ph(128, 8 * 512, BF16) for _ in range(2)]
            for v_, cap in ((0, cm0), (1, cm1)):
                Bd.dma("sp", (m01[v_].ap.rearrange("p (j n) -> p j n", j=8),
                              m01[v_].reg()),
                       (cap.rearrange("(j p) n -> p j n", p=128),
                        dreg("c_m01_" + str(v_))))
            pbt = [Bd.ph(128, 512, BF16) for _ in range(2)]
            exv = [Bd.ph(128, 512, BF16) for _ in range(2)]
            ofm = [Bd.ph(128, T, BF16) for _ in range(4)]
            wo4 = Bd.ph(128, 4096, BF16)
            denr = Bd.ph(1, 512, F32)
            reps = Bd.ph(64, 512, F32)
            zres = Bd.ph(128, T, F32)
            for k in range(4):
                Bd.dma("pool", wo4[:, 1024 * k:1024 * (k + 1)],
                       (self.inp["l1f_wo"][l, 128 * k:128 * (k + 1), :], dreg("l1f_wo")))
            for h in range(8):
                t2, o = h // 2, 64 * (h % 2)
                for w in range(4):
                    i0 = 512 * w
                    ops = Bd.psum(2 + w % 2)
                    jts = range(4, 8) if w == 0 else range(8)
                    first = True
                    for jt in jts:
                        jc = 512 * w + 128 * jt
                        ps = Bd.psum(jt % 2)
                        P.mm(ps.full, [((kpb[t2].ap[o:o + 64, jc:jc + 128],
                                         kpb[t2].reg(o, o + 64, jc, jc + 128)),
                                        (qf[t2].ap[o:o + 64, i0:i0 + 512],
                                         qf[t2].reg(o, o + 64, i0, i0 + 512)))])
                        ex = exv[jt % 2]
                        pb = pbt[jt % 2]
                        self.act(ex.full, ps.full, AF.Exp, scale=8.0)
                        mv = m01[0 if w == 0 else 1]
                        self.tt(ALU.mult, pb.full, ex.full,
                                (mv.ap[:, 512 * jt:512 * (jt + 1)],
                                 mv.reg(0, 128, 512 * jt, 512 * (jt + 1))), eng="pool")
                        mt = (512 * (w - 1) + 128 * jt) // 128
                        P.mm((ops.ap[0:65, :], ops.reg()),
                             [((vt[mt].ap[:, 65 * h:65 * (h + 1)],
                                vt[mt].reg(0, 128, 65 * h, 65 * (h + 1))), pb.full)],
                             start=first, stop=(jt == max(jts)))
                        first = False
                    self.B.t("dve", lambda e, ops=ops: e.reciprocal(
                        denr.ap, ops.ap[64:65, :]), r=[ops.full], w=[denr.full])
                    psr = Bd.psum(4)
                    P.mm((psr.ap[0:64, :], psr.reg()),
                         [((self.ones_row_f.ap[0:1, 0:64],
                            self.ones_row_f.reg(0, 1, 0, 64)), denr.full)])
                    self.copy(reps.full, (psr.ap[0:64, :], psr.reg()))
                    self.tt(ALU.mult, (ofm[t2].ap[o:o + 64, i0:i0 + 512],
                                       ofm[t2].reg(o, o + 64, i0, i0 + 512)),
                            (ops.ap[0:64, :], ops.reg()), reps.full)
            for dt in range(DT):
                Bd.dma("sp", zres, self.dsl(zsrc, zname, b, dt))
                for c in range(4):
                    c0 = 512 * c
                    ps = Bd.psum(5 + c % 2)
                    P.mm(ps.full, [(wo4[:, 1024 * k + 128 * dt:1024 * k + 128 * (dt + 1)],
                                    ofm[k][:, c0:c0 + 512]) for k in range(4)])
                    self.tt(ALU.add, zres[:, c0:c0 + 512], zres[:, c0:c0 + 512], ps.full)
                Bd.dma("sp", self.dsl(self.ZS, "ZSs", b, dt), zres)
            Bd.ph_reset()
            h2b = [Bd.ph(128, T, BF16) for _ in range(DT)]
            hnb = [Bd.ph(128, T, BF16) for _ in range(DT)]
            zld2 = Bd.ph(128, T, F32)
            sq2 = Bd.ph(128, 512, BF16)
            mean = Bd.ph(1, T, F32)
            e2 = Bd.ph(1, T, F32)
            inv = Bd.ph(1, T, F32)
            msr = Bd.ph(1, T, F32)
            t1c = Bd.ph(128, 512, F32)
            for dt in range(DT):
                Bd.dma("sp", zld2, self.dsl(self.ZS, "ZSs", b, dt))
                if dt < 4:
                    self.copy(h2b[dt].full, zld2.full)
                else:
                    self.memset(h2b[dt][:, 0:1], 0.0)
                    self.copy(h2b[dt][:, 1:T], zld2[:, 0:T - 1])
            onesb = self.ones_col_b.full
            for c in range(4):
                c0 = 512 * c
                ps_s, ps_q = Bd.psum(c % 2), Bd.psum(2 + c % 2)
                P.mm((ps_s.ap[0:1, :], ps_s.reg()),
                     [(onesb, h2b[dt][:, c0:c0 + 512]) for dt in range(DT)])
                for dt in range(DT):
                    self.tt(ALU.mult, sq2.full, h2b[dt][:, c0:c0 + 512],
                            h2b[dt][:, c0:c0 + 512])
                    P.mm((ps_q.ap[0:1, :], ps_q.reg()), [(onesb, sq2.full)],
                         start=(dt == 0), stop=(dt == DT - 1))
                self.ts((mean.ap[:, c0:c0 + 512], mean.reg(0, 1, c0, c0 + 512)),
                        (ps_s.ap[0:1, :], ps_s.reg()), 1.0 / D, op0=ALU.mult)
                self.ts((e2.ap[:, c0:c0 + 512], e2.reg(0, 1, c0, c0 + 512)),
                        (ps_q.ap[0:1, :], ps_q.reg()), 1.0 / D, op0=ALU.mult)
            self.tt(ALU.mult, inv.full, mean.full, mean.full)
            self.tt(ALU.subtract, inv.full, e2.full, inv.full)
            self.act(inv.full, inv.full, AF.Ln, bias=1e-5)
            self.act(inv.full, inv.full, AF.Exp, scale=-0.5)
            self.stt(msr.full, mean.full, -1.0, inv.full, op0=ALU.mult, op1=ALU.mult)
            for c in range(4):
                c0 = 512 * c
                psA_, psB_ = Bd.psum(4), Bd.psum(5)
                P.mm(psA_.full, [((self.ones_row_f.ap, self.ones_row_f.reg()),
                                  (inv.ap[:, c0:c0 + 512], inv.reg(0, 1, c0, c0 + 512)))])
                P.mm(psB_.full, [((self.ones_row_f.ap, self.ones_row_f.reg()),
                                  (msr.ap[:, c0:c0 + 512], msr.reg(0, 1, c0, c0 + 512)))])
                for dt in range(DT):
                    self.tt(ALU.mult, t1c.full, h2b[dt][:, c0:c0 + 512], psA_.full)
                    self.tt(ALU.add, t1c.full, t1c.full, psB_.full)
                    self.ts(hnb[dt][:, c0:c0 + 512], t1c.full,
                            self.vcol(f"l1f_lng{l}", dt), self.vcol(f"l1f_lnb{l}", dt),
                            op0=ALU.mult, op1=ALU.add)
            mkc = Bd.mark()
            w1b = [Bd.ph(128, 1024, BF16) for _ in range(4)]
            w2s = Bd.ph(128, 22 * 1024, BF16)
            hgg = [Bd.ph(128, 512, BF16) for _ in range(22)]
            gg = Bd.ph(128, 512, F32)
            widths = [128] * 21 + [42]
            for fi in range(22):
                Bd.dma("pool", (w2s.ap[0:widths[fi], 1024 * fi:1024 * (fi + 1)],
                                w2s.reg(0, 128, 1024 * fi, 1024 * (fi + 1))),
                       (self.inp["l1f_w2"][l, 128 * fi:128 * fi + widths[fi], :],
                        dreg("l1f_w2")))
            for c in range(4):
                c0 = 512 * c
                for fi in range(22):
                    wdt = widths[fi]
                    wa_ = w1b[2 * (fi % 2)]
                    wg_ = w1b[2 * (fi % 2) + 1]
                    Bd.dma("pool", (wa_.ap[:, 0:8 * wdt], wa_.reg(0, 128, 0, 8 * wdt)),
                           (self.inp["l1f_w1"].rearrange("l (kt p) m -> l p kt m", p=128)
                            [l, :, :, 128 * fi:128 * fi + wdt], dreg("l1f_w1")))
                    Bd.dma("pool", (wg_.ap[:, 0:8 * wdt], wg_.reg(0, 128, 0, 8 * wdt)),
                           (self.inp["l1f_w1"].rearrange("l (kt p) m -> l p kt m", p=128)
                            [l, :, :, HID + 128 * fi:HID + 128 * fi + wdt], dreg("l1f_w1")))
                    psa = Bd.psum(fi % 2)
                    psg = Bd.psum(2 + fi % 2)
                    P.mm((psa.ap[0:wdt, :], psa.reg()),
                         [((wa_.ap[:, wdt * k:wdt * (k + 1)],
                            wa_.reg(0, 128, wdt * k, wdt * (k + 1))),
                           hnb[k][:, c0:c0 + 512]) for k in range(DT)])
                    P.mm((psg.ap[0:wdt, :], psg.reg()),
                         [((wg_.ap[:, wdt * k:wdt * (k + 1)],
                            wg_.reg(0, 128, wdt * k, wdt * (k + 1))),
                           hnb[k][:, c0:c0 + 512]) for k in range(DT)])
                    self.act((gg.ap[0:wdt, :], gg.reg(0, wdt, 0, 512)),
                             (psg.ap[0:wdt, :], psg.reg()), AF.Gelu,
                             bias=self.vcol(f"l1f_b1g{l}", fi, P=wdt))
                    self.stt(hgg[fi][0:wdt, 0:512],
                             (psa.ap[0:wdt, :], psa.reg()),
                             self.vcol(f"l1f_b1a{l}", fi, P=wdt),
                             (gg.ap[0:wdt, :], gg.reg(0, wdt, 0, 512)),
                             op0=ALU.add, op1=ALU.mult)
                for dt in range(DT):
                    ps = Bd.psum(4 + dt % 2)
                    P.mm(ps.full, [((w2s.ap[0:widths[fi], 1024 * fi + 128 * dt:1024 * fi + 128 * (dt + 1)],
                                     w2s.reg(0, 128, 1024 * fi, 1024 * (fi + 1))),
                                    hgg[fi][0:widths[fi], 0:512]) for fi in range(22)])
                    Bd.dma("sp", (zld2.ap[:, c0:c0 + 512],
                                  zld2.reg(0, 128, c0, c0 + 512)),
                           self.dsl(self.ZS, "ZSs", b, dt, c0, c0 + 512))
                    self.stt(zld2[:, c0:c0 + 512], ps.full, self.vcol(f"l1f_b2_{l}", dt),
                             zld2[:, c0:c0 + 512], op0=ALU.add, op1=ALU.add)
                    Bd.dma("sp", self.dsl(self.ZS, "ZSs", b, dt, c0, c0 + 512),
                           (zld2.ap[:, c0:c0 + 512], zld2.reg(0, 128, c0, c0 + 512)))
        Bd.ph_reset()
        xr = Bd.ph(128, T, F32)
        zr = Bd.ph(128, T, F32)
        for dt in range(DT):
            Bd.dma("sp", xr, self.dsl(self.X, "Xs", b, dt))
            Bd.dma("sp", zr, self.dsl(self.ZS, "ZSs", b, dt))
            self.tt(ALU.add, xr.full, xr.full, zr.full)
            Bd.dma("sp", self.dsl(self.X, "Xs", b, dt), xr)

    def quant_fp4(self, dst, src, tmpa, tmpb, tmpr, tmps, N, P_=128,
                  eng1="dve", eng2="pool"):
        nb = N // 16
        bc = lambda t: t[0].unsqueeze(2).to_broadcast([P_, nb, 16])
        v3 = lambda t: t[0].rearrange("p (b s) -> p b s", s=16)
        self.B.t(eng1, lambda e: e.tensor_reduce(
            tmps[0], v3(src), axis=AX.X, op=ALU.max, apply_absolute_value=True),
            r=[src], w=[tmps])
        self.ts(tmps, tmps, 1.0 / 6.0, 1e-12, op0=ALU.mult, op1=ALU.max, eng=eng1)
        self.B.t("dve", lambda e: e.reciprocal(tmpr[0], tmps[0]), r=[tmps], w=[tmpr])
        self.act(tmpa, src, AF.Abs)
        self.B.t(eng1, lambda e: e.tensor_tensor(v3(tmpa), v3(tmpa), bc(tmpr),
                                                 op=ALU.mult), r=[tmpa, tmpr], w=[tmpa])
        self.ts(tmpb, tmpa, 0.25, None, op0=ALU.is_gt, eng=eng1)
        for k, m in enumerate((0.75, 1.25, 1.75, 2.5, 3.5, 5.0)):
            self.stt(tmpb, tmpa, m, tmpb, op0=ALU.is_gt, op1=ALU.add, eng=eng1)
        self.ts(tmpa, tmpb, -4.0, 0.0, op0=ALU.add, op1=ALU.max, eng=eng1)
        self.tt(ALU.add, tmpa, tmpa, tmpb, eng=eng2)
        self.ts(tmpb, tmpb, -6.0, 0.0, op0=ALU.add, op1=ALU.max, eng=eng1)
        self.stt(tmpb, tmpa, 0.5, tmpb, op0=ALU.mult, op1=ALU.add, eng=eng1)
        self.B.t(eng2, lambda e: e.tensor_tensor(v3(tmpb), v3(tmpb), bc(tmps),
                                                 op=ALU.mult), r=[tmpb, tmps], w=[tmpb])
        self.ts(tmpa, src, 0.0, 2.0, op0=ALU.is_ge, op1=ALU.mult, eng=eng1)
        self.stt(dst, tmpa, -1.0, tmpb, op0=ALU.add, op1=ALU.mult, eng=eng1)

    def l3_prep_weights(self):
        P, Bd = self, self.B
        Bd.ph_reset()
        src = Bd.ph(128, 2816, F32)
        tmpa = Bd.ph(128, 2816, F32)
        tmpb = Bd.ph(128, 2816, F32)
        tmpr = Bd.ph(128, 176, F32)
        tmps = Bd.ph(128, 176, F32)
        dstb = Bd.ph(128, 2816, BF16)
        ev = Bd.ph(128, 128, BF16)
        w3q = lambda wi, a: self.W3Q[wi, :].rearrange("(a p n) -> a p n", a=a, p=128)
        for wi, wname in ((0, "l3_wgate"), (1, "l3_wup")):
            for sl_ in range(22):
                r0 = 128 * sl_
                Bd.dma("sp", (src.ap[:, 0:1024], src.reg(0, 128, 0, 1024)),
                       (self.inp[wname][r0:r0 + 128, :], dreg(wname)))
                self.quant_fp4((dstb.ap[:, 0:1024], dstb.reg(0, 128, 0, 1024)),
                               (src.ap[:, 0:1024], src.reg(0, 128, 0, 1024)),
                               (tmpa.ap[:, 0:1024], tmpa.reg(0, 128, 0, 1024)),
                               (tmpb.ap[:, 0:1024], tmpb.reg(0, 128, 0, 1024)),
                               (tmpr.ap[:, 0:64], tmpr.reg(0, 128, 0, 64)),
                               (tmps.ap[:, 0:64], tmps.reg(0, 128, 0, 64)), 1024)
                for ds in range(8):
                    pst = Bd.psumT()
                    self.transpose((pst.ap[:, 0:128], pst.reg(0, 128, 0, 128)),
                                   (dstb.ap[:, 128 * ds:128 * (ds + 1)],
                                    dstb.reg(0, 128, 128 * ds, 128 * (ds + 1))))
                    self.copy(ev.full, (pst.ap[:, 0:128], pst.reg(0, 128, 0, 128)))
                    Bd.dma("sp", (w3q(wi, 8)[ds, :, r0:r0 + 128], dreg("W3Q")), ev)
        for sl_ in range(8):
            r0 = 128 * sl_
            Bd.dma("sp", src, (self.inp["l3_wdown"][r0:r0 + 128, :], dreg("l3_wdown")))
            self.quant_fp4(dstb.full, src.full, tmpa.full, tmpb.full, tmpr.full, tmps.full, 2816)
            for isl in range(22):
                pst = Bd.psumT()
                self.transpose((pst.ap[:, 0:128], pst.reg(0, 128, 0, 128)),
                               (dstb.ap[:, 128 * isl:128 * (isl + 1)],
                                dstb.reg(0, 128, 128 * isl, 128 * (isl + 1))))
                self.copy(ev.full, (pst.ap[:, 0:128], pst.reg(0, 128, 0, 128)))
                Bd.dma("sp", (w3q(2, 22)[isl, :, r0:r0 + 128], dreg("W3Q")), ev)

    def l3_attn(self, b):
        P, Bd = self, self.B
        Bd.ph_reset()
        xnb = [Bd.ph(128, T, BF16) for _ in range(DT)]
        xtm = [Bd.ph(128, 1024, BF16) for _ in range(16)]
        tb = [[Bd.ph(128, 512, BF16) for _ in range(16)] for _ in range(2)]
        xres = [Bd.ph(128, 512, F32) for _ in range(4)]
        for dt in range(DT):
            Bd.dma("sp", xnb[dt], self.dsl(self.XNB, "XNBs", b, dt, esz=2))
        for mt in range(16):
            for dt in range(DT):
                pst = Bd.psumT()
                self.transpose((pst.ap[:, 0:128], pst.reg(0, 128, 0, 128)),
                               (xnb[dt].ap[:, 128 * mt:128 * (mt + 1)],
                                xnb[dt].reg(0, 128, 128 * mt, 128 * (mt + 1))))
                self.copy((xtm[mt].ap[:, 128 * dt:128 * (dt + 1)],
                           xtm[mt].reg(0, 128, 128 * dt, 128 * (dt + 1))),
                          (pst.ap[:, 0:128], pst.reg(0, 128, 0, 128)))
        for c in range(4):
            c0 = 512 * c
            tbs = tb[c % 2]
            for mt in range(16):
                Bd.dma("pool", tbs[mt], (self.inp["l3_toep"][128 * mt:128 * (mt + 1),
                                                             c0:c0 + 512], dreg("l3_toep")))
            for dt in range(DT):
                ps = Bd.psum(dt % 4)
                P.mm(ps.full, [((xtm[mt].ap[:, 128 * dt:128 * (dt + 1)],
                                 xtm[mt].reg(0, 128, 128 * dt, 128 * (dt + 1))),
                                tbs[mt].full) for mt in range(16)])
                xrb = xres[dt % 4]
                Bd.dma("sp", xrb, self.dsl(self.X, "Xs", b, dt, c0, c0 + 512))
                self.tt(ALU.add, xrb.full, xrb.full, ps.full)
                Bd.dma("sp", self.dsl(self.X, "Xs", b, dt, c0, c0 + 512), xrb)

    def l3_ffn(self, b):
        P, Bd = self, self.B
        Bd.ph_reset()
        wgq = Bd.ph(128, 8 * 2816, BF16)
        wuq = Bd.ph(128, 8 * 2816, BF16)
        wdq = Bd.ph(128, 22 * 1024, BF16)
        xq = [Bd.ph(128, T, BF16) for _ in range(DT)]
        xc = [Bd.ph(128, 128, F32) for _ in range(2)]
        ttm = [Bd.ph(128, 1024, F32)] * 2
        tmpa = [Bd.ph(128, 1024, F32) for _ in range(2)]
        tmpb = [Bd.ph(128, 1024, F32) for _ in range(2)]
        tmpr = [Bd.ph(128, 64, F32)] * 2
        tmps = [Bd.ph(128, 64, F32)] * 2
        qtm = [Bd.ph(128, 1024, BF16)] * 2
        hq1 = [Bd.ph(128, 128, BF16) for _ in range(22)]
        hq = [hq1, hq1]
        sg = [Bd.ph(128, 512, F32)] * 2
        h32 = [Bd.ph(128, 512, F32)] * 2
        hqb = [Bd.ph(128, 512, BF16)] * 2
        xres = [Bd.ph(128, 128, F32) for _ in range(2)]
        w3q = lambda wi, a: self.W3Q[wi, :].rearrange("(a p n) -> a p n", a=a, p=128)
        for ds in range(8):
            Bd.dma("sp", (wgq.ap[:, 2816 * ds:2816 * (ds + 1)],
                          wgq.reg(0, 128, 2816 * ds, 2816 * (ds + 1))),
                   (w3q(0, 8)[ds], dreg("W3Q")))
            Bd.dma("sp", (wuq.ap[:, 2816 * ds:2816 * (ds + 1)],
                          wuq.reg(0, 128, 2816 * ds, 2816 * (ds + 1))),
                   (w3q(1, 8)[ds], dreg("W3Q")))
        for isl in range(22):
            Bd.dma("sp", (wdq.ap[:, 1024 * isl:1024 * (isl + 1)],
                          wdq.reg(0, 128, 1024 * isl, 1024 * (isl + 1))),
                   (w3q(2, 22)[isl], dreg("W3Q")))
        for mt in range(16):
            m2_ = mt % 2
            for dt in range(DT):
                xcb = xc[(mt * DT + dt) % 2]
                Bd.dma("sp", xcb, self.dsl(self.XN32, "XN32s", b, dt,
                                           128 * mt, 128 * (mt + 1)))
                psf = Bd.psum(5 + dt % 2)
                self.transpose((psf.ap[:, 0:128], psf.reg(0, 128, 0, 128)),
                               xcb.full, bf16=False)
                self.copy((ttm[m2_].ap[:, 128 * dt:128 * (dt + 1)],
                           ttm[m2_].reg(0, 128, 128 * dt, 128 * (dt + 1))),
                          (psf.ap[:, 0:128], psf.reg(0, 128, 0, 128)))
            self.quant_fp4(qtm[m2_].full, ttm[m2_].full, tmpa[m2_].full,
                           tmpb[m2_].full, tmpr[m2_].full, tmps[m2_].full, 1024)
            for dt in range(DT):
                pst = Bd.psumT()
                self.transpose((pst.ap[:, 0:128], pst.reg(0, 128, 0, 128)),
                               (qtm[m2_].ap[:, 128 * dt:128 * (dt + 1)],
                                qtm[m2_].reg(0, 128, 128 * dt, 128 * (dt + 1))))
                self.copy((xq[dt].ap[:, 128 * mt:128 * (mt + 1)],
                           xq[dt].reg(0, 128, 128 * mt, 128 * (mt + 1))),
                          (pst.ap[:, 0:128], pst.reg(0, 128, 0, 128)))
        ics = [(0, 512), (512, 512), (1024, 512), (1536, 512), (2048, 512), (2560, 256)]
        for mt in range(16):
            m2_ = mt % 2
            for ici, (i0, iw) in enumerate(ics):
                psg = Bd.psum(2 * (ici % 2))
                psu = Bd.psum(2 * (ici % 2) + 1)
                P.mm((psg.ap[:, 0:iw], psg.reg()),
                     [((xq[ds].ap[:, 128 * mt:128 * (mt + 1)],
                        xq[ds].reg(0, 128, 128 * mt, 128 * (mt + 1))),
                       (wgq.ap[:, 2816 * ds + i0:2816 * ds + i0 + iw],
                        wgq.reg(0, 128, 2816 * ds + i0, 2816 * ds + i0 + iw)))
                      for ds in range(8)])
                P.mm((psu.ap[:, 0:iw], psu.reg()),
                     [((xq[ds].ap[:, 128 * mt:128 * (mt + 1)],
                        xq[ds].reg(0, 128, 128 * mt, 128 * (mt + 1))),
                       (wuq.ap[:, 2816 * ds + i0:2816 * ds + i0 + iw],
                        wuq.reg(0, 128, 2816 * ds + i0, 2816 * ds + i0 + iw)))
                      for ds in range(8)])
                qi = ici % 2
                self.act((sg[qi].ap[:, 0:iw], sg[qi].reg(0, 128, 0, iw)),
                         (psg.ap[:, 0:iw], psg.reg()), AF.Silu)
                self.tt(ALU.mult, (h32[qi].ap[:, 0:iw], h32[qi].reg(0, 128, 0, iw)),
                        (sg[qi].ap[:, 0:iw], sg[qi].reg(0, 128, 0, iw)),
                        (psu.ap[:, 0:iw], psu.reg()))
                self.quant_fp4((hqb[qi].ap[:, 0:iw], hqb[qi].reg(0, 128, 0, iw)),
                               (h32[qi].ap[:, 0:iw], h32[qi].reg(0, 128, 0, iw)),
                               (tmpa[qi].ap[:, 0:iw], tmpa[qi].reg(0, 128, 0, iw)),
                               (tmpb[qi].ap[:, 0:iw], tmpb[qi].reg(0, 128, 0, iw)),
                               (tmpr[qi].ap[:, 0:iw // 16], tmpr[qi].reg(0, 128, 0, iw // 16)),
                               (tmps[qi].ap[:, 0:iw // 16], tmps[qi].reg(0, 128, 0, iw // 16)), iw)
                for j in range(iw // 128):
                    isl = i0 // 128 + j
                    pst = Bd.psumT()
                    self.transpose((pst.ap[:, 0:128], pst.reg(0, 128, 0, 128)),
                                   (hqb[qi].ap[:, 128 * j:128 * (j + 1)],
                                    hqb[qi].reg(0, 128, 128 * j, 128 * (j + 1))))
                    self.copy(hq[m2_][isl].full,
                              (pst.ap[:, 0:128], pst.reg(0, 128, 0, 128)))
            for dt in range(DT):
                psd = Bd.psum(4 + dt % 2)
                P.mm((psd.ap[:, 0:128], psd.reg()),
                     [((wdq.ap[:, 1024 * isl + 128 * dt:1024 * isl + 128 * (dt + 1)],
                        wdq.reg(0, 128, 1024 * isl, 1024 * (isl + 1))),
                       hq[m2_][isl].full) for isl in range(22)])
                xrb = xres[dt % 2]
                Bd.dma("sp", xrb, self.dsl(self.X, "Xs", b, dt,
                                           128 * mt, 128 * (mt + 1)))
                self.tt(ALU.add, xrb.full, xrb.full, (psd.ap[:, 0:128], psd.reg()))
                Bd.dma("sp", self.dsl(self.X, "Xs", b, dt, 128 * mt, 128 * (mt + 1)),
                       xrb)

    def l2_attn(self, b):
        P, Bd = self, self.B
        Bd.ph_reset()
        qf = [Bd.ph(128, T, BF16) for _ in range(DT)]
        kf = [Bd.ph(128, T, BF16) for _ in range(DT)]
        vt = [Bd.ph(128, 16 * 65, BF16) for _ in range(16)]
        ocb = [Bd.ph(128, T, BF16) for _ in range(DT)]
        gcols = Bd.ph(64, 48, F32)
        lnw = Bd.ph(128, 16, F32)
        mk0 = Bd.mark()
        xnb = [Bd.ph(128, T, BF16) for _ in range(DT)]
        wbig = Bd.ph(128, 8192, BF16)
        vtmp = Bd.ph(128, 512, BF16)
        for dt in range(DT):
            Bd.dma("sp", xnb[dt], self.dsl(self.XNB, "XNBs", b, dt, esz=2))
        for dst, wname in ((qf, "l2a_wq"), (kf, "l2a_wk"), (None, "l2a_wv")):
            for k in range(DT):
                Bd.dma("pool", wbig[:, 1024 * k:1024 * (k + 1)],
                       (self.inp[wname][128 * k:128 * (k + 1), :], dreg(wname)))
            for dt in range(DT):
                for c in range(4):
                    c0 = 512 * c
                    ps = Bd.psum(c % 4)
                    P.mm(ps.full, [(wbig[:, 1024 * k + 128 * dt:1024 * k + 128 * (dt + 1)],
                                    xnb[k][:, c0:c0 + 512]) for k in range(DT)])
                    if dst is not None:
                        self.act(dst[dt][:, c0:c0 + 512], ps.full, AF.Copy)
                    else:
                        self.act(vtmp.full, ps.full, AF.Copy)
                        pstt = Bd.psumT()
                        for hh in range(2):
                            h = 2 * dt + hh
                            o2 = 64 * hh
                            for j in range(4):
                                pt = 4 * c + j
                                self.transpose(
                                    (pstt.ap[0:128, 0:64], pstt.reg(0, 128, 0, 64)),
                                    (vtmp.ap[o2:o2 + 64, 128 * j:128 * (j + 1)],
                                     vtmp.reg(o2, o2 + 64, 128 * j, 128 * (j + 1))),
                                    bf16=True, kp=64, base=o2)
                                self.copy((vt[pt].ap[:, 65 * h:65 * h + 64],
                                           vt[pt].reg(0, 128, 65 * h, 65 * h + 64)),
                                          (pstt.ap[0:128, 0:64], pstt.reg(0, 128, 0, 64)))
        for pt in range(16):
            self.memset((vt[pt].ap[:, 64:16 * 65:65], vt[pt].reg(0, 128, 0, 16 * 65)), 1.0)
        Bd.reset_to(mk0)
        vf = [Bd.ph(128, T, BF16) for _ in range(DT)]
        wck = Bd.ph(64, 32 * 64, BF16)
        wcv = Bd.ph(64, 32 * 64, BF16)
        khc = Bd.ph(64, T, BF16)
        wgt = Bd.ph(64, 4, BF16)
        ckb = [Bd.ph(128, 127, BF16) for _ in range(8)]
        cvt = [Bd.ph(128, 65, BF16) for _ in range(16)]
        pcn = Bd.ph(128, 127, BF16)
        pcT = Bd.ph(128, 2048, BF16)
        den = Bd.ph(128, 1, F32)
        qm = Bd.ph(64, 1, F32)
        qmb = Bd.ph(64, 1, BF16)
        grow = Bd.ph(1, 4, F32)
        g4 = Bd.ph(4, 1, F32)
        Bd.dma("pool", (wck.ap.rearrange("q (j n) -> q j n", j=32), wck.reg()),
               (self.inp["l2a_wck"].rearrange("(j q) n -> q j n", q=64),
                dreg("l2a_wck")))
        Bd.dma("pool", (wcv.ap.rearrange("q (j n) -> q j n", j=32), wcv.reg()),
               (self.inp["l2a_wcv"].rearrange("(j q) n -> q j n", q=64),
                dreg("l2a_wcv")))
        Bd.dma("pool", wgt, (self.inp["l2a_wg"], dreg("l2a_wg")))
        for h in range(16):
            t2, o = h // 2, 64 * (h % 2)
            for pt in range(16):
                pstt = Bd.psumT()
                self.transpose((pstt.ap[0:64, 0:128], pstt.reg(0, 64, 0, 128)),
                               (vt[pt].ap[:, 65 * h:65 * h + 64],
                                vt[pt].reg(0, 128, 65 * h, 65 * h + 64)),
                               bf16=True, kp=128)
                self.copy((vf[t2].ap[o:o + 64, 128 * pt:128 * (pt + 1)],
                           vf[t2].reg(o, o + 64, 128 * pt, 128 * (pt + 1))),
                          (pstt.ap[0:64, 0:128], pstt.reg(0, 64, 0, 128)))
        impb = Bd.psum(6)
        first_imp = True
        for h in range(16):
            t2, o = h // 2, 64 * (h % 2)
            self.B.t("dve", lambda e, t2=t2, o=o: e.reduce_sum(
                qm.ap, qf[t2].ap[o:o + 64, :], axis=AX.X),
                r=[qf[t2].full], w=[qm.full])
            self.copy(qmb.full, qm.full)
            psg = Bd.psum(4)
            P.mm((psg.ap[0:4, 0:1], psg.reg()), [((wgt.ap, wgt.reg()), qmb.full)])
            self.act(g4.full, (psg.ap[0:4, 0:1], psg.reg()),
                     AF.Sigmoid, scale=1.0 / T, bias=self.vcol("l2a_bg", 0, P=4))
            psr = Bd.psum(5)
            P.mm((psr.ap[0:1, 0:4], psr.reg()),
                 [(g4.full, (self.identf.ap[0:4, 0:4], self.identf.reg(0, 4, 0, 4)))])
            self.copy(grow.full, (psr.ap[0:1, 0:4], psr.reg()))
            for j in range(3):
                psr2 = Bd.psum(4)
                P.mm((psr2.ap[0:64, 0:1], psr2.reg()),
                     [((self.ones_row_f.ap[0:1, 0:64], self.ones_row_f.reg(0, 1, 0, 64)),
                       (grow.ap[0:1, j:j + 1], grow.reg(0, 1, j, j + 1)))])
                self.copy((gcols.ap[:, 3 * h + j:3 * h + j + 1],
                           gcols.reg(0, 64, 3 * h + j, 3 * h + j + 1)),
                          (psr2.ap[0:64, 0:1], psr2.reg()))
            for which, wc, dstt in ((0, wck, ckb), (1, wcv, None)):
                psc = Bd.psum(4)
                kvsrc = kf if which == 0 else vf
                self.copy(khc.full, (kvsrc[t2].ap[o:o + 64, :],
                                     kvsrc[t2].reg(o, o + 64, 0, T)))
                pairs = []
                for j in range(32):
                    lhs = (wc.ap[0:64, 64 * j:64 * (j + 1)],
                           wc.reg(0, 64, 64 * j, 64 * (j + 1)))
                    rhs = (khc.ap[0:64, j:j + 16 * 126 + 1:16],
                           khc.reg(0, 64, 0, T))
                    pairs.append((lhs, rhs))
                P.mm((psc.ap[0:64, 0:127], psc.reg()), pairs)
                if which == 0:
                    self.ts((ckb[t2].ap[o:o + 64, 0:127],
                             ckb[t2].reg(o, o + 64, 0, 127)),
                            (psc.ap[0:64, 0:127], psc.reg()),
                            self.vcol("l2a_bck", 0, P=64), op0=ALU.add)
                else:
                    self.ts(pcn[0:64, 0:127], (psc.ap[0:64, 0:127], psc.reg()),
                            self.vcol("l2a_bcv", 0, P=64), op0=ALU.add)
                    pstc = Bd.psumT()
                    self.transpose((pstc.ap[0:127, 0:64], pstc.reg(0, 127, 0, 64)),
                                   pcn[0:64, 0:127], bf16=True, kp=64)
                    self.copy((cvt[h].ap[0:127, 0:64], cvt[h].reg(0, 127, 0, 64)),
                              (pstc.ap[0:127, 0:64], pstc.reg(0, 127, 0, 64)))
                    self.memset((cvt[h].ap[0:127, 64:65], cvt[h].reg(0, 127, 64, 65)), 1.0)
            for it in range(16):
                psS = Bd.psum(it % 2)
                P.mm((psS.ap[:, 0:127], psS.reg()),
                     [((qf[t2].ap[o:o + 64, 128 * it:128 * (it + 1)],
                        qf[t2].reg(o, o + 64, 128 * it, 128 * (it + 1))),
                       (ckb[t2].ap[o:o + 64, 0:127], ckb[t2].reg(o, o + 64, 0, 127)))])
                self.act(pcn[0:128, 0:127], (psS.ap[:, 0:127], psS.reg()),
                         AF.Exp, scale=0.125, accum=den.full)
                self.B.t("dve", lambda e: e.reciprocal(den.ap, den.ap),
                         r=[den.full], w=[den.full])
                self.ts(pcn[0:128, 0:127], pcn[0:128, 0:127], den.full, op0=ALU.mult)
                if DBG and b == 0 and h == 15 and it == 15:
                    self.dbg("l2_pcn_last", pcn)
                    self.dbg("l2_den_last", den)
                pstp = Bd.psumT()
                self.transpose((pstp.ap[0:127, 0:128], pstp.reg(0, 127, 0, 128)),
                               pcn[0:128, 0:127], bf16=True, kp=128)
                self.copy((pcT.ap[0:127, 128 * it:128 * (it + 1)],
                           pcT.reg(0, 127, 128 * it, 128 * (it + 1))),
                          (pstp.ap[0:127, 0:128], pstp.reg(0, 127, 0, 128)))
                P.mm((impb.ap[0:1, 0:127], impb.reg()),
                     [((self.ones_col_b.ap, self.ones_col_b.reg()),
                       pcn[0:128, 0:127])], start=first_imp, stop=(h == 15 and it == 15))
                first_imp = False
            for c in range(4):
                c0 = 512 * c
                pso = Bd.psum(2 + c % 2)
                P.mm((pso.ap[0:64, :], pso.reg()),
                     [((cvt[h].ap[0:127, 0:64], cvt[h].reg(0, 127, 0, 64)),
                       (pcT.ap[0:127, c0:c0 + 512], pcT.reg(0, 127, c0, c0 + 512)))])
                self.act((ocb[t2].ap[o:o + 64, c0:c0 + 512],
                          ocb[t2].reg(o, o + 64, c0, c0 + 512)),
                         (pso.ap[0:64, :], pso.reg()), AF.Copy)
        bi = Bd.ph(1, 127, F32)
        ohall = Bd.ph(1, 127, F32)
        onesr = Bd.ph(1, 127, F32)
        cum = Bd.ph(1, 127, F32)
        fo = Bd.ph(1, 127, F32)
        mrow = Bd.ph(1, 1, F32)
        ohc = Bd.ph(127, 1, BF16)
        wcol = Bd.ph(128, 1, F32)
        ecl = Bd.ph(127, 16 * 128, BF16)
        cE = self.lazy_const("c_Eb", lambda: np.ascontiguousarray(_l2_E().astype(BF)))
        Bd.dma("sp", ecl, (cE, dreg("c_Eb")))
        self.copy(bi.full, (impb.ap[0:1, 0:127], impb.reg()))
        self.memset(onesr.full, 1.0)
        self.memset(ohall.full, 0.0)
        for it16 in range(16):
            self.B.t("dve", lambda e: e.reduce_max(mrow.ap, bi.ap, axis=AX.X),
                     r=[bi.full], w=[mrow.full])
            self.ts(fo, bi, (mrow.ap, mrow.reg()), None, op0=ALU.is_ge, eng="dve")
            self.B.t("dve", lambda e: e.tensor_tensor_scan(
                cum.ap, onesr.ap, fo.ap, 0.0, op0=ALU.mult, op1=ALU.max),
                r=[onesr.full, fo.full], w=[cum.full])
            self.copy((fo.ap[0:1, 0:1], fo.reg(0, 1, 0, 1)),
                      (cum.ap[0:1, 0:1], cum.reg(0, 1, 0, 1)))
            self.tt(ALU.subtract, (fo.ap[0:1, 1:127], fo.reg(0, 1, 1, 127)),
                    (cum.ap[0:1, 1:127], cum.reg(0, 1, 1, 127)),
                    (cum.ap[0:1, 0:126], cum.reg(0, 1, 0, 126)))
            self.tt(ALU.add, ohall.full, ohall.full, fo.full)
            self.stt(bi, fo, -1e38, bi, op0=ALU.mult, op1=ALU.add, eng="dve")
        psoc = Bd.psum(4)
        P.mm((psoc.ap[0:127, 0:1], psoc.reg()),
             [((ohall.ap, ohall.reg()), (self.ones_row_f.ap[0:1, 0:1],
                                         self.ones_row_f.reg(0, 1, 0, 1)))])
        self.copy(ohc.full, (psoc.ap[0:127, 0:1], psoc.reg()))
        for pt in range(16):
            psw = Bd.psum(5)
            P.mm((psw.ap[:, 0:1], psw.reg()),
                 [((ecl.ap[0:127, 128 * pt:128 * (pt + 1)],
                    ecl.reg(0, 127, 128 * pt, 128 * (pt + 1))), ohc.full)])
            self.copy(wcol.full, (psw.ap[:, 0:1], psw.reg()))
            if DBG and b == 0 and pt == 0:
                self.dbg("l2_ohc", ohc)
                self.dbg("l2_wcol0", wcol)
            self.ts((lnw.ap[:, pt:pt + 1], lnw.reg(0, 128, pt, pt + 1)),
                    wcol.full, 1e-30, None, op0=ALU.max, eng="dve")
            self.act((lnw.ap[:, pt:pt + 1], lnw.reg(0, 128, pt, pt + 1)),
                     (lnw.ap[:, pt:pt + 1], lnw.reg(0, 128, pt, pt + 1)), AF.Ln)
            self.ts(wcol.full, wcol.full, 0.5, -1e38, op0=ALU.is_le, op1=ALU.mult)
            self.tt(ALU.add, (lnw.ap[:, pt:pt + 1], lnw.reg(0, 128, pt, pt + 1)),
                    (lnw.ap[:, pt:pt + 1], lnw.reg(0, 128, pt, pt + 1)), wcol.full)
        if DBG and b == 0:
            self.dbg("l2_ckb7", ckb[7])
            self.dbg("l2_cvt15", cvt[15])
            self.dbg("l2_pcT", pcT)
            self.dbg("l2_gcols", gcols)
            self.dbg("l2_ohall", ohall)
            self.dbg("l2_lnw", lnw)
            self.dbg("l2_ocb0", ocb[0])
        Bd.reset_to(mk0)
        ofin = [Bd.ph(128, T, BF16) for _ in range(DT)]
        wo8 = Bd.ph(128, 8192, BF16)
        pw = [Bd.ph(128, 512, BF16) for _ in range(2)]
        pw2 = [Bd.ph(128, 512, BF16) for _ in range(2)]
        denr = Bd.ph(1, 512, F32)
        reps = Bd.ph(64, 512, F32)
        repw = Bd.ph(64, 512, F32)
        osl = Bd.ph(64, 512, F32)
        owl = Bd.ph(64, 512, F32)
        xres = Bd.ph(128, T, F32)
        for k in range(DT):
            Bd.dma("pool", wo8[:, 1024 * k:1024 * (k + 1)],
                   (self.inp["l2a_wo"][128 * k:128 * (k + 1), :], dreg("l2a_wo")))
        for h in range(16):
            t2, o = h // 2, 64 * (h % 2)
            for ic in range(4):
                i0 = 512 * ic
                psel = Bd.psum(2)
                pwin = Bd.psum(3)
                for pt in range(16):
                    psS = Bd.psum(pt % 2)
                    P.mm(psS.full,
                         [((kf[t2].ap[o:o + 64, 128 * pt:128 * (pt + 1)],
                            kf[t2].reg(o, o + 64, 128 * pt, 128 * (pt + 1))),
                           (qf[t2].ap[o:o + 64, i0:i0 + 512],
                            qf[t2].reg(o, o + 64, i0, i0 + 512)))])
                    pb = pw[pt % 2]
                    self.act(pb.full, psS.full, AF.Exp, scale=0.125,
                             bias=(lnw.ap[:, pt:pt + 1], lnw.reg(0, 128, pt, pt + 1)))
                    P.mm((psel.ap[0:65, :], psel.reg()),
                         [((vt[pt].ap[:, 65 * h:65 * (h + 1)],
                            vt[pt].reg(0, 128, 65 * h, 65 * (h + 1))), pb.full)],
                         start=(pt == 0), stop=(pt == 15))
                    if pt >= 12:
                        pb2 = pw2[pt % 2]
                        self.act(pb2.full, psS.full, AF.Exp, scale=0.125)
                        P.mm((pwin.ap[0:65, :], pwin.reg()),
                             [((vt[pt].ap[:, 65 * h:65 * (h + 1)],
                                vt[pt].reg(0, 128, 65 * h, 65 * (h + 1))), pb2.full)],
                             start=(pt == 12), stop=(pt == 15))
                for which, pso, dst in ((0, psel, osl), (1, pwin, owl)):
                    self.B.t("dve", lambda e, pso=pso: e.reciprocal(
                        denr.ap, pso.ap[64:65, :]), r=[pso.full], w=[denr.full])
                    psr = Bd.psum(4 + which)
                    P.mm((psr.ap[0:64, :], psr.reg()),
                         [((self.ones_row_f.ap[0:1, 0:64],
                            self.ones_row_f.reg(0, 1, 0, 64)), denr.full)])
                    self.copy(reps.full, (psr.ap[0:64, :], psr.reg()))
                    self.tt(ALU.mult, dst.full, (pso.ap[0:64, :], pso.reg()), reps.full)
                if DBG and b == 0 and h == 0 and ic == 0:
                    self.dbg("l2_osl00", osl)
                    self.dbg("l2_owl00", owl)
                self.ts(repw.full, (ocb[t2].ap[o:o + 64, i0:i0 + 512],
                                    ocb[t2].reg(o, o + 64, i0, i0 + 512)),
                        (gcols.ap[:, 3 * h:3 * h + 1], gcols.reg(0, 64, 3 * h, 3 * h + 1)),
                        None, op0=ALU.mult)
                self.stt(repw, osl.full, (gcols.ap[:, 3 * h + 1:3 * h + 2],
                                          gcols.reg(0, 64, 3 * h + 1, 3 * h + 2)),
                         repw.full, op0=ALU.mult, op1=ALU.add)
                self.stt((ofin[t2].ap[o:o + 64, i0:i0 + 512],
                          ofin[t2].reg(o, o + 64, i0, i0 + 512)),
                         owl.full, (gcols.ap[:, 3 * h + 2:3 * h + 3],
                                    gcols.reg(0, 64, 3 * h + 2, 3 * h + 3)),
                         repw.full, op0=ALU.mult, op1=ALU.add)
        if DBG and b == 0:
            self.dbg("l2_ofin0", ofin[0])
        for dt in range(DT):
            Bd.dma("sp", xres, self.dsl(self.X, "Xs", b, dt))
            for c in range(4):
                c0 = 512 * c
                ps = Bd.psum(6)
                P.mm(ps.full, [(wo8[:, 1024 * k + 128 * dt:1024 * (k) + 128 * (dt + 1)],
                                ofin[k][:, c0:c0 + 512]) for k in range(DT)])
                self.tt(ALU.add, xres[:, c0:c0 + 512], xres[:, c0:c0 + 512], ps.full)
            Bd.dma("sp", self.dsl(self.X, "Xs", b, dt), xres)

    def l2_ffn(self, b):
        P, Bd = self, self.B
        Bd.ph_reset()
        xnb = [Bd.ph(128, T, BF16) for _ in range(DT)]
        gw = Bd.ph(128, 64, BF16)
        lg = Bd.ph(8, T, F32)
        Wsb = Bd.ph(8, T, F32)
        oh8 = Bd.ph(8, 1024, F32)
        Lt = Bd.ph(128, 8, F32)
        m1 = Bd.ph(128, 1, F32)
        nm1 = Bd.ph(128, 1, F32)
        m2 = Bd.ph(128, 1, F32)
        rd = Bd.ph(128, 1, F32)
        msk = Bd.ph(128, 8, F32)
        W8 = Bd.ph(128, 8, F32)
        acc = [Bd.ph(128, 1024, F32) for _ in range(DT)]
        zb = [Bd.ph(128, 1024, BF16) for _ in range(16)]
        sb = [Bd.ph(128, 1024, BF16) for _ in range(16)]
        w1b = [Bd.ph(128, 1024, BF16) for _ in range(2)]
        w2b = [Bd.ph(128, 1024, BF16) for _ in range(2)]
        wgb = [Bd.ph(128, 2048, BF16) for _ in range(2)]
        w3b = [Bd.ph(128, 2048, BF16) for _ in range(2)]
        wrp = Bd.ph(128, 1024, F32)
        t1f = Bd.ph(128, 512, F32)
        ysb = Bd.ph(128, 512, F32)
        yt = Bd.ph(128, 512, F32)
        xres = Bd.ph(128, 1024, F32)
        for dt in range(DT):
            Bd.dma("sp", xnb[dt], self.dsl(self.XNB, "XNBs", b, dt, esz=2))
        Bd.dma("pool", (gw.ap.rearrange("p (k n) -> p k n", k=8), gw.reg()),
               (self.inp["l2f_gw"].rearrange("(k p) n -> p k n", p=128),
                dreg("l2f_gw")))
        for c in range(4):
            c0 = 512 * c
            psl = Bd.psum(4)
            P.mm((psl.ap[0:8, :], psl.reg()),
                 [((gw.ap[:, 8 * k:8 * (k + 1)], gw.reg(0, 128, 8 * k, 8 * (k + 1))),
                   xnb[k][:, c0:c0 + 512]) for k in range(DT)])
            self.ts((lg.ap[0:8, c0:c0 + 512], lg.reg(0, 8, c0, c0 + 512)),
                    (psl.ap[0:8, :], psl.reg()),
                    self.vcol("l2f_gateb", 0, P=8), op0=ALU.add)
        for mt in range(16):
            psf = Bd.psum(5)
            self.transpose((psf.ap[0:128, 0:8], psf.reg()),
                           (lg.ap[0:8, 128 * mt:128 * (mt + 1)],
                            lg.reg(0, 8, 128 * mt, 128 * (mt + 1))), bf16=False, kp=8)
            self.copy(Lt.full, (psf.ap[0:128, 0:8], psf.reg()))
            self.red(m1, Lt, op=ALU.max)
            self.ts(nm1, m1, -1.0, None, op0=ALU.mult)
            self.ts(msk, Lt, m1.full, -1e38, op0=ALU.is_ge, op1=ALU.mult)
            self.tt(ALU.add, msk, msk, Lt)
            self.red(m2, msk, op=ALU.max)
            self.tt(ALU.subtract, rd, m2, m1)
            self.act(rd, rd, AF.Exp)
            self.ts(rd, rd, 1.0, None, op0=ALU.add)
            self.B.t("dve", lambda e: e.reciprocal(rd.ap, rd.ap),
                     r=[rd.full], w=[rd.full])
            self.act(W8, Lt, AF.Exp, bias=nm1.full)
            self.ts(msk, Lt, m2.full, 1.0, op0=ALU.is_ge, op1=ALU.mult)
            self.tt(ALU.mult, W8, W8, msk)
            self.ts(W8, W8, rd.full, 0.0, op0=ALU.mult, op1=ALU.add)
            psf2 = Bd.psum(6)
            self.transpose((psf2.ap[0:8, 0:128], psf2.reg()), W8.full,
                           bf16=False, kp=128)
            self.copy((Wsb.ap[0:8, 128 * mt:128 * (mt + 1)],
                       Wsb.reg(0, 8, 128 * mt, 128 * (mt + 1))),
                      (psf2.ap[0:8, 0:128], psf2.reg()))
        oh8np = np.zeros((8, 1024), np.float32)
        for e in range(8):
            oh8np[e, 128 * e:128 * (e + 1)] = 1.0
        Bd.dma("sp", oh8, (self.lazy_const("c_oh8", lambda: oh8np), dreg("c_oh8")))
        for c2 in range(2):
            q0 = 1024 * c2
            for dt in range(DT):
                self.memset(acc[dt], 0.0)
            for e in range(8):
                psw = Bd.psum(5, 2)
                for hh in range(2):
                    P.mm((psw.ap[:, 512 * hh:512 * (hh + 1)],
                          psw.reg(0, 128, 512 * hh, 512 * (hh + 1))),
                         [((oh8.ap[:, 128 * e:128 * (e + 1)],
                            oh8.reg(0, 8, 128 * e, 128 * (e + 1))),
                           (Wsb.ap[0:8, q0 + 512 * hh:q0 + 512 * (hh + 1)],
                            Wsb.reg(0, 8, q0 + 512 * hh, q0 + 512 * (hh + 1))))])
                self.copy(wrp, psw)
                for ft in range(16):
                    wa_, wb_ = w1b[ft % 2], w2b[ft % 2]
                    Bd.dma("pool", (wa_.ap, wa_.reg()),
                           (self.inp["l2f_w1"].rearrange("e (kt p) m -> e p kt m", p=128)
                            [e, :, :, 128 * ft:128 * (ft + 1)], dreg("l2f_w1")))
                    Bd.dma("pool", (wb_.ap, wb_.reg()),
                           (self.inp["l2f_w2"].rearrange("e (kt p) m -> e p kt m", p=128)
                            [e, :, :, 128 * ft:128 * (ft + 1)], dreg("l2f_w2")))
                    for hh in range(2):
                        h0 = q0 + 512 * hh
                        psa = Bd.psum(hh % 2)
                        psb_ = Bd.psum(2 + hh % 2)
                        P.mm(psa.full, [((wa_.ap[:, 128 * k:128 * (k + 1)],
                                          wa_.reg(0, 128, 128 * k, 128 * (k + 1))),
                                         xnb[k][:, h0:h0 + 512]) for k in range(DT)])
                        P.mm(psb_.full, [((wb_.ap[:, 128 * k:128 * (k + 1)],
                                           wb_.reg(0, 128, 128 * k, 128 * (k + 1))),
                                          xnb[k][:, h0:h0 + 512]) for k in range(DT)])
                        self.ts(t1f, psa.full, self.vcol(f"l2f_b1_{e}", ft),
                                op0=ALU.add)
                        wait = self.stt((zb[ft].ap[:, 512 * hh:512 * (hh + 1)],
                                         zb[ft].reg(0, 128, 1024 * hh // 2,
                                                    1024 * (hh + 1) // 2)),
                                        psb_.full, self.vcol(f"l2f_b2_{e}", ft),
                                        t1f.full, op0=ALU.add, op1=ALU.mult)
                for gt in range(16):
                    wg_ = wgb[gt % 2]
                    Bd.dma("pool", (wg_.ap, wg_.reg()),
                           (self.inp["l2f_wg"].rearrange("e (kt p) m -> e p kt m", p=128)
                            [e, :, :, 128 * gt:128 * (gt + 1)], dreg("l2f_wg")))
                    for hh in range(2):
                        psg = Bd.psum(4 + hh % 2)
                        P.mm(psg.full, [((wg_.ap[:, 128 * k:128 * (k + 1)],
                                          wg_.reg(0, 128, 128 * k, 128 * (k + 1))),
                                         (zb[k].ap[:, 512 * hh:512 * (hh + 1)],
                                          zb[k].reg(0, 128, 1024 * hh // 2,
                                                    1024 * (hh + 1) // 2)))
                                        for k in range(16)])
                        self.act((sb[gt].ap[:, 512 * hh:512 * (hh + 1)],
                                  sb[gt].reg(0, 128, 1024 * hh // 2,
                                             1024 * (hh + 1) // 2)),
                                 psg.full, AF.Silu, bias=self.vcol(f"l2f_bg_{e}", gt))
                for dt in range(DT):
                    w3_ = w3b[dt % 2]
                    Bd.dma("pool", (w3_.ap, w3_.reg()),
                           (self.inp["l2f_w3"].rearrange("e (kt p) m -> e p kt m", p=128)
                            [e, :, :, 128 * dt:128 * (dt + 1)], dreg("l2f_w3")))
                    for hh in range(2):
                        psy = Bd.psum(hh % 2)
                        P.mm(psy.full, [((w3_.ap[:, 128 * k:128 * (k + 1)],
                                          w3_.reg(0, 128, 128 * k, 128 * (k + 1))),
                                         (sb[k].ap[:, 512 * hh:512 * (hh + 1)],
                                          sb[k].reg(0, 128, 1024 * hh // 2,
                                                    1024 * (hh + 1) // 2)))
                                        for k in range(16)])
                        self.ts(ysb, psy.full, self.vcol(f"l2f_b3_{e}", dt),
                                op0=ALU.add)
                        self.tt(ALU.mult, yt, ysb,
                                (wrp.ap[:, 512 * hh:512 * (hh + 1)],
                                 wrp.reg(0, 128, 512 * hh, 512 * (hh + 1))), eng="pool")
                        self.tt(ALU.add, (acc[dt].ap[:, 512 * hh:512 * (hh + 1)],
                                          acc[dt].reg(0, 128, 512 * hh, 512 * (hh + 1))),
                                (acc[dt].ap[:, 512 * hh:512 * (hh + 1)],
                                 acc[dt].reg(0, 128, 512 * hh, 512 * (hh + 1))),
                                yt.full, eng="pool")
            for dt in range(DT):
                Bd.dma("sp", xres, self.dsl(self.X, "Xs", b, dt, q0, q0 + 1024))
                self.tt(ALU.add, xres.full, xres.full, acc[dt].full)
                Bd.dma("sp", self.dsl(self.X, "Xs", b, dt, q0, q0 + 1024), xres)

    def l0_ffn(self, b):
        P, Bd = self, self.B
        Bd.ph_reset()
        xnb = [Bd.ph(128, T, BF16) for _ in range(DT)]
        w2 = Bd.ph(128, 32 * 1024, BF16)
        w1b = [Bd.ph(128, 1024, BF16) for _ in range(2)]
        hsb = Bd.ph(128, 32 * 512, BF16)
        ssb = [Bd.ph(128, 512, F32) for _ in range(DT)]
        sb16 = [Bd.ph(128, 512, BF16) for _ in range(DT)]
        sq16 = [Bd.ph(128, 512, BF16) for _ in range(DT)]
        mean = Bd.ph(1, 512, F32)
        e2 = Bd.ph(1, 512, F32)
        inv = Bd.ph(1, 512, F32)
        msr = Bd.ph(1, 512, F32)
        t1 = Bd.ph(128, 512, F32)
        xres = Bd.ph(128, 512, F32)
        xn32c = Bd.ph(128, 512, F32)
        for dt in range(DT):
            Bd.dma("sp", xnb[dt], self.dsl(self.XNB, "XNBs", b, dt, esz=2))
        for f in range(32):
            Bd.dma("pool", w2[:, 1024 * f:1024 * (f + 1)],
                   (self.inp["l0f_w2"][128 * f:128 * (f + 1), :], dreg("l0f_w2")))
        onesb = self.ones_col_b.full
        for c in range(4):
            c0 = 512 * c
            for f in range(32):
                wb = w1b[f % 2]
                Bd.dma("pool", wb,
                       (self.inp["l0f_w1"].rearrange("(kt p) m -> p kt m", p=128)[:, :, 128 * f:128 * (f + 1)],
                        dreg("l0f_w1")))
                ps_h = Bd.psum(f % 2)
                P.mm(ps_h.full, [(wb[:, 128 * k:128 * (k + 1)],
                                  xnb[k][:, c0:c0 + 512]) for k in range(DT)])
                self.act(hsb[:, 512 * f:512 * (f + 1)], ps_h.full, AF.Relu,
                         bias=self.vcol("l0f_b1", f))
            for dt in range(DT):
                ps_o = Bd.psum(2 + dt % 2)
                P.mm(ps_o.full, [(w2[:, 1024 * f + 128 * dt:1024 * f + 128 * (dt + 1)],
                                  hsb[:, 512 * f:512 * (f + 1)])
                                 for f in range(32)])
                Bd.dma("sp", xn32c, self.dsl(self.XN32, "XN32s", b, dt, c0, c0 + 512))
                self.stt(ssb[dt].full, ps_o.full, self.vcol("l0f_b2", dt),
                         (xn32c.ap, xn32c.reg()), op0=ALU.add, op1=ALU.add)
                self.copy(sb16[dt].full, ssb[dt].full)
                self.tt(ALU.mult, sq16[dt].full, sb16[dt].full, sb16[dt].full)
            ps_s, ps_q = Bd.psum(4), Bd.psum(5)
            P.mm((ps_s.ap[0:1, :], ps_s.reg()), [(onesb, sb16[dt].full) for dt in range(DT)])
            P.mm((ps_q.ap[0:1, :], ps_q.reg()), [(onesb, sq16[dt].full) for dt in range(DT)])
            self.ts(mean.full, (ps_s.ap[0:1, :], ps_s.reg()), 1.0 / D, op0=ALU.mult)
            self.ts(e2.full, (ps_q.ap[0:1, :], ps_q.reg()), 1.0 / D, op0=ALU.mult)
            self.tt(ALU.mult, inv.full, mean.full, mean.full)
            self.tt(ALU.subtract, inv.full, e2.full, inv.full)
            self.act(inv.full, inv.full, AF.Ln, bias=1e-5)
            self.act(inv.full, inv.full, AF.Exp, scale=-0.5)
            self.stt(msr.full, mean.full, -1.0, inv.full, op0=ALU.mult, op1=ALU.mult)
            psA_, psB_ = Bd.psum(4), Bd.psum(5)
            P.mm(psA_.full, [((self.ones_row_f.ap, self.ones_row_f.reg()), inv.full)])
            P.mm(psB_.full, [((self.ones_row_f.ap, self.ones_row_f.reg()), msr.full)])
            for dt in range(DT):
                self.tt(ALU.mult, t1.full, ssb[dt].full, psA_.full)
                self.tt(ALU.add, t1.full, t1.full, psB_.full)
                self.ts(t1.full, t1.full, self.vcol("l0f_gamma", dt),
                        self.vcol("l0f_beta", dt), op0=ALU.mult, op1=ALU.add)
                if DBG and b == 0 and c == 0 and dt == 0:
                    self.dbg("ffn_out00", t1)
                Bd.dma("sp", xres, self.dsl(self.X, "Xs", b, dt, c0, c0 + 512))
                self.tt(ALU.add, xres.full, xres.full, t1.full)
                Bd.dma("sp", self.dsl(self.X, "Xs", b, dt, c0, c0 + 512), xres)
            if DBG and b == 0 and c == 0:
                self.dbg("ffn_h0", Tile(hsb.ap[:, 0:512], "S", 0, hsb.byte0, BF16, 128, 512))
                self.dbg("ffn_s0", ssb[0])
                self.dbg("ffn_mean", mean)
                self.dbg("ffn_inv", inv)


DBG = []


def build_custom(stage_fns):
    nc = bass.Bass("TRN2", target_bir_lowering=False, debug=False)
    with ExitStack() as S:
        blk = S.enter_context(nc.Block())
        P = Prog(nc, S)
        P.load_consts()
        for fn in stage_fns:
            fn(P)
        P.B.finish(blk)
    return nc


def build_program(n_blocks=4, dump=None):
    nc = bass.Bass("TRN2", target_bir_lowering=False, debug=False)
    with ExitStack() as S:
        blk = S.enter_context(nc.Block())
        P = Prog(nc, S, n_blocks)
        P.load_consts()
        if n_blocks >= 4:
            P.l3_prep_weights()
        for i in range(n_blocks):
            for b in range(B):
                P.ln_entry(b)
                if i == 0:
                    P.l0_attn(b)
                    P.l0_ffn(b)
                elif i == 1:
                    P.l1_attn(b)
                    P.l1_ffn(b)
                elif i == 2:
                    P.l2_attn(b)
                    P.l2_ffn(b)
                elif i == 3:
                    P.l3_attn(b)
                    P.l3_ffn(b)
        for b in range(B):
            for dt in range(DT):
                P.B.dma("sp", (P.xout[b, 128 * dt:128 * (dt + 1), :], dreg("xout")),
                        P.dsl(P.X, "Xs", b, dt))
        if dump:
            for j, (dr, name, shape_esz) in enumerate(dump):
                pass
        P.B.finish(blk)
    return nc


_PROG_CACHE = {}


def kernel(**inputs):
    nc = build_program(4)
    in_maps = [stage_inputs(inputs, c) for c in range(NCORES)]
    res = run_bass_kernel_spmd(nc, in_maps, core_ids=list(range(NCORES)))
    outs = []
    for c in range(NCORES):
        o = np.asarray(res.results[c]["xout"], np.float32)
        outs.append(o.transpose(0, 2, 1))
    return np.ascontiguousarray(np.concatenate(outs, axis=0))


# version 03
# speedup vs baseline: 1.0382x; 1.0296x over previous
import numpy as np
import ml_dtypes
import concourse.bass as bass
import concourse.mybir as mybir
from concourse.bass_utils import run_bass_kernel_spmd
from contextlib import ExitStack

F32 = mybir.dt.float32
BF16 = mybir.dt.bfloat16
U8 = mybir.dt.uint8
AF = mybir.ActivationFunctionType
ALU = mybir.AluOpType
AX = mybir.AxisListType
BF = ml_dtypes.bfloat16

NCORES = 8
B = 2
T = 2048
D = 1024
DT = D // 128

SB_BYTES = 206 * 1024
BUCKET = 4096


class Reg:
    __slots__ = ("space", "p0", "p1", "b0", "b1")

    def __init__(self, space, p0, p1, b0, b1):
        self.space, self.p0, self.p1, self.b0, self.b1 = space, p0, p1, b0, b1


class Tile:
    __slots__ = ("ap", "space", "p0", "byte0", "dtype", "P", "N", "esz")

    def __init__(self, ap, space, p0, byte0, dtype, P, N):
        self.ap, self.space, self.p0, self.byte0 = ap, space, p0, byte0
        self.dtype, self.P, self.N = dtype, P, N
        self.esz = mybir.dt.size(dtype)

    def __getitem__(self, idx):
        ps, es = idx
        p0, p1 = (ps.start or 0), (ps.stop if ps.stop is not None else self.P)
        e0 = es.start or 0
        e1 = es.stop if es.stop is not None else self.N
        step = es.step or 1
        ap = self.ap[p0:p1, e0:e1:step] if step != 1 else self.ap[p0:p1, e0:e1]
        return ap, self.reg(p0, p1, e0, e1)

    def reg(self, p0=0, p1=None, e0=0, e1=None):
        p1 = self.P if p1 is None else p1
        e1 = self.N if e1 is None else e1
        return Reg(self.space, self.p0 + p0, self.p0 + p1,
                   self.byte0 + e0 * self.esz, self.byte0 + e1 * self.esz)

    @property
    def full(self):
        return self.ap, self.reg()


def dreg(name, lo=0, hi=1 << 62):
    return Reg("D:" + name, 0, 128, lo, hi)


class MK:
    ENGS = ("pe", "act", "dve", "pool", "sp")

    def __init__(self, nc, stack):
        self.nc = nc
        self.S = stack
        self.arena = stack.enter_context(nc.sbuf_tensor("AR", [128, SB_BYTES], U8))
        self.psA = stack.enter_context(nc.psum_tensor("PSA", [128, 3584], F32))
        self.psT = stack.enter_context(nc.psum_tensor("PST", [128, 1024], BF16))
        self.NLANE = 4
        self.sem = {}
        for s in ("pe", "act", "dve", "pool", "sp"):
            self.sem[s] = stack.enter_context(nc.semaphore("s_" + s))
        for q in ("sq", "pq"):
            for l in range(self.NLANE):
                self.sem[q + str(l)] = stack.enter_context(nc.semaphore("s_" + q + str(l)))
        self.count = {s: 0 for s in self.sem}
        self.qidx = {"sq": 0, "pq": 0}
        self.vc = {e: {} for e in self.ENGS}
        self.tasks = {e: [] for e in self.ENGS}
        self.acc = {}
        self.off = 0
        self.ph_off = None
        self.n_waits = 0
        self.n_tasks = 0
        self.serial = set()
        self.same_engine_sync = True

    def _carve(self, P, N, dtype, at):
        esz = mybir.dt.size(dtype)
        o = (at + 31) // 32 * 32
        nb = N * esz
        assert o + nb <= SB_BYTES, f"SBUF arena overflow: {o+nb}"
        ap = self.arena[0:P, o:o + nb].bitcast(dtype)
        return Tile(ap, "S", 0, o, dtype, P, N), o + nb

    def carve(self, P, N, dtype):
        t, self.off = self._carve(P, N, dtype, self.off)
        assert self.ph_off is None, "carve() after phase allocs started"
        return t

    def ph_reset(self):
        self.ph_off = self.off

    def mark(self):
        return self.ph_off

    def reset_to(self, m):
        self.ph_off = m

    def ph(self, P, N, dtype):
        if self.ph_off is None:
            self.ph_off = self.off
        t, self.ph_off = self._carve(P, N, dtype, self.ph_off)
        return t

    def psum(self, bank, nbanks=1):
        assert bank + nbanks <= 7
        ap = self.psA[:, 512 * bank:512 * (bank + nbanks)]
        return Tile(ap, "P", 0, 512 * 4 * bank, F32, 128, 512 * nbanks)

    def psumT(self):
        return Tile(self.psT[:, :], "T", 0, 0, BF16, 128, 1024)

    @staticmethod
    def _buckets(reg):
        if reg.space in ("P", "T"):
            lo = reg.b0 // 2048
            hi = (reg.b1 - 1) // 2048
            return [(reg.space, i) for i in range(lo, hi + 1)]
        if reg.space == "S":
            lo = reg.b0 // BUCKET
            hi = (reg.b1 - 1) // BUCKET
            return [("S", i) for i in range(lo, hi + 1)]
        return [(reg.space, 0)]

    @staticmethod
    def _overlap(a, b):
        if a.space != b.space:
            return False
        if a.space in ("P", "T"):
            return (a.b0 // 2048 <= (b.b1 - 1) // 2048
                    and b.b0 // 2048 <= (a.b1 - 1) // 2048)
        return a.p0 < b.p1 and b.p0 < a.p1 and a.b0 < b.b1 and b.b0 < a.b1

    def _conflicts(self, reg, kind, waits, self_stream):
        for bk in self._buckets(reg):
            for (r2, k2, s2, v2) in self.acc.get(bk, ()):
                if k2 == "r" and kind == "r":
                    continue
                if s2 == self_stream:
                    continue
                if self._overlap(reg, r2):
                    if waits.get(s2, 0) < v2:
                        waits[s2] = v2

    def _record(self, reg, kind, stream, val):
        for bk in self._buckets(reg):
            lst = self.acc.setdefault(bk, [])
            if kind == "w":
                lst[:] = [a for a in lst if not (
                    reg.space == a[0].space and reg.p0 <= a[0].p0 and reg.p1 >= a[0].p1
                    and reg.b0 <= a[0].b0 and reg.b1 >= a[0].b1)]
            lst.append((reg, kind, stream, val))
            if len(lst) > 600:
                mx = {}
                for (r2, k2, s2, v2) in lst:
                    mx[s2] = max(mx.get(s2, 0), v2)
                sp = lst[0][0].space
                lst[:] = [(Reg(sp, 0, 128, 0, 1 << 62), "w", s2, v2)
                          for s2, v2 in mx.items()]

    @staticmethod
    def _regs(lst):
        out = []
        for x in (lst or ()):
            if isinstance(x, Tile):
                out.append(x.reg())
            elif isinstance(x, tuple):
                out.append(x[1])
            else:
                out.append(x)
        return out

    def t(self, eng, emit, r=None, w=None, inc=1, tag="", rs=None):
        rr, ww = self._regs(r), self._regs(w)
        waits = {}
        skip = "pe" if (eng == "pe" and self.same_engine_sync) else (
            None if self.same_engine_sync else eng)
        for reg in rr:
            self._conflicts(reg, "r", waits, skip)
        for reg in ww:
            self._conflicts(reg, "w", waits, skip)
        strict = self._regs(rs) if rs else []
        for reg in strict:
            self._conflicts(reg, "r", waits, None)
        if strict:
            rr = rr + strict
        if eng in self.serial:
            for s2 in self.count:
                if s2 != eng and self.count[s2] > 0:
                    waits[s2] = self.count[s2]
        need = []
        for s, v in waits.items():
            if self.vc[eng].get(s, 0) < v:
                need.append((s, v))
                self.vc[eng][s] = v
        self.count[eng] += inc
        val = self.count[eng]
        for reg in rr:
            self._record(reg, "r", eng, val)
        for reg in ww:
            self._record(reg, "w", eng, val)
        sem = self.sem
        myname = eng
        self.n_waits += len(need)
        self.n_tasks += 1
        import sys as _sys
        fr = _sys._getframe(1)
        loc = f"{fr.f_code.co_name}:{fr.f_lineno}"
        fr2 = fr.f_back
        if fr2 is not None:
            loc += f"<{fr2.f_code.co_name}:{fr2.f_lineno}"

        def run(engine, need=need, emit=emit, val=val, inc=inc, tag=tag, loc=loc):
            for s, v in need:
                engine.wait_ge(sem[s], v)
            try:
                ins = emit(engine)
            except Exception as ex:
                raise RuntimeError(f"task emit failed at {loc} tag={tag}: {ex}") from ex
            ins.then_inc(sem[myname], inc)
        self.tasks[eng].append(run)
        return val

    def dma(self, q, out, in_, r=None, w=None, tag=""):
        qname = "sq" if q == "sp" else "pq"
        eng = "sp" if q == "sp" else "pool"
        lane = self.qidx[qname] % self.NLANE
        self.qidx[qname] += 1
        qs = qname + str(lane)

        def ap_of(x):
            if isinstance(x, Tile):
                return x.ap
            if isinstance(x, tuple):
                return x[0]
            return x
        out_ap, in_ap = ap_of(out), ap_of(in_)
        rr = self._regs([in_] if not isinstance(in_, list) else in_)
        ww = self._regs([out] if not isinstance(out, list) else out)
        waits = {}
        for reg in rr:
            self._conflicts(reg, "r", waits, None)
        for reg in ww:
            self._conflicts(reg, "w", waits, None)
        if ("dma" in self.serial) or (eng in self.serial):
            for s2 in self.count:
                if s2 != qs and self.count[s2] > 0:
                    waits[s2] = self.count[s2]
        if self.count[qs] > waits.get(qs, 0):
            waits[qs] = self.count[qs]
        need = []
        for s, v in waits.items():
            if self.vc[eng].get(s, 0) < v:
                need.append((s, v))
                self.vc[eng][s] = v
        self.count[qs] += 16
        val = self.count[qs]
        for reg in rr:
            self._record(reg, "r", qs, val)
        for reg in ww:
            self._record(reg, "w", qs, val)
        sem = self.sem
        self.n_tasks += 1
        self.n_waits += len(need)

        def run(engine, need=need, val=val):
            for s, v in need:
                engine.wait_ge(sem[s], v)
            engine.dma_start(out=out_ap, in_=in_ap).then_inc(sem[qs], 16)
        self.tasks[eng].append(run)
        return val

    def finish(self, block):
        fin = [(s, self.count[s]) for s in self.count if self.count[s] > 0]

        def last(engine):
            for s, v in fin:
                if s != "sp":
                    engine.wait_ge(self.sem[s], v)
        self.tasks["sp"].append(last)
        engmap = {"pe": block.tensor, "act": block.scalar, "dve": block.vector,
                  "pool": block.gpsimd, "sp": block.sync}
        for eng in self.ENGS:
            tl = self.tasks[eng]
            if not tl:
                continue

            def body(engine, tl=tl):
                for fn in tl:
                    fn(engine)
            engmap[eng](body)


_VEC_REG = []


def _vec_layout():
    if _VEC_REG:
        return _VEC_REG
    def add(name, ncols):
        _VEC_REG.append((name, ncols))
    add("l0_ba", 8); add("l0_bx", 8); add("l0_lmbd", 8)
    add("l0f_b1", 32); add("l0f_b2", 8); add("l0f_gamma", 8); add("l0f_beta", 8)
    add("l1a_bout", 8)
    for l in range(2):
        add(f"l1f_lng{l}", 8); add(f"l1f_lnb{l}", 8)
        add(f"l1f_b1a{l}", 22); add(f"l1f_b1g{l}", 22); add(f"l1f_b2_{l}", 8)
        add(f"l1f_qs{l}", 1); add(f"l1f_qss{l}", 1)
        add(f"l1f_ks{l}", 1); add(f"l1f_kss{l}", 1)
    add("l2a_bck", 1); add("l2a_bcv", 1); add("l2a_bg", 1)
    add("l2f_gateb", 1)
    for e in range(8):
        add(f"l2f_b1_{e}", 16); add(f"l2f_b2_{e}", 16)
        add(f"l2f_bg_{e}", 16); add(f"l2f_b3_{e}", 8)
    return _VEC_REG


def _vec_cols():
    lay = _vec_layout()
    off, out = 0, {}
    for name, n in lay:
        out[name] = (off, n)
        off += n
    return out, off


def _col128(v):
    v = np.asarray(v, np.float32).reshape(-1)
    assert v.size % 128 == 0 or v.size <= 128, v.size
    if v.size % 128:
        v = np.pad(v, (0, 128 - v.size % 128))
    return v.reshape(-1, 128).T


def _pad_col(v, n=128):
    v = np.asarray(v, np.float32).reshape(-1)
    return np.pad(v, (0, n - v.size))[:, None]


def stage_vecs(I):
    cols, NV = _vec_cols()
    V = np.zeros((128, NV), np.float32)

    def put(name, arr):
        o, n = cols[name]
        arr = np.asarray(arr, np.float32)
        assert arr.shape == (128, n) or (n == 1 and arr.shape == (128, 1)), (name, arr.shape)
        V[:, o:o + n] = arr
    put("l0_ba", _col128(I["L0_attn_ba"]))
    put("l0_bx", _col128(I["L0_attn_bx"]))
    put("l0_lmbd", _col128(I["L0_attn_lmbd"]))
    put("l0f_b1", _col128(I["L0_ffn_b1"][0]))
    put("l0f_b2", _col128(I["L0_ffn_b2"][0]))
    put("l0f_gamma", _col128(I["L0_ffn_gamma"]))
    put("l0f_beta", _col128(I["L0_ffn_beta"]))
    put("l1a_bout", _col128(I["L1_attn_bout"]))
    HID = 2730
    for l in range(2):
        put(f"l1f_lng{l}", _col128(I["L1_ffn_ln_g"][l]))
        put(f"l1f_lnb{l}", _col128(I["L1_ffn_ln_b"][l]))
        b1 = np.asarray(I["L1_ffn_b1"][l], np.float32)
        b1a = np.pad(b1[:HID], (0, 22 * 128 - HID))
        b1g = np.pad(b1[HID:], (0, 22 * 128 - HID))
        put(f"l1f_b1a{l}", b1a.reshape(22, 128).T)
        put(f"l1f_b1g{l}", b1g.reshape(22, 128).T)
        put(f"l1f_b2_{l}", _col128(I["L1_ffn_b2"][l]))
        qs = np.asarray(I["L1_ffn_q_scale"][l], np.float32)
        ks = np.asarray(I["L1_ffn_k_scale"][l], np.float32)
        sig = np.concatenate([qs[32:], qs[:32]])
        sigk = np.concatenate([ks[32:], ks[:32]])
        put(f"l1f_qs{l}", _pad_col(np.tile(qs, 2)))
        put(f"l1f_qss{l}", _pad_col(np.tile(sig, 2)))
        put(f"l1f_ks{l}", _pad_col(np.tile(ks, 2)))
        put(f"l1f_kss{l}", _pad_col(np.tile(sigk, 2)))
    put("l2a_bck", _pad_col(I["L2_attn_bck"]))
    put("l2a_bcv", _pad_col(I["L2_attn_bcv"]))
    put("l2a_bg", _pad_col(I["L2_attn_bg"]))
    put("l2f_gateb", _pad_col(I["L2_ffn_gate_b"]))
    for e in range(8):
        put(f"l2f_b1_{e}", _col128(I["L2_ffn_b1"][e]))
        put(f"l2f_b2_{e}", _col128(I["L2_ffn_b2"][e]))
        put(f"l2f_bg_{e}", _col128(I["L2_ffn_bg"][e]))
        put(f"l2f_b3_{e}", _col128(I["L2_ffn_b3"][e]))
    return np.ascontiguousarray(V)


def stage_toeplitz(filt):
    f = np.asarray(filt, np.float32)
    idx = np.arange(T)[:, None] - np.arange(T)[None, :]
    Tm = np.where((idx >= 0) & (idx < f.size), f[np.clip(idx, 0, f.size - 1)], 0.0)
    return np.ascontiguousarray(Tm.astype(np.float32))


def stage_inputs(I, core):
    m = {}
    x = np.asarray(I["L0_attn_x"], np.float32)[2 * core:2 * core + 2]
    m["x_fm"] = np.ascontiguousarray(x.transpose(0, 2, 1))
    m["vecs"] = stage_vecs(I)
    m["wa_t"] = np.ascontiguousarray(np.asarray(I["L0_attn_Wa"], np.float32).T)
    m["wx_t"] = np.ascontiguousarray(np.asarray(I["L0_attn_Wx"], np.float32).T)
    m["l0f_w1"] = np.ascontiguousarray(np.asarray(I["L0_ffn_w1"], np.float32)[0])
    m["l0f_w2"] = np.ascontiguousarray(np.asarray(I["L0_ffn_w2"], np.float32)[0])
    m["l1a_wq"] = np.asarray(I["L1_attn_Wq"], np.float32)
    m["l1a_wkv"] = np.asarray(I["L1_attn_Wkv"], np.float32)
    m["l1a_wout"] = np.asarray(I["L1_attn_Wout"], np.float32)
    m["l1f_wqkv"] = np.asarray(I["L1_ffn_wqkv"], np.float32)
    m["l1f_wo"] = np.asarray(I["L1_ffn_wo"], np.float32)
    m["l1f_w1"] = np.asarray(I["L1_ffn_w1"], np.float32)
    m["l1f_w2"] = np.asarray(I["L1_ffn_w2"], np.float32)
    m["l2a_wq"] = np.asarray(I["L2_attn_Wq"], np.float32)
    m["l2a_wk"] = np.asarray(I["L2_attn_Wk"], np.float32)
    m["l2a_wv"] = np.asarray(I["L2_attn_Wv"], np.float32)
    m["l2a_wo"] = np.asarray(I["L2_attn_Wo"], np.float32)
    m["l2a_wck"] = np.asarray(I["L2_attn_Wck"], np.float32)
    m["l2a_wcv"] = np.asarray(I["L2_attn_Wcv"], np.float32)
    m["l2a_wg"] = np.ascontiguousarray(
        np.pad(np.asarray(I["L2_attn_Wg"], np.float32), ((0, 0), (0, 1))))
    m["l2f_gw"] = np.asarray(I["L2_ffn_gate_w"], np.float32)
    m["l2f_w1"] = np.asarray(I["L2_ffn_w1"], np.float32)
    m["l2f_w2"] = np.asarray(I["L2_ffn_w2"], np.float32)
    m["l2f_wg"] = np.asarray(I["L2_ffn_wg"], np.float32)
    m["l2f_w3"] = np.asarray(I["L2_ffn_w3"], np.float32)
    m["l3_toep"] = stage_toeplitz(I["L3_attn_filt"])
    m["l3_wgate"] = np.asarray(I["L3_ffn_w_gate"], np.float32)
    m["l3_wup"] = np.asarray(I["L3_ffn_w_up"], np.float32)
    m["l3_wdown"] = np.asarray(I["L3_ffn_w_down"], np.float32)
    return m


def _rot_tables():
    DH, WIN = 64, 512
    half = np.arange(0, DH, 2, dtype=np.float64)
    inv_freq = 1.0 / 10000.0 ** (half / DH)
    t = np.arange(T, dtype=np.float64)
    freqs = t[:, None] * inv_freq[None, :]
    freqs = np.concatenate([freqs, freqs], axis=1)
    base = (half + 0.4 * DH) / (1.4 * DH)
    power = (t - T // 2) / float(WIN // 2)
    scale = base[None, :] ** power[:, None]
    scale = np.concatenate([scale, scale], axis=1)
    cos, sin = np.cos(freqs), np.sin(freqs)
    cs_q = (cos * scale).T.astype(np.float32)
    ss_q = (sin * scale).T.astype(np.float32)
    cs_k = (cos / scale).T.astype(np.float32)
    ss_k = (sin / scale).T.astype(np.float32)
    tile2 = lambda a: np.ascontiguousarray(np.tile(a, (2, 1)))
    return tile2(cs_q), tile2(ss_q), tile2(cs_k), tile2(ss_k)


def _rot_perm():
    R = np.zeros((128, 128), np.float32)
    for h in range(2):
        o = 64 * h
        for i in range(32):
            R[o + 32 + i, o + i] = -1.0
            R[o + i, o + 32 + i] = 1.0
    return np.ascontiguousarray(R)


def _l1_masksT():
    j = np.arange(1024)[:, None]
    i = np.arange(512)[None, :]
    ok = (j >= i) & (j <= i + 512)
    m1 = np.where(ok, 0.0, -1e38).astype(np.float32)
    m0 = np.where(ok & (j >= 512), 0.0, -1e38).astype(np.float32)
    return m0, m1


def _l2_E():
    E = np.zeros((127, T), np.float32)
    for n in range(127):
        for c in range(64):
            p = min(16 * n + c, T - 1)
            E[n, p] += 1.0
    return E


class Prog:
    def __init__(self, nc, S, n_blocks=4):
        self.nc = nc
        self.n_blocks = n_blocks
        self.B = MK(nc, S)
        self.cols, NV = _vec_cols()
        d = nc.dram_tensor
        self.inp = {}
        for name, shape in [
            ("x_fm", [B, D, T]), ("vecs", [128, NV]),
            ("wa_t", [D, D]), ("wx_t", [D, D]),
            ("l0f_w1", [D, 4096]), ("l0f_w2", [4096, D]),
            ("l1a_wq", [D, D]), ("l1a_wkv", [D, D]), ("l1a_wout", [D, D]),
            ("l1f_wqkv", [2, D, 1536]), ("l1f_wo", [2, 512, D]),
            ("l1f_w1", [2, D, 5460]), ("l1f_w2", [2, 2730, D]),
            ("l2a_wq", [D, D]), ("l2a_wk", [D, D]), ("l2a_wv", [D, D]),
            ("l2a_wo", [D, D]), ("l2a_wck", [2048, 64]), ("l2a_wcv", [2048, 64]),
            ("l2a_wg", [64, 4]),
            ("l2f_gw", [D, 8]), ("l2f_w1", [8, D, 2048]), ("l2f_w2", [8, D, 2048]),
            ("l2f_wg", [8, 2048, 2048]), ("l2f_w3", [8, 2048, D]),
            ("l3_toep", [T, T]), ("l3_wgate", [2816, D]), ("l3_wup", [2816, D]),
            ("l3_wdown", [D, 2816]),
        ]:
            self.inp[name] = d(name, shape, F32, kind="ExternalInput").ap()
        self.xout = d("xout", [B, D, T], F32, kind="ExternalOutput").ap()
        self.X = d("Xs", [B, DT, 128, T], F32, kind="Internal").ap()
        self.XNB = d("XNBs", [B, DT, 128, T], BF16, kind="Internal").ap()
        self.XN32 = d("XN32s", [B, DT, 128, T], F32, kind="Internal").ap()
        self.W3Q = d("W3Q", [3, 2816 * D], BF16, kind="Internal").ap()
        self.ZS = d("ZSs", [B, DT, 128, T], F32, kind="Internal").ap()
        it = nc.inline_tensor
        self.c_identb = it(np.eye(128, dtype=BF), "c_identb").ap()
        self.c_identf = it(np.eye(128, dtype=np.float32), "c_identf").ap()
        self.c_ones_col_b = it(np.ones((128, 1), dtype=BF), "c_ones_col_b").ap()
        self.c_ones_row_f = it(np.ones((1, 128), dtype=np.float32), "c_ones_row_f").ap()
        self.c_ones_row_b = it(np.ones((1, 128), dtype=BF), "c_ones_row_b").ap()
        self.c_rperm = it(_rot_perm().astype(BF), "c_rperm").ap()
        self._lazy = {}
        Bld = self.B
        self.vt = Bld.carve(128, NV, F32)
        self.lsg8 = Bld.carve(128, 8, F32)
        self.identb = Bld.carve(128, 128, BF16)
        self.identf = Bld.carve(128, 128, F32)
        self.ones_col_b = Bld.carve(128, 1, BF16)
        self.ones_row_f = Bld.carve(1, 128, F32)
        self.ones_row_b = Bld.carve(1, 128, BF16)
        self.rperm = Bld.carve(128, 128, BF16)
        self.cst = Bld.carve(128, 32, F32)
        self.g64b = Bld.carve(128, 2, BF16)
        self.g64rep = Bld.carve(2, 128, F32)
        self.cmap = {}

    def kf(self, v):
        v = float(v)
        if v not in self.cmap:
            col = len(self.cmap)
            assert col < 32
            self.cmap[v] = col
            ap = self.cst.ap[:, col:col + 1]
            self.nc.const_aps.register(F32, v, ap)
            reg = self.cst.reg(0, 128, col, col + 1)
            self.B.t("dve", lambda e, ap=ap, v=v: e.memset(ap, v), w=[reg])
        col = self.cmap[v]
        return self.cst.reg(0, 128, col, col + 1)

    def lazy_const(self, name, fn):
        if name not in self._lazy:
            self._lazy[name] = self.nc.inline_tensor(fn(), name).ap()
        return self._lazy[name]

    def dbg(self, name, tile, P_=128, N=None, dt=F32):
        if isinstance(tile, Tile):
            ap, reg = tile.full
            P_, N, dt = tile.P, tile.N, tile.dtype
        else:
            ap, reg = tile
        o = self.nc.dram_tensor("dbg_" + name, [P_, N], dt, kind="ExternalOutput").ap()
        self.B.dma("sp", (o, dreg("dbg_" + name)), (ap, reg))

    def vcol(self, name, j0=0, n=1, P=128):
        o, nc_ = self.cols[name]
        assert j0 + n <= nc_, (name, j0, n)
        return self.vt[0:P, o + j0:o + j0 + n]

    @staticmethod
    def _tl(x):
        return x.full if isinstance(x, Tile) else x

    def tt(self, op, out, a, b, eng="dve"):
        out, a, b = self._tl(out), self._tl(a), self._tl(b)
        self.B.t(eng, lambda e: e.tensor_tensor(out[0], a[0], b[0], op=op),
                 r=[a, b], w=[out])

    def ts(self, out, a, s1, s2=None, op0=ALU.mult, op1=ALU.bypass, eng="dve"):
        out, a = self._tl(out), self._tl(a)
        s1, s2 = self._tl(s1), self._tl(s2)
        rr = [a]
        rss = []
        s1v = s1[0] if isinstance(s1, tuple) else s1
        s2v = s2[0] if isinstance(s2, tuple) else s2
        if isinstance(s1, tuple):
            rss.append(s1)
        if isinstance(s2, tuple):
            rss.append(s2)
        if s2 is None:
            self.B.t(eng, lambda e: e.tensor_single_scalar(out[0], a[0], s1v, op=op0),
                     r=rr, w=[out], rs=rss)
        else:
            self.B.t(eng, lambda e: e.tensor_scalar(out[0], a[0], s1v, s2v,
                                                    op0=op0, op1=op1),
                     r=rr, w=[out], rs=rss)

    def stt(self, out, in0, scalar, in1, op0, op1, eng="dve"):
        out, in0, in1, scalar = self._tl(out), self._tl(in0), self._tl(in1), self._tl(scalar)
        rr = [in0, in1]
        rss = []
        sv = scalar[0] if isinstance(scalar, tuple) else scalar
        if isinstance(scalar, tuple):
            rss.append(scalar)
        self.B.t(eng, lambda e: e.scalar_tensor_tensor(out[0], in0[0], sv, in1[0],
                                                       op0=op0, op1=op1),
                 r=rr, w=[out], rs=rss)

    def copy(self, out, a, eng="dve"):
        out, a = self._tl(out), self._tl(a)
        self.B.t(eng, lambda e: e.tensor_copy(out[0], a[0]), r=[a], w=[out])

    def act(self, out, a, func, bias=0.0, scale=1.0, accum=None, eng="act"):
        out, a, accum = self._tl(out), self._tl(a), self._tl(accum)
        bias, scale = self._tl(bias), self._tl(scale)
        rr = [a]
        rss = []
        bv = bias[0] if isinstance(bias, tuple) else bias
        sv = scale[0] if isinstance(scale, tuple) else scale
        if isinstance(bias, tuple):
            rss.append(bias)
        elif isinstance(bias, float) and bias != 0.0:
            rss.append(self.kf(bias))
        if isinstance(scale, tuple):
            rss.append(scale)
        ww = [out]
        kw = {}
        if accum is not None:
            kw["accum_out"] = accum[0]
            ww = [out, accum]
        self.B.t(eng, lambda e: e.activation(out[0], a[0], func, bias=bv, scale=sv, **kw),
                 r=rr, w=ww, rs=rss)

    def red(self, out, a, op=ALU.max, absval=False, eng="dve"):
        out, a = self._tl(out), self._tl(a)
        self.B.t(eng, lambda e: e.tensor_reduce(out[0], a[0], axis=AX.X, op=op,
                                                apply_absolute_value=absval),
                 r=[a], w=[out])

    def memset(self, out, v, eng="dve"):
        out = self._tl(out)
        self.B.t(eng, lambda e: e.memset(out[0], v), w=[out])

    def mm(self, ps, pairs, tag="", start=True, stop=True):
        rr = []
        for lt, rt in pairs:
            rr += [lt, rt]

        def emit(e):
            n = len(pairs)
            for k, (lt, rt) in enumerate(pairs):
                ins = e.matmul(ps[0], lt[0], rt[0], start=(start and k == 0),
                               stop=(stop and k == n - 1))
            return ins
        self.B.t("pe", emit, r=rr, w=[ps], tag=tag)

    def transpose(self, ps_out, a, bf16=True, kp=128, base=0):
        idt = self.identb if bf16 else self.identf
        ident = idt[base:base + kp, base:base + kp]
        self.B.t("pe", lambda e: e.transpose(ps_out[0], a[0], ident[0]),
                 r=[a, ident], w=[ps_out])

    def dsl(self, dr, name, b, dt, c0=0, c1=T, esz=4):
        slab = (b * DT + dt)
        ap = dr[b, dt, :, c0:c1]
        return ap, Reg("D:" + name, 0, 128, slab * 128 * T * esz, (slab + 1) * 128 * T * esz)

    def load_consts(self):
        Bd = self.B
        Bd.dma("sp", self.vt, (self.inp["vecs"], dreg("vecs")))
        Bd.dma("sp", self.identb, (self.c_identb, dreg("c_identb")))
        Bd.dma("sp", self.identf, (self.c_identf, dreg("c_identf")))
        Bd.dma("sp", self.ones_col_b, (self.c_ones_col_b, dreg("c_ones_col_b")))
        Bd.dma("sp", self.ones_row_f, (self.c_ones_row_f, dreg("c_ones_row_f")))
        Bd.dma("sp", self.ones_row_b, (self.c_ones_row_b, dreg("c_ones_row_b")))
        Bd.dma("sp", self.rperm, (self.c_rperm, dreg("c_rperm")))
        g64np = np.zeros((128, 2), np.float32)
        g64np[0:64, 0] = 1.0
        g64np[64:128, 1] = 1.0
        Bd.dma("sp", self.g64b, (self.nc.inline_tensor(g64np.astype(BF), "c_g64b").ap(),
                                 dreg("c_g64b")))
        Bd.dma("sp", self.g64rep, (self.nc.inline_tensor(
            np.ascontiguousarray(g64np.T), "c_g64rep").ap(), dreg("c_g64rep")))
        for b in range(B):
            for dt in range(DT):
                Bd.dma("sp", self.dsl(self.X, "Xs", b, dt),
                       (self.inp["x_fm"][b, 128 * dt:128 * (dt + 1), :], dreg("x_fm")))
        t1 = Bd.carve(128, 8, F32)
        self.act(t1.full, self.vcol("l0_lmbd", 0, 8), AF.Sigmoid)
        self.act(t1.full, t1.full, AF.Ln)
        self.ts(self.lsg8.full, t1.full, 8.0, op0=ALU.mult)

    def ln_entry(self, b):
        P, Bd = self, self.B
        Bd.ph_reset()
        xt = [Bd.ph(128, T, F32) for _ in range(DT)]
        xnb = [Bd.ph(128, T, BF16) for _ in range(DT)]
        sqh = [Bd.ph(128, 1024, BF16) for _ in range(DT)]
        mean = Bd.ph(1, T, F32)
        e2 = Bd.ph(1, T, F32)
        tmp = Bd.ph(1, T, F32)
        inv = Bd.ph(1, T, F32)
        msr = Bd.ph(1, T, F32)
        t1 = Bd.ph(128, 512, F32)
        for dt in range(DT):
            Bd.dma("sp", xt[dt], self.dsl(self.X, "Xs", b, dt))
            self.copy(xnb[dt].full, xt[dt].full)
        onesb = self.ones_col_b.full
        for h in range(2):
            hc = 1024 * h
            for dt in range(DT):
                self.tt(ALU.mult, sqh[dt].full, xnb[dt][:, hc:hc + 1024],
                        xnb[dt][:, hc:hc + 1024])
            for c in range(2):
                c0, cg = hc + 512 * c, 512 * c
                ps_s, ps_q = Bd.psum(c % 2), Bd.psum(2 + c % 2)
                P.mm((ps_s.ap[0:1, :], ps_s.reg()),
                     [(onesb, xnb[dt][:, c0:c0 + 512]) for dt in range(DT)])
                P.mm((ps_q.ap[0:1, :], ps_q.reg()),
                     [(onesb, sqh[dt][:, cg:cg + 512]) for dt in range(DT)])
                self.ts((mean.ap[:, c0:c0 + 512], mean.reg(0, 1, c0, c0 + 512)),
                        (ps_s.ap[0:1, :], ps_s.reg()), 1.0 / D, op0=ALU.mult)
                self.ts((e2.ap[:, c0:c0 + 512], e2.reg(0, 1, c0, c0 + 512)),
                        (ps_q.ap[0:1, :], ps_q.reg()), 1.0 / D, op0=ALU.mult)
        self.tt(ALU.mult, tmp.full, mean.full, mean.full)
        self.tt(ALU.subtract, tmp.full, e2.full, tmp.full)
        self.act(inv.full, tmp.full, AF.Ln, bias=1e-5)
        self.act(inv.full, inv.full, AF.Exp, scale=-0.5)
        self.stt(msr.full, mean.full, -1.0, inv.full, op0=ALU.mult, op1=ALU.mult)
        for c in range(4):
            c0 = 512 * c
            psA_ = Bd.psum(4)
            psB_ = Bd.psum(5)
            P.mm(psA_.full, [((self.ones_row_f.ap, self.ones_row_f.reg()),
                              (inv.ap[:, c0:c0 + 512], inv.reg(0, 1, c0, c0 + 512)))])
            P.mm(psB_.full, [((self.ones_row_f.ap, self.ones_row_f.reg()),
                              (msr.ap[:, c0:c0 + 512], msr.reg(0, 1, c0, c0 + 512)))])
            for dt in range(DT):
                xc = xt[dt][:, c0:c0 + 512]
                self.tt(ALU.mult, t1.full, xc, psA_.full)
                self.tt(ALU.add, xc, t1.full, psB_.full)
                self.act(xnb[dt][:, c0:c0 + 512], xc, AF.Copy)
        for dt in range(DT):
            Bd.dma("sp", self.dsl(self.XN32, "XN32s", b, dt), xt[dt])
            Bd.dma("sp", self.dsl(self.XNB, "XNBs", b, dt, esz=2), xnb[dt])
        if DBG and b == 0:
            self.dbg("ln_mean", mean)
            self.dbg("ln_e2", e2)
            self.dbg("ln_inv", inv)
            self.dbg("ln_msr", msr)

    def l0_attn(self, b):
        P, Bd = self, self.B
        Bd.ph_reset()
        xnb = [Bd.ph(128, T, BF16) for _ in range(DT)]
        xn32 = Bd.ph(128, T, F32)
        wa = Bd.ph(128, 8192, BF16)
        wx = Bd.ph(128, 8192, BF16)
        rbuf = Bd.ph(128, T, F32)
        ibuf = Bd.ph(128, T, F32)
        abuf = Bd.ph(128, T, F32)
        bbuf = Bd.ph(128, T, F32)
        ytil = Bd.ph(128, T, F32)
        xres = Bd.ph(128, T, F32)
        for dt in range(DT):
            Bd.dma("sp", xnb[dt], self.dsl(self.XNB, "XNBs", b, dt, esz=2))
        for k in range(DT):
            Bd.dma("pool", wa[:, 1024 * k:1024 * (k + 1)],
                   (self.inp["wa_t"][128 * k:128 * (k + 1), :], dreg("wa_t")))
            Bd.dma("pool", wx[:, 1024 * k:1024 * (k + 1)],
                   (self.inp["wx_t"][128 * k:128 * (k + 1), :], dreg("wx_t")))
        for dt in range(DT):
            Bd.dma("sp", xn32, self.dsl(self.XN32, "XN32s", b, dt))
            Bd.dma("sp", xres, self.dsl(self.X, "Xs", b, dt))
            for c in range(4):
                c0 = 512 * c
                ps_r, ps_i = Bd.psum(c % 2), Bd.psum(2 + c % 2)
                P.mm(ps_r.full, [(wa[:, 1024 * k + 128 * dt:1024 * k + 128 * (dt + 1)],
                                  xnb[k][:, c0:c0 + 512]) for k in range(DT)])
                P.mm(ps_i.full, [(wx[:, 1024 * k + 128 * dt:1024 * k + 128 * (dt + 1)],
                                  xnb[k][:, c0:c0 + 512]) for k in range(DT)])
                self.act(rbuf[:, c0:c0 + 512], ps_r.full, AF.Sigmoid,
                         bias=self.vcol("l0_ba", dt))
                self.act(ibuf[:, c0:c0 + 512], ps_i.full, AF.Sigmoid,
                         bias=self.vcol("l0_bx", dt))
            self.act(abuf.full, rbuf.full, AF.Exp, scale=self.lsg8[:, dt:dt + 1])
            self.tt(ALU.mult, bbuf.full, abuf.full, abuf.full)
            self.act(bbuf.full, bbuf.full, AF.Ln, bias=1.0, scale=-1.0)
            self.act(bbuf.full, bbuf.full, AF.Exp, scale=0.5)
            self.tt(ALU.mult, ibuf.full, ibuf.full, (xn32.ap, xn32.reg()))
            self.tt(ALU.mult, bbuf.full, bbuf.full, ibuf.full)
            Bd.t("dve", lambda e, o=ytil, a_=abuf, b_=bbuf: e.tensor_tensor_scan(
                o.ap, a_.ap, b_.ap, 0.0, op0=ALU.mult, op1=ALU.add),
                r=[abuf.full, bbuf.full], w=[ytil.full])
            self.tt(ALU.add, xres.full, xres.full, ytil.full)
            Bd.dma("sp", self.dsl(self.X, "Xs", b, dt), xres)
        if DBG and b == 0:
            self.dbg("r7", rbuf)
            self.dbg("a7", abuf)
            self.dbg("b7", bbuf)
            self.dbg("y7", ytil)
            self.dbg("xattn", xres)

    def mm_proj(self, w_sb, src_tiles, out_write, nk=DT, nchunk=4, csz=512):
        pass

    def l1_attn(self, b):
        P, Bd = self, self.B
        Bd.ph_reset()
        qf = [Bd.ph(128, T, BF16) for _ in range(DT)]
        kf = [Bd.ph(128, T, BF16) for _ in range(DT)]
        vt = [Bd.ph(128, 16 * 65, BF16) for _ in range(16)]
        pbt = [Bd.ph(128, 512, BF16) for _ in range(2)]
        mk = Bd.mark()
        xnb = [Bd.ph(128, T, BF16) for _ in range(DT)]
        wq = Bd.ph(128, 8192, BF16)
        wkv = Bd.ph(128, 8192, BF16)
        for dt in range(DT):
            Bd.dma("sp", xnb[dt], self.dsl(self.XNB, "XNBs", b, dt, esz=2))
        for k in range(DT):
            Bd.dma("pool", wq[:, 1024 * k:1024 * (k + 1)],
                   (self.inp["l1a_wq"][128 * k:128 * (k + 1), :], dreg("l1a_wq")))
            Bd.dma("pool", wkv[:, 1024 * k:1024 * (k + 1)],
                   (self.inp["l1a_wkv"][128 * k:128 * (k + 1), :], dreg("l1a_wkv")))
        for dst, w_sb in ((qf, wq), (kf, wkv)):
            for dt in range(DT):
                for c in range(4):
                    c0 = 512 * c
                    ps = Bd.psum(c % 2 + (0 if dst is qf else 2))
                    P.mm(ps.full, [(w_sb[:, 1024 * k + 128 * dt:1024 * k + 128 * (dt + 1)],
                                    xnb[k][:, c0:c0 + 512]) for k in range(DT)])
                    self.act(dst[dt][:, c0:c0 + 512], ps.full, AF.Copy)
        pst = Bd.psumT()
        for h in range(16):
            t2, o = h // 2, 64 * (h % 2)
            for pt in range(16):
                self.transpose((pst.ap[0:128, 0:64], pst.reg(0, 128, 0, 64)),
                               (kf[t2].ap[o:o + 64, 128 * pt:128 * (pt + 1)],
                                kf[t2].reg(o, o + 64, 128 * pt, 128 * (pt + 1))),
                               bf16=True, kp=64, base=o)
                self.copy((vt[pt].ap[:, 65 * h:65 * h + 64],
                           vt[pt].reg(0, 128, 65 * h, 65 * h + 64)),
                          (pst.ap[0:128, 0:64], pst.reg(0, 128, 0, 64)))
        for pt in range(16):
            self.memset((vt[pt].ap[:, 64:16 * 65:65], vt[pt].reg(0, 128, 0, 16 * 65)), 1.0)
        Bd.reset_to(mk)
        ofm = [Bd.ph(128, T, BF16) for _ in range(DT)]
        wo = Bd.ph(128, 8192, BF16)
        den = Bd.ph(1, 512, F32)
        repd = Bd.ph(64, 512, F32)
        xres = Bd.ph(128, T, F32)
        for h in range(16):
            t2, o = h // 2, 64 * (h % 2)
            for ic in range(4):
                i0 = 512 * ic
                ops = Bd.psum(4 + ic % 2)
                for pt in range(16):
                    ps = Bd.psum(pt % 2)
                    P.mm(ps.full, [((kf[t2].ap[o:o + 64, 128 * pt:128 * (pt + 1)],
                                     kf[t2].reg(o, o + 64, 128 * pt, 128 * (pt + 1))),
                                    (qf[t2].ap[o:o + 64, i0:i0 + 512],
                                     qf[t2].reg(o, o + 64, i0, i0 + 512)))])
                    pb = pbt[pt % 2]
                    if pt % 2 == 0:
                        self.act(pb.full, ps.full, AF.Relu, scale=0.125)
                    else:
                        self.ts(pb.full, ps.full, 0.125, 0.0, op0=ALU.mult,
                                op1=ALU.max)
                    P.mm((ops.ap[0:65, :], ops.reg()),
                         [((vt[pt].ap[:, 65 * h:65 * (h + 1)],
                            vt[pt].reg(0, 128, 65 * h, 65 * (h + 1))), pb.full)],
                         start=(pt == 0), stop=(pt == 15))
                self.act((ofm[t2].ap[o:o + 64, i0:i0 + 512],
                          ofm[t2].reg(o, o + 64, i0, i0 + 512)),
                         (ops.ap[0:64, :], ops.reg()), AF.Copy)
        for k in range(DT):
            Bd.dma("pool", wo[:, 1024 * k:1024 * (k + 1)],
                   (self.inp["l1a_wout"][128 * k:128 * (k + 1), :], dreg("l1a_wout")))
        for dt in range(DT):
            Bd.dma("sp", xres, self.dsl(self.X, "Xs", b, dt))
            for c in range(4):
                c0 = 512 * c
                ps = Bd.psum(6)
                P.mm(ps.full, [(wo[:, 1024 * k + 128 * dt:1024 * k + 128 * (dt + 1)],
                                ofm[k][:, c0:c0 + 512]) for k in range(DT)])
                self.stt(xres[:, c0:c0 + 512], ps.full, self.vcol("l1a_bout", dt),
                         xres[:, c0:c0 + 512], op0=ALU.add, op1=ALU.add)
            Bd.dma("sp", self.dsl(self.X, "Xs", b, dt), xres)

    def l1_ffn(self, b):
        P, Bd = self, self.B
        HID = 2730
        rotc = [self.lazy_const("c_csq", lambda: _rot_tables()[0]),
                self.lazy_const("c_ssq", lambda: _rot_tables()[1]),
                self.lazy_const("c_csk", lambda: _rot_tables()[2]),
                self.lazy_const("c_ssk", lambda: _rot_tables()[3])]
        cm0 = self.lazy_const("c_m01_0", lambda: np.ascontiguousarray(
            (_l1_masksT()[0] == 0).astype(BF)))
        cm1 = self.lazy_const("c_m01_1", lambda: np.ascontiguousarray(
            (_l1_masksT()[1] == 0).astype(BF)))
        for l in range(2):
            Bd.ph_reset()
            tabs = [Bd.ph(128, T, F32) for _ in range(4)]
            qf = [Bd.ph(128, T, BF16) for _ in range(4)]
            kpb = [Bd.ph(128, 2560, BF16) for _ in range(4)]
            vt = [Bd.ph(128, 8 * 65, BF16) for _ in range(16)]
            mka = Bd.mark()
            hsh = [Bd.ph(128, T, BF16) for _ in range(DT)]
            wq3 = Bd.ph(128, DT * 1536, BF16)
            zld = Bd.ph(128, T, F32)
            sqb = Bd.ph(128, T, BF16)
            inv2 = Bd.ph(2, 512, F32)
            t1k = Bd.ph(128, 2048, F32)
            for i, cap in enumerate(rotc):
                Bd.dma("sp", tabs[i], (cap, dreg(f"c_rt{i}")))
                sc = [f"l1f_qs{l}", f"l1f_qss{l}", f"l1f_ks{l}", f"l1f_kss{l}"][i]
                self.ts(tabs[i].full, tabs[i].full, self.vcol(sc), op0=ALU.mult)
            zsrc, zname = (self.XN32, "XN32s") if l == 0 else (self.ZS, "ZSs")
            for dt in range(DT):
                Bd.dma("sp", zld, self.dsl(zsrc, zname, b, dt))
                if dt < 4:
                    self.copy(hsh[dt].full, zld.full)
                else:
                    self.memset(hsh[dt][:, 0:1], 0.0)
                    self.copy(hsh[dt][:, 1:T], zld[:, 0:T - 1])
            for k in range(DT):
                Bd.dma("pool", wq3[:, 1536 * k:1536 * (k + 1)],
                       (self.inp["l1f_wqkv"][l, 128 * k:128 * (k + 1), :],
                        dreg("l1f_wqkv")))
            for ft in range(8):
                dst = qf[ft] if ft < 4 else None
                for c in range(4):
                    c0 = 512 * c
                    ps = Bd.psum(c % 2)
                    P.mm(ps.full, [(wq3[:, 1536 * k + 128 * ft:1536 * k + 128 * (ft + 1)],
                                    hsh[k][:, c0:c0 + 512]) for k in range(DT)])
                    if ft < 4:
                        self.act(qf[ft][:, c0:c0 + 512], ps.full, AF.Copy)
                    else:
                        self.act(kpb[ft - 4][:, 512 + c0:512 + c0 + 512], ps.full, AF.Copy)
            for mt in range(16):
                ps = Bd.psum(2 + mt % 2)
                P.mm(ps.full, [((hsh[k].ap[:, 128 * mt:128 * (mt + 1)],
                                 hsh[k].reg(0, 128, 128 * mt, 128 * (mt + 1))),
                                (wq3.ap[:, 1536 * k + 1024:1536 * (k + 1)],
                                 wq3.reg(0, 128, 1536 * k + 1024, 1536 * (k + 1))))
                               for k in range(DT)])
                self.B.t("dve", lambda e, mt=mt, ps=ps: e.tensor_copy(
                    vt[mt].ap.rearrange("p (h c) -> p h c", c=65)[:, :, 0:64],
                    ps.ap.rearrange("p (h c) -> p h c", c=64)),
                    r=[ps.full], w=[vt[mt].full])
                self.memset((vt[mt].ap[:, 64:8 * 65:65], vt[mt].reg()), 1.0)
            for which in range(2):
                ctab, stab = (tabs[0], tabs[1]) if which == 0 else (tabs[2], tabs[3])
                for t2 in range(4):
                    if which == 0:
                        src = qf[t2]
                        off = 0
                    else:
                        src = kpb[t2]
                        off = 512
                    sl = lambda c0, n=512: (src.ap[:, off + c0:off + c0 + n],
                                            src.reg(0, 128, off + c0, off + c0 + n))
                    self.tt(ALU.mult, sqb[:, 0:T], sl(0, T), sl(0, T))
                    for c in range(4):
                        c0 = 512 * c
                        psn = Bd.psum(4)
                        P.mm((psn.ap[0:2, :], psn.reg()),
                             [(self.g64b.full,
                               (sqb.ap[:, c0:c0 + 512], sqb.reg(0, 128, c0, c0 + 512)))])
                        self.ts(inv2.full, (psn.ap[0:2, :], psn.reg()), 1e-24, op0=ALU.max)
                        self.act(inv2.full, inv2.full, AF.Ln)
                        self.act(inv2.full, inv2.full, AF.Exp, scale=-0.5)
                        psr = Bd.psum(5)
                        P.mm(psr.full, [(self.g64rep.full, inv2.full)])
                        self.tt(ALU.mult, sl(c0), sl(c0), psr.full)
                    for hh in range(2):
                        h0 = 1024 * hh
                        pst = Bd.psum(5, 2)
                        P.mm((pst.ap[:, 0:512], pst.reg(0, 128, 0, 512)),
                             [((self.rperm.ap, self.rperm.reg()), sl(h0, 512))])
                        P.mm((pst.ap[:, 512:1024], pst.reg(0, 128, 512, 1024)),
                             [((self.rperm.ap, self.rperm.reg()), sl(h0 + 512, 512))])
                        t1 = t1k
                        self.tt(ALU.mult, (t1.ap[:, 0:1024], t1.reg(0, 128, 0, 1024)),
                                sl(h0, 1024),
                                (ctab.ap[:, h0:h0 + 1024], ctab.reg(0, 128, h0, h0 + 1024)))
                        self.tt(ALU.mult, (t1.ap[:, 1024:2048], t1.reg(0, 128, 1024, 2048)),
                                (pst.ap[:, 0:1024], pst.reg(0, 128, 0, 1024)),
                                (stab.ap[:, h0:h0 + 1024], stab.reg(0, 128, h0, h0 + 1024)))
                        self.tt(ALU.add, sl(h0, 1024),
                                (t1.ap[:, 0:1024], t1.reg(0, 128, 0, 1024)),
                                (t1.ap[:, 1024:2048], t1.reg(0, 128, 1024, 2048)))
            for t2 in range(4):
                self.memset(kpb[t2][:, 0:512], 0.0)
            Bd.reset_to(mka)
            m01 = [Bd.ph(128, 8 * 512, BF16) for _ in range(2)]
            for v_, cap in ((0, cm0), (1, cm1)):
                Bd.dma("sp", (m01[v_].ap.rearrange("p (j n) -> p j n", j=8),
                              m01[v_].reg()),
                       (cap.rearrange("(j p) n -> p j n", p=128),
                        dreg("c_m01_" + str(v_))))
            pbt = [Bd.ph(128, 512, BF16) for _ in range(2)]
            exv = [Bd.ph(128, 512, BF16) for _ in range(2)]
            ofm = [Bd.ph(128, T, BF16) for _ in range(4)]
            wo4 = Bd.ph(128, 4096, BF16)
            denr = Bd.ph(1, 512, F32)
            reps = Bd.ph(64, 512, F32)
            zres = Bd.ph(128, T, F32)
            for k in range(4):
                Bd.dma("pool", wo4[:, 1024 * k:1024 * (k + 1)],
                       (self.inp["l1f_wo"][l, 128 * k:128 * (k + 1), :], dreg("l1f_wo")))
            for h in range(8):
                t2, o = h // 2, 64 * (h % 2)
                for w in range(4):
                    i0 = 512 * w
                    ops = Bd.psum(2 + w % 2)
                    jts = range(4, 8) if w == 0 else range(8)
                    first = True
                    for jt in jts:
                        jc = 512 * w + 128 * jt
                        ps = Bd.psum(jt % 2)
                        P.mm(ps.full, [((kpb[t2].ap[o:o + 64, jc:jc + 128],
                                         kpb[t2].reg(o, o + 64, jc, jc + 128)),
                                        (qf[t2].ap[o:o + 64, i0:i0 + 512],
                                         qf[t2].reg(o, o + 64, i0, i0 + 512)))])
                        ex = exv[jt % 2]
                        pb = pbt[jt % 2]
                        self.act(ex.full, ps.full, AF.Exp, scale=8.0)
                        mv = m01[0 if w == 0 else 1]
                        self.tt(ALU.mult, pb.full, ex.full,
                                (mv.ap[:, 512 * jt:512 * (jt + 1)],
                                 mv.reg(0, 128, 512 * jt, 512 * (jt + 1))), eng="pool")
                        mt = (512 * (w - 1) + 128 * jt) // 128
                        P.mm((ops.ap[0:65, :], ops.reg()),
                             [((vt[mt].ap[:, 65 * h:65 * (h + 1)],
                                vt[mt].reg(0, 128, 65 * h, 65 * (h + 1))), pb.full)],
                             start=first, stop=(jt == max(jts)))
                        first = False
                    self.B.t("dve", lambda e, ops=ops: e.reciprocal(
                        denr.ap, ops.ap[64:65, :]), r=[ops.full], w=[denr.full])
                    psr = Bd.psum(4)
                    P.mm((psr.ap[0:64, :], psr.reg()),
                         [((self.ones_row_f.ap[0:1, 0:64],
                            self.ones_row_f.reg(0, 1, 0, 64)), denr.full)])
                    self.copy(reps.full, (psr.ap[0:64, :], psr.reg()))
                    self.tt(ALU.mult, (ofm[t2].ap[o:o + 64, i0:i0 + 512],
                                       ofm[t2].reg(o, o + 64, i0, i0 + 512)),
                            (ops.ap[0:64, :], ops.reg()), reps.full)
            for dt in range(DT):
                Bd.dma("sp", zres, self.dsl(zsrc, zname, b, dt))
                for c in range(4):
                    c0 = 512 * c
                    ps = Bd.psum(5 + c % 2)
                    P.mm(ps.full, [(wo4[:, 1024 * k + 128 * dt:1024 * k + 128 * (dt + 1)],
                                    ofm[k][:, c0:c0 + 512]) for k in range(4)])
                    self.tt(ALU.add, zres[:, c0:c0 + 512], zres[:, c0:c0 + 512], ps.full)
                Bd.dma("sp", self.dsl(self.ZS, "ZSs", b, dt), zres)
            Bd.ph_reset()
            h2b = [Bd.ph(128, T, BF16) for _ in range(DT)]
            hnb = [Bd.ph(128, T, BF16) for _ in range(DT)]
            zld2 = Bd.ph(128, T, F32)
            sq2 = Bd.ph(128, 512, BF16)
            mean = Bd.ph(1, T, F32)
            e2 = Bd.ph(1, T, F32)
            inv = Bd.ph(1, T, F32)
            msr = Bd.ph(1, T, F32)
            t1c = Bd.ph(128, 512, F32)
            for dt in range(DT):
                Bd.dma("sp", zld2, self.dsl(self.ZS, "ZSs", b, dt))
                if dt < 4:
                    self.copy(h2b[dt].full, zld2.full)
                else:
                    self.memset(h2b[dt][:, 0:1], 0.0)
                    self.copy(h2b[dt][:, 1:T], zld2[:, 0:T - 1])
            onesb = self.ones_col_b.full
            for c in range(4):
                c0 = 512 * c
                ps_s, ps_q = Bd.psum(c % 2), Bd.psum(2 + c % 2)
                P.mm((ps_s.ap[0:1, :], ps_s.reg()),
                     [(onesb, h2b[dt][:, c0:c0 + 512]) for dt in range(DT)])
                for dt in range(DT):
                    self.tt(ALU.mult, sq2.full, h2b[dt][:, c0:c0 + 512],
                            h2b[dt][:, c0:c0 + 512])
                    P.mm((ps_q.ap[0:1, :], ps_q.reg()), [(onesb, sq2.full)],
                         start=(dt == 0), stop=(dt == DT - 1))
                self.ts((mean.ap[:, c0:c0 + 512], mean.reg(0, 1, c0, c0 + 512)),
                        (ps_s.ap[0:1, :], ps_s.reg()), 1.0 / D, op0=ALU.mult)
                self.ts((e2.ap[:, c0:c0 + 512], e2.reg(0, 1, c0, c0 + 512)),
                        (ps_q.ap[0:1, :], ps_q.reg()), 1.0 / D, op0=ALU.mult)
            self.tt(ALU.mult, inv.full, mean.full, mean.full)
            self.tt(ALU.subtract, inv.full, e2.full, inv.full)
            self.act(inv.full, inv.full, AF.Ln, bias=1e-5)
            self.act(inv.full, inv.full, AF.Exp, scale=-0.5)
            self.stt(msr.full, mean.full, -1.0, inv.full, op0=ALU.mult, op1=ALU.mult)
            for c in range(4):
                c0 = 512 * c
                psA_, psB_ = Bd.psum(4), Bd.psum(5)
                P.mm(psA_.full, [((self.ones_row_f.ap, self.ones_row_f.reg()),
                                  (inv.ap[:, c0:c0 + 512], inv.reg(0, 1, c0, c0 + 512)))])
                P.mm(psB_.full, [((self.ones_row_f.ap, self.ones_row_f.reg()),
                                  (msr.ap[:, c0:c0 + 512], msr.reg(0, 1, c0, c0 + 512)))])
                for dt in range(DT):
                    self.tt(ALU.mult, t1c.full, h2b[dt][:, c0:c0 + 512], psA_.full)
                    self.tt(ALU.add, t1c.full, t1c.full, psB_.full)
                    self.ts(hnb[dt][:, c0:c0 + 512], t1c.full,
                            self.vcol(f"l1f_lng{l}", dt), self.vcol(f"l1f_lnb{l}", dt),
                            op0=ALU.mult, op1=ALU.add)
            mkc = Bd.mark()
            w1b = [Bd.ph(128, 1024, BF16) for _ in range(4)]
            w2s = Bd.ph(128, 22 * 1024, BF16)
            hgg = [Bd.ph(128, 512, BF16) for _ in range(22)]
            gg = Bd.ph(128, 512, F32)
            widths = [128] * 21 + [42]
            for fi in range(22):
                Bd.dma("pool", (w2s.ap[0:widths[fi], 1024 * fi:1024 * (fi + 1)],
                                w2s.reg(0, 128, 1024 * fi, 1024 * (fi + 1))),
                       (self.inp["l1f_w2"][l, 128 * fi:128 * fi + widths[fi], :],
                        dreg("l1f_w2")))
            for c in range(4):
                c0 = 512 * c
                for fi in range(22):
                    wdt = widths[fi]
                    wa_ = w1b[2 * (fi % 2)]
                    wg_ = w1b[2 * (fi % 2) + 1]
                    Bd.dma("pool", (wa_.ap[:, 0:8 * wdt], wa_.reg(0, 128, 0, 8 * wdt)),
                           (self.inp["l1f_w1"].rearrange("l (kt p) m -> l p kt m", p=128)
                            [l, :, :, 128 * fi:128 * fi + wdt], dreg("l1f_w1")))
                    Bd.dma("pool", (wg_.ap[:, 0:8 * wdt], wg_.reg(0, 128, 0, 8 * wdt)),
                           (self.inp["l1f_w1"].rearrange("l (kt p) m -> l p kt m", p=128)
                            [l, :, :, HID + 128 * fi:HID + 128 * fi + wdt], dreg("l1f_w1")))
                    psa = Bd.psum(fi % 2)
                    psg = Bd.psum(2 + fi % 2)
                    P.mm((psa.ap[0:wdt, :], psa.reg()),
                         [((wa_.ap[:, wdt * k:wdt * (k + 1)],
                            wa_.reg(0, 128, wdt * k, wdt * (k + 1))),
                           hnb[k][:, c0:c0 + 512]) for k in range(DT)])
                    P.mm((psg.ap[0:wdt, :], psg.reg()),
                         [((wg_.ap[:, wdt * k:wdt * (k + 1)],
                            wg_.reg(0, 128, wdt * k, wdt * (k + 1))),
                           hnb[k][:, c0:c0 + 512]) for k in range(DT)])
                    self.act((gg.ap[0:wdt, :], gg.reg(0, wdt, 0, 512)),
                             (psg.ap[0:wdt, :], psg.reg()), AF.Gelu,
                             bias=self.vcol(f"l1f_b1g{l}", fi, P=wdt))
                    self.stt(hgg[fi][0:wdt, 0:512],
                             (psa.ap[0:wdt, :], psa.reg()),
                             self.vcol(f"l1f_b1a{l}", fi, P=wdt),
                             (gg.ap[0:wdt, :], gg.reg(0, wdt, 0, 512)),
                             op0=ALU.add, op1=ALU.mult)
                for dt in range(DT):
                    ps = Bd.psum(4 + dt % 2)
                    P.mm(ps.full, [((w2s.ap[0:widths[fi], 1024 * fi + 128 * dt:1024 * fi + 128 * (dt + 1)],
                                     w2s.reg(0, 128, 1024 * fi, 1024 * (fi + 1))),
                                    hgg[fi][0:widths[fi], 0:512]) for fi in range(22)])
                    Bd.dma("sp", (zld2.ap[:, c0:c0 + 512],
                                  zld2.reg(0, 128, c0, c0 + 512)),
                           self.dsl(self.ZS, "ZSs", b, dt, c0, c0 + 512))
                    self.stt(zld2[:, c0:c0 + 512], ps.full, self.vcol(f"l1f_b2_{l}", dt),
                             zld2[:, c0:c0 + 512], op0=ALU.add, op1=ALU.add)
                    Bd.dma("sp", self.dsl(self.ZS, "ZSs", b, dt, c0, c0 + 512),
                           (zld2.ap[:, c0:c0 + 512], zld2.reg(0, 128, c0, c0 + 512)))
        Bd.ph_reset()
        xr = Bd.ph(128, T, F32)
        zr = Bd.ph(128, T, F32)
        for dt in range(DT):
            Bd.dma("sp", xr, self.dsl(self.X, "Xs", b, dt))
            Bd.dma("sp", zr, self.dsl(self.ZS, "ZSs", b, dt))
            self.tt(ALU.add, xr.full, xr.full, zr.full)
            Bd.dma("sp", self.dsl(self.X, "Xs", b, dt), xr)

    def quant_fp4(self, dst, src, tmpa, tmpb, tmpr, tmps, N, P_=128,
                  eng1="dve", eng2="pool"):
        nb = N // 16
        bc = lambda t: t[0].unsqueeze(2).to_broadcast([P_, nb, 16])
        v3 = lambda t: t[0].rearrange("p (b s) -> p b s", s=16)
        self.B.t(eng1, lambda e: e.tensor_reduce(
            tmps[0], v3(src), axis=AX.X, op=ALU.max, apply_absolute_value=True),
            r=[src], w=[tmps])
        self.ts(tmps, tmps, 1.0 / 6.0, 1e-12, op0=ALU.mult, op1=ALU.max, eng=eng1)
        self.B.t("dve", lambda e: e.reciprocal(tmpr[0], tmps[0]), r=[tmps], w=[tmpr])
        self.act(tmpa, src, AF.Abs)
        self.B.t(eng1, lambda e: e.tensor_tensor(v3(tmpa), v3(tmpa), bc(tmpr),
                                                 op=ALU.mult), r=[tmpa, tmpr], w=[tmpa])
        self.ts(tmpb, tmpa, 0.25, None, op0=ALU.is_gt, eng=eng1)
        for k, m in enumerate((0.75, 1.25, 1.75, 2.5, 3.5, 5.0)):
            self.stt(tmpb, tmpa, m, tmpb, op0=ALU.is_gt, op1=ALU.add, eng=eng1)
        self.ts(tmpa, tmpb, -4.0, 0.0, op0=ALU.add, op1=ALU.max, eng=eng1)
        self.tt(ALU.add, tmpa, tmpa, tmpb, eng=eng2)
        self.ts(tmpb, tmpb, -6.0, 0.0, op0=ALU.add, op1=ALU.max, eng=eng1)
        self.stt(tmpb, tmpa, 0.5, tmpb, op0=ALU.mult, op1=ALU.add, eng=eng1)
        self.B.t(eng2, lambda e: e.tensor_tensor(v3(tmpb), v3(tmpb), bc(tmps),
                                                 op=ALU.mult), r=[tmpb, tmps], w=[tmpb])
        self.ts(tmpa, src, 0.0, 2.0, op0=ALU.is_ge, op1=ALU.mult, eng=eng1)
        self.stt(dst, tmpa, -1.0, tmpb, op0=ALU.add, op1=ALU.mult, eng=eng1)

    def l3_prep_weights(self):
        P, Bd = self, self.B
        Bd.ph_reset()
        src = Bd.ph(128, 2816, F32)
        tmpa = Bd.ph(128, 2816, F32)
        tmpb = Bd.ph(128, 2816, F32)
        tmpr = Bd.ph(128, 176, F32)
        tmps = Bd.ph(128, 176, F32)
        dstb = Bd.ph(128, 2816, BF16)
        ev = Bd.ph(128, 128, BF16)
        w3q = lambda wi, a: self.W3Q[wi, :].rearrange("(a p n) -> a p n", a=a, p=128)
        for wi, wname in ((0, "l3_wgate"), (1, "l3_wup")):
            for sl_ in range(22):
                r0 = 128 * sl_
                Bd.dma("sp", (src.ap[:, 0:1024], src.reg(0, 128, 0, 1024)),
                       (self.inp[wname][r0:r0 + 128, :], dreg(wname)))
                self.quant_fp4((dstb.ap[:, 0:1024], dstb.reg(0, 128, 0, 1024)),
                               (src.ap[:, 0:1024], src.reg(0, 128, 0, 1024)),
                               (tmpa.ap[:, 0:1024], tmpa.reg(0, 128, 0, 1024)),
                               (tmpb.ap[:, 0:1024], tmpb.reg(0, 128, 0, 1024)),
                               (tmpr.ap[:, 0:64], tmpr.reg(0, 128, 0, 64)),
                               (tmps.ap[:, 0:64], tmps.reg(0, 128, 0, 64)), 1024)
                for ds in range(8):
                    pst = Bd.psumT()
                    self.transpose((pst.ap[:, 0:128], pst.reg(0, 128, 0, 128)),
                                   (dstb.ap[:, 128 * ds:128 * (ds + 1)],
                                    dstb.reg(0, 128, 128 * ds, 128 * (ds + 1))))
                    self.copy(ev.full, (pst.ap[:, 0:128], pst.reg(0, 128, 0, 128)))
                    Bd.dma("sp", (w3q(wi, 8)[ds, :, r0:r0 + 128], dreg("W3Q")), ev)
        for sl_ in range(8):
            r0 = 128 * sl_
            Bd.dma("sp", src, (self.inp["l3_wdown"][r0:r0 + 128, :], dreg("l3_wdown")))
            self.quant_fp4(dstb.full, src.full, tmpa.full, tmpb.full, tmpr.full, tmps.full, 2816)
            for isl in range(22):
                pst = Bd.psumT()
                self.transpose((pst.ap[:, 0:128], pst.reg(0, 128, 0, 128)),
                               (dstb.ap[:, 128 * isl:128 * (isl + 1)],
                                dstb.reg(0, 128, 128 * isl, 128 * (isl + 1))))
                self.copy(ev.full, (pst.ap[:, 0:128], pst.reg(0, 128, 0, 128)))
                Bd.dma("sp", (w3q(2, 22)[isl, :, r0:r0 + 128], dreg("W3Q")), ev)

    def l3_attn(self, b):
        P, Bd = self, self.B
        Bd.ph_reset()
        xnb = [Bd.ph(128, T, BF16) for _ in range(DT)]
        xtm = [Bd.ph(128, 1024, BF16) for _ in range(16)]
        tb = [[Bd.ph(128, 512, BF16) for _ in range(16)] for _ in range(2)]
        xres = [Bd.ph(128, 512, F32) for _ in range(4)]
        for dt in range(DT):
            Bd.dma("sp", xnb[dt], self.dsl(self.XNB, "XNBs", b, dt, esz=2))
        for mt in range(16):
            for dt in range(DT):
                pst = Bd.psumT()
                self.transpose((pst.ap[:, 0:128], pst.reg(0, 128, 0, 128)),
                               (xnb[dt].ap[:, 128 * mt:128 * (mt + 1)],
                                xnb[dt].reg(0, 128, 128 * mt, 128 * (mt + 1))))
                self.copy((xtm[mt].ap[:, 128 * dt:128 * (dt + 1)],
                           xtm[mt].reg(0, 128, 128 * dt, 128 * (dt + 1))),
                          (pst.ap[:, 0:128], pst.reg(0, 128, 0, 128)))
        for c in range(4):
            c0 = 512 * c
            tbs = tb[c % 2]
            for mt in range(16):
                Bd.dma("pool", tbs[mt], (self.inp["l3_toep"][128 * mt:128 * (mt + 1),
                                                             c0:c0 + 512], dreg("l3_toep")))
            for dt in range(DT):
                ps = Bd.psum(dt % 4)
                P.mm(ps.full, [((xtm[mt].ap[:, 128 * dt:128 * (dt + 1)],
                                 xtm[mt].reg(0, 128, 128 * dt, 128 * (dt + 1))),
                                tbs[mt].full) for mt in range(16)])
                xrb = xres[dt % 4]
                Bd.dma("sp", xrb, self.dsl(self.X, "Xs", b, dt, c0, c0 + 512))
                self.tt(ALU.add, xrb.full, xrb.full, ps.full)
                Bd.dma("sp", self.dsl(self.X, "Xs", b, dt, c0, c0 + 512), xrb)

    def l3_ffn(self, b):
        P, Bd = self, self.B
        Bd.ph_reset()
        wgq = Bd.ph(128, 8 * 2816, BF16)
        wuq = Bd.ph(128, 8 * 2816, BF16)
        wdq = Bd.ph(128, 22 * 1024, BF16)
        xq = [Bd.ph(128, T, BF16) for _ in range(DT)]
        xc = [Bd.ph(128, 128, F32) for _ in range(2)]
        ttm = [Bd.ph(128, 1024, F32)] * 2
        tmpa = [Bd.ph(128, 1024, F32) for _ in range(2)]
        tmpb = [Bd.ph(128, 1024, F32) for _ in range(2)]
        tmpr = [Bd.ph(128, 64, F32)] * 2
        tmps = [Bd.ph(128, 64, F32)] * 2
        qtm = [Bd.ph(128, 1024, BF16)] * 2
        hq1 = [Bd.ph(128, 128, BF16) for _ in range(22)]
        hq = [hq1, hq1]
        sg = [Bd.ph(128, 512, F32)] * 2
        h32 = [Bd.ph(128, 512, F32)] * 2
        hqb = [Bd.ph(128, 512, BF16)] * 2
        xres = [Bd.ph(128, 128, F32) for _ in range(2)]
        w3q = lambda wi, a: self.W3Q[wi, :].rearrange("(a p n) -> a p n", a=a, p=128)
        for ds in range(8):
            Bd.dma("sp", (wgq.ap[:, 2816 * ds:2816 * (ds + 1)],
                          wgq.reg(0, 128, 2816 * ds, 2816 * (ds + 1))),
                   (w3q(0, 8)[ds], dreg("W3Q")))
            Bd.dma("sp", (wuq.ap[:, 2816 * ds:2816 * (ds + 1)],
                          wuq.reg(0, 128, 2816 * ds, 2816 * (ds + 1))),
                   (w3q(1, 8)[ds], dreg("W3Q")))
        for isl in range(22):
            Bd.dma("sp", (wdq.ap[:, 1024 * isl:1024 * (isl + 1)],
                          wdq.reg(0, 128, 1024 * isl, 1024 * (isl + 1))),
                   (w3q(2, 22)[isl], dreg("W3Q")))
        for mt in range(16):
            m2_ = mt % 2
            for dt in range(DT):
                xcb = xc[(mt * DT + dt) % 2]
                Bd.dma("sp", xcb, self.dsl(self.XN32, "XN32s", b, dt,
                                           128 * mt, 128 * (mt + 1)))
                psf = Bd.psum(5 + dt % 2)
                self.transpose((psf.ap[:, 0:128], psf.reg(0, 128, 0, 128)),
                               xcb.full, bf16=False)
                self.copy((ttm[m2_].ap[:, 128 * dt:128 * (dt + 1)],
                           ttm[m2_].reg(0, 128, 128 * dt, 128 * (dt + 1))),
                          (psf.ap[:, 0:128], psf.reg(0, 128, 0, 128)))
            self.quant_fp4(qtm[m2_].full, ttm[m2_].full, tmpa[m2_].full,
                           tmpb[m2_].full, tmpr[m2_].full, tmps[m2_].full, 1024)
            for dt in range(DT):
                pst = Bd.psumT()
                self.transpose((pst.ap[:, 0:128], pst.reg(0, 128, 0, 128)),
                               (qtm[m2_].ap[:, 128 * dt:128 * (dt + 1)],
                                qtm[m2_].reg(0, 128, 128 * dt, 128 * (dt + 1))))
                self.copy((xq[dt].ap[:, 128 * mt:128 * (mt + 1)],
                           xq[dt].reg(0, 128, 128 * mt, 128 * (mt + 1))),
                          (pst.ap[:, 0:128], pst.reg(0, 128, 0, 128)))
        ics = [(0, 512), (512, 512), (1024, 512), (1536, 512), (2048, 512), (2560, 256)]
        for mt in range(16):
            m2_ = mt % 2
            for ici, (i0, iw) in enumerate(ics):
                psg = Bd.psum(2 * (ici % 2))
                psu = Bd.psum(2 * (ici % 2) + 1)
                P.mm((psg.ap[:, 0:iw], psg.reg()),
                     [((xq[ds].ap[:, 128 * mt:128 * (mt + 1)],
                        xq[ds].reg(0, 128, 128 * mt, 128 * (mt + 1))),
                       (wgq.ap[:, 2816 * ds + i0:2816 * ds + i0 + iw],
                        wgq.reg(0, 128, 2816 * ds + i0, 2816 * ds + i0 + iw)))
                      for ds in range(8)])
                P.mm((psu.ap[:, 0:iw], psu.reg()),
                     [((xq[ds].ap[:, 128 * mt:128 * (mt + 1)],
                        xq[ds].reg(0, 128, 128 * mt, 128 * (mt + 1))),
                       (wuq.ap[:, 2816 * ds + i0:2816 * ds + i0 + iw],
                        wuq.reg(0, 128, 2816 * ds + i0, 2816 * ds + i0 + iw)))
                      for ds in range(8)])
                qi = ici % 2
                self.act((sg[qi].ap[:, 0:iw], sg[qi].reg(0, 128, 0, iw)),
                         (psg.ap[:, 0:iw], psg.reg()), AF.Silu)
                self.tt(ALU.mult, (h32[qi].ap[:, 0:iw], h32[qi].reg(0, 128, 0, iw)),
                        (sg[qi].ap[:, 0:iw], sg[qi].reg(0, 128, 0, iw)),
                        (psu.ap[:, 0:iw], psu.reg()))
                self.quant_fp4((hqb[qi].ap[:, 0:iw], hqb[qi].reg(0, 128, 0, iw)),
                               (h32[qi].ap[:, 0:iw], h32[qi].reg(0, 128, 0, iw)),
                               (tmpa[qi].ap[:, 0:iw], tmpa[qi].reg(0, 128, 0, iw)),
                               (tmpb[qi].ap[:, 0:iw], tmpb[qi].reg(0, 128, 0, iw)),
                               (tmpr[qi].ap[:, 0:iw // 16], tmpr[qi].reg(0, 128, 0, iw // 16)),
                               (tmps[qi].ap[:, 0:iw // 16], tmps[qi].reg(0, 128, 0, iw // 16)), iw)
                for j in range(iw // 128):
                    isl = i0 // 128 + j
                    pst = Bd.psumT()
                    self.transpose((pst.ap[:, 0:128], pst.reg(0, 128, 0, 128)),
                                   (hqb[qi].ap[:, 128 * j:128 * (j + 1)],
                                    hqb[qi].reg(0, 128, 128 * j, 128 * (j + 1))))
                    self.copy(hq[m2_][isl].full,
                              (pst.ap[:, 0:128], pst.reg(0, 128, 0, 128)))
            for dt in range(DT):
                psd = Bd.psum(4 + dt % 2)
                P.mm((psd.ap[:, 0:128], psd.reg()),
                     [((wdq.ap[:, 1024 * isl + 128 * dt:1024 * isl + 128 * (dt + 1)],
                        wdq.reg(0, 128, 1024 * isl, 1024 * (isl + 1))),
                       hq[m2_][isl].full) for isl in range(22)])
                xrb = xres[dt % 2]
                Bd.dma("sp", xrb, self.dsl(self.X, "Xs", b, dt,
                                           128 * mt, 128 * (mt + 1)))
                self.tt(ALU.add, xrb.full, xrb.full, (psd.ap[:, 0:128], psd.reg()))
                Bd.dma("sp", self.dsl(self.X, "Xs", b, dt, 128 * mt, 128 * (mt + 1)),
                       xrb)

    def l2_attn(self, b):
        P, Bd = self, self.B
        Bd.ph_reset()
        qf = [Bd.ph(128, T, BF16) for _ in range(DT)]
        kf = [Bd.ph(128, T, BF16) for _ in range(DT)]
        vt = [Bd.ph(128, 16 * 65, BF16) for _ in range(16)]
        ocb = [Bd.ph(128, T, BF16) for _ in range(DT)]
        gcols = Bd.ph(64, 48, F32)
        lnw = Bd.ph(128, 16, F32)
        mk0 = Bd.mark()
        xnb = [Bd.ph(128, T, BF16) for _ in range(DT)]
        wbig = Bd.ph(128, 8192, BF16)
        vtmp = Bd.ph(128, 512, BF16)
        for dt in range(DT):
            Bd.dma("sp", xnb[dt], self.dsl(self.XNB, "XNBs", b, dt, esz=2))
        for dst, wname in ((qf, "l2a_wq"), (kf, "l2a_wk"), (None, "l2a_wv")):
            for k in range(DT):
                Bd.dma("pool", wbig[:, 1024 * k:1024 * (k + 1)],
                       (self.inp[wname][128 * k:128 * (k + 1), :], dreg(wname)))
            for dt in range(DT):
                for c in range(4):
                    c0 = 512 * c
                    ps = Bd.psum(c % 4)
                    P.mm(ps.full, [(wbig[:, 1024 * k + 128 * dt:1024 * k + 128 * (dt + 1)],
                                    xnb[k][:, c0:c0 + 512]) for k in range(DT)])
                    if dst is not None:
                        self.act(dst[dt][:, c0:c0 + 512], ps.full, AF.Copy)
                    else:
                        self.act(vtmp.full, ps.full, AF.Copy)
                        pstt = Bd.psumT()
                        for hh in range(2):
                            h = 2 * dt + hh
                            o2 = 64 * hh
                            for j in range(4):
                                pt = 4 * c + j
                                self.transpose(
                                    (pstt.ap[0:128, 0:64], pstt.reg(0, 128, 0, 64)),
                                    (vtmp.ap[o2:o2 + 64, 128 * j:128 * (j + 1)],
                                     vtmp.reg(o2, o2 + 64, 128 * j, 128 * (j + 1))),
                                    bf16=True, kp=64, base=o2)
                                self.copy((vt[pt].ap[:, 65 * h:65 * h + 64],
                                           vt[pt].reg(0, 128, 65 * h, 65 * h + 64)),
                                          (pstt.ap[0:128, 0:64], pstt.reg(0, 128, 0, 64)))
        for pt in range(16):
            self.memset((vt[pt].ap[:, 64:16 * 65:65], vt[pt].reg(0, 128, 0, 16 * 65)), 1.0)
        Bd.reset_to(mk0)
        vf = [Bd.ph(128, T, BF16) for _ in range(DT)]
        wck = Bd.ph(64, 32 * 64, BF16)
        wcv = Bd.ph(64, 32 * 64, BF16)
        khc = Bd.ph(64, T, BF16)
        wgt = Bd.ph(64, 4, BF16)
        ckb = [Bd.ph(128, 127, BF16) for _ in range(8)]
        cvt = [Bd.ph(128, 65, BF16) for _ in range(16)]
        pcn = Bd.ph(128, 127, BF16)
        pcT = Bd.ph(128, 2048, BF16)
        den = Bd.ph(128, 1, F32)
        qm = Bd.ph(64, 1, F32)
        qmb = Bd.ph(64, 1, BF16)
        grow = Bd.ph(1, 4, F32)
        g4 = Bd.ph(4, 1, F32)
        Bd.dma("pool", (wck.ap.rearrange("q (j n) -> q j n", j=32), wck.reg()),
               (self.inp["l2a_wck"].rearrange("(j q) n -> q j n", q=64),
                dreg("l2a_wck")))
        Bd.dma("pool", (wcv.ap.rearrange("q (j n) -> q j n", j=32), wcv.reg()),
               (self.inp["l2a_wcv"].rearrange("(j q) n -> q j n", q=64),
                dreg("l2a_wcv")))
        Bd.dma("pool", wgt, (self.inp["l2a_wg"], dreg("l2a_wg")))
        for h in range(16):
            t2, o = h // 2, 64 * (h % 2)
            for pt in range(16):
                pstt = Bd.psumT()
                self.transpose((pstt.ap[0:64, 0:128], pstt.reg(0, 64, 0, 128)),
                               (vt[pt].ap[:, 65 * h:65 * h + 64],
                                vt[pt].reg(0, 128, 65 * h, 65 * h + 64)),
                               bf16=True, kp=128)
                self.copy((vf[t2].ap[o:o + 64, 128 * pt:128 * (pt + 1)],
                           vf[t2].reg(o, o + 64, 128 * pt, 128 * (pt + 1))),
                          (pstt.ap[0:64, 0:128], pstt.reg(0, 64, 0, 128)))
        impb = Bd.psum(6)
        first_imp = True
        for h in range(16):
            t2, o = h // 2, 64 * (h % 2)
            self.B.t("dve", lambda e, t2=t2, o=o: e.reduce_sum(
                qm.ap, qf[t2].ap[o:o + 64, :], axis=AX.X),
                r=[qf[t2].full], w=[qm.full])
            self.copy(qmb.full, qm.full)
            psg = Bd.psum(4)
            P.mm((psg.ap[0:4, 0:1], psg.reg()), [((wgt.ap, wgt.reg()), qmb.full)])
            self.act(g4.full, (psg.ap[0:4, 0:1], psg.reg()),
                     AF.Sigmoid, scale=1.0 / T, bias=self.vcol("l2a_bg", 0, P=4))
            psr = Bd.psum(5)
            P.mm((psr.ap[0:1, 0:4], psr.reg()),
                 [(g4.full, (self.identf.ap[0:4, 0:4], self.identf.reg(0, 4, 0, 4)))])
            self.copy(grow.full, (psr.ap[0:1, 0:4], psr.reg()))
            for j in range(3):
                psr2 = Bd.psum(4)
                P.mm((psr2.ap[0:64, 0:1], psr2.reg()),
                     [((self.ones_row_f.ap[0:1, 0:64], self.ones_row_f.reg(0, 1, 0, 64)),
                       (grow.ap[0:1, j:j + 1], grow.reg(0, 1, j, j + 1)))])
                self.copy((gcols.ap[:, 3 * h + j:3 * h + j + 1],
                           gcols.reg(0, 64, 3 * h + j, 3 * h + j + 1)),
                          (psr2.ap[0:64, 0:1], psr2.reg()))
            for which, wc, dstt in ((0, wck, ckb), (1, wcv, None)):
                psc = Bd.psum(4)
                kvsrc = kf if which == 0 else vf
                self.copy(khc.full, (kvsrc[t2].ap[o:o + 64, :],
                                     kvsrc[t2].reg(o, o + 64, 0, T)))
                pairs = []
                for j in range(32):
                    lhs = (wc.ap[0:64, 64 * j:64 * (j + 1)],
                           wc.reg(0, 64, 64 * j, 64 * (j + 1)))
                    rhs = (khc.ap[0:64, j:j + 16 * 126 + 1:16],
                           khc.reg(0, 64, 0, T))
                    pairs.append((lhs, rhs))
                P.mm((psc.ap[0:64, 0:127], psc.reg()), pairs)
                if which == 0:
                    self.ts((ckb[t2].ap[o:o + 64, 0:127],
                             ckb[t2].reg(o, o + 64, 0, 127)),
                            (psc.ap[0:64, 0:127], psc.reg()),
                            self.vcol("l2a_bck", 0, P=64), op0=ALU.add)
                else:
                    self.ts(pcn[0:64, 0:127], (psc.ap[0:64, 0:127], psc.reg()),
                            self.vcol("l2a_bcv", 0, P=64), op0=ALU.add)
                    pstc = Bd.psumT()
                    self.transpose((pstc.ap[0:127, 0:64], pstc.reg(0, 127, 0, 64)),
                                   pcn[0:64, 0:127], bf16=True, kp=64)
                    self.copy((cvt[h].ap[0:127, 0:64], cvt[h].reg(0, 127, 0, 64)),
                              (pstc.ap[0:127, 0:64], pstc.reg(0, 127, 0, 64)))
                    self.memset((cvt[h].ap[0:127, 64:65], cvt[h].reg(0, 127, 64, 65)), 1.0)
            for it in range(16):
                psS = Bd.psum(it % 2)
                P.mm((psS.ap[:, 0:127], psS.reg()),
                     [((qf[t2].ap[o:o + 64, 128 * it:128 * (it + 1)],
                        qf[t2].reg(o, o + 64, 128 * it, 128 * (it + 1))),
                       (ckb[t2].ap[o:o + 64, 0:127], ckb[t2].reg(o, o + 64, 0, 127)))])
                self.act(pcn[0:128, 0:127], (psS.ap[:, 0:127], psS.reg()),
                         AF.Exp, scale=0.125, accum=den.full)
                self.B.t("dve", lambda e: e.reciprocal(den.ap, den.ap),
                         r=[den.full], w=[den.full])
                self.ts(pcn[0:128, 0:127], pcn[0:128, 0:127], den.full, op0=ALU.mult)
                if DBG and b == 0 and h == 15 and it == 15:
                    self.dbg("l2_pcn_last", pcn)
                    self.dbg("l2_den_last", den)
                pstp = Bd.psumT()
                self.transpose((pstp.ap[0:127, 0:128], pstp.reg(0, 127, 0, 128)),
                               pcn[0:128, 0:127], bf16=True, kp=128)
                self.copy((pcT.ap[0:127, 128 * it:128 * (it + 1)],
                           pcT.reg(0, 127, 128 * it, 128 * (it + 1))),
                          (pstp.ap[0:127, 0:128], pstp.reg(0, 127, 0, 128)))
                P.mm((impb.ap[0:1, 0:127], impb.reg()),
                     [((self.ones_col_b.ap, self.ones_col_b.reg()),
                       pcn[0:128, 0:127])], start=first_imp, stop=(h == 15 and it == 15))
                first_imp = False
            for c in range(4):
                c0 = 512 * c
                pso = Bd.psum(2 + c % 2)
                P.mm((pso.ap[0:64, :], pso.reg()),
                     [((cvt[h].ap[0:127, 0:64], cvt[h].reg(0, 127, 0, 64)),
                       (pcT.ap[0:127, c0:c0 + 512], pcT.reg(0, 127, c0, c0 + 512)))])
                self.act((ocb[t2].ap[o:o + 64, c0:c0 + 512],
                          ocb[t2].reg(o, o + 64, c0, c0 + 512)),
                         (pso.ap[0:64, :], pso.reg()), AF.Copy)
        bi = Bd.ph(1, 127, F32)
        ohall = Bd.ph(1, 127, F32)
        onesr = Bd.ph(1, 127, F32)
        cum = Bd.ph(1, 127, F32)
        fo = Bd.ph(1, 127, F32)
        mrow = Bd.ph(1, 1, F32)
        ohc = Bd.ph(127, 1, BF16)
        wcol = Bd.ph(128, 1, F32)
        ecl = Bd.ph(127, 16 * 128, BF16)
        cE = self.lazy_const("c_Eb", lambda: np.ascontiguousarray(_l2_E().astype(BF)))
        Bd.dma("sp", ecl, (cE, dreg("c_Eb")))
        self.copy(bi.full, (impb.ap[0:1, 0:127], impb.reg()))
        self.memset(onesr.full, 1.0)
        self.memset(ohall.full, 0.0)
        for it16 in range(16):
            self.B.t("dve", lambda e: e.reduce_max(mrow.ap, bi.ap, axis=AX.X),
                     r=[bi.full], w=[mrow.full])
            self.ts(fo, bi, (mrow.ap, mrow.reg()), None, op0=ALU.is_ge, eng="dve")
            self.B.t("dve", lambda e: e.tensor_tensor_scan(
                cum.ap, onesr.ap, fo.ap, 0.0, op0=ALU.mult, op1=ALU.max),
                r=[onesr.full, fo.full], w=[cum.full])
            self.copy((fo.ap[0:1, 0:1], fo.reg(0, 1, 0, 1)),
                      (cum.ap[0:1, 0:1], cum.reg(0, 1, 0, 1)))
            self.tt(ALU.subtract, (fo.ap[0:1, 1:127], fo.reg(0, 1, 1, 127)),
                    (cum.ap[0:1, 1:127], cum.reg(0, 1, 1, 127)),
                    (cum.ap[0:1, 0:126], cum.reg(0, 1, 0, 126)))
            self.tt(ALU.add, ohall.full, ohall.full, fo.full)
            self.stt(bi, fo, -1e38, bi, op0=ALU.mult, op1=ALU.add, eng="dve")
        psoc = Bd.psum(4)
        P.mm((psoc.ap[0:127, 0:1], psoc.reg()),
             [((ohall.ap, ohall.reg()), (self.ones_row_f.ap[0:1, 0:1],
                                         self.ones_row_f.reg(0, 1, 0, 1)))])
        self.copy(ohc.full, (psoc.ap[0:127, 0:1], psoc.reg()))
        for pt in range(16):
            psw = Bd.psum(5)
            P.mm((psw.ap[:, 0:1], psw.reg()),
                 [((ecl.ap[0:127, 128 * pt:128 * (pt + 1)],
                    ecl.reg(0, 127, 128 * pt, 128 * (pt + 1))), ohc.full)])
            self.copy(wcol.full, (psw.ap[:, 0:1], psw.reg()))
            if DBG and b == 0 and pt == 0:
                self.dbg("l2_ohc", ohc)
                self.dbg("l2_wcol0", wcol)
            self.ts((lnw.ap[:, pt:pt + 1], lnw.reg(0, 128, pt, pt + 1)),
                    wcol.full, 1e-30, None, op0=ALU.max, eng="dve")
            self.act((lnw.ap[:, pt:pt + 1], lnw.reg(0, 128, pt, pt + 1)),
                     (lnw.ap[:, pt:pt + 1], lnw.reg(0, 128, pt, pt + 1)), AF.Ln)
            self.ts(wcol.full, wcol.full, 0.5, -1e38, op0=ALU.is_le, op1=ALU.mult)
            self.tt(ALU.add, (lnw.ap[:, pt:pt + 1], lnw.reg(0, 128, pt, pt + 1)),
                    (lnw.ap[:, pt:pt + 1], lnw.reg(0, 128, pt, pt + 1)), wcol.full)
        if DBG and b == 0:
            self.dbg("l2_ckb7", ckb[7])
            self.dbg("l2_cvt15", cvt[15])
            self.dbg("l2_pcT", pcT)
            self.dbg("l2_gcols", gcols)
            self.dbg("l2_ohall", ohall)
            self.dbg("l2_lnw", lnw)
            self.dbg("l2_ocb0", ocb[0])
        Bd.reset_to(mk0)
        ofin = [Bd.ph(128, T, BF16) for _ in range(DT)]
        wo8 = Bd.ph(128, 8192, BF16)
        pw = [Bd.ph(128, 512, BF16) for _ in range(2)]
        pw2 = [Bd.ph(128, 512, BF16) for _ in range(2)]
        denr = Bd.ph(1, 512, F32)
        reps = Bd.ph(64, 512, F32)
        repw = Bd.ph(64, 512, F32)
        osl = Bd.ph(64, 512, F32)
        owl = Bd.ph(64, 512, F32)
        xres = Bd.ph(128, T, F32)
        for k in range(DT):
            Bd.dma("pool", wo8[:, 1024 * k:1024 * (k + 1)],
                   (self.inp["l2a_wo"][128 * k:128 * (k + 1), :], dreg("l2a_wo")))
        for h in range(16):
            t2, o = h // 2, 64 * (h % 2)
            for ic in range(4):
                i0 = 512 * ic
                psel = Bd.psum(2)
                pwin = Bd.psum(3)
                for pt in range(16):
                    psS = Bd.psum(pt % 2)
                    P.mm(psS.full,
                         [((kf[t2].ap[o:o + 64, 128 * pt:128 * (pt + 1)],
                            kf[t2].reg(o, o + 64, 128 * pt, 128 * (pt + 1))),
                           (qf[t2].ap[o:o + 64, i0:i0 + 512],
                            qf[t2].reg(o, o + 64, i0, i0 + 512)))])
                    pb = pw[pt % 2]
                    self.act(pb.full, psS.full, AF.Exp, scale=0.125,
                             bias=(lnw.ap[:, pt:pt + 1], lnw.reg(0, 128, pt, pt + 1)))
                    P.mm((psel.ap[0:65, :], psel.reg()),
                         [((vt[pt].ap[:, 65 * h:65 * (h + 1)],
                            vt[pt].reg(0, 128, 65 * h, 65 * (h + 1))), pb.full)],
                         start=(pt == 0), stop=(pt == 15))
                    if pt >= 12:
                        pb2 = pw2[pt % 2]
                        self.act(pb2.full, psS.full, AF.Exp, scale=0.125)
                        P.mm((pwin.ap[0:65, :], pwin.reg()),
                             [((vt[pt].ap[:, 65 * h:65 * (h + 1)],
                                vt[pt].reg(0, 128, 65 * h, 65 * (h + 1))), pb2.full)],
                             start=(pt == 12), stop=(pt == 15))
                for which, pso, dst in ((0, psel, osl), (1, pwin, owl)):
                    self.B.t("dve", lambda e, pso=pso: e.reciprocal(
                        denr.ap, pso.ap[64:65, :]), r=[pso.full], w=[denr.full])
                    psr = Bd.psum(4 + which)
                    P.mm((psr.ap[0:64, :], psr.reg()),
                         [((self.ones_row_f.ap[0:1, 0:64],
                            self.ones_row_f.reg(0, 1, 0, 64)), denr.full)])
                    self.copy(reps.full, (psr.ap[0:64, :], psr.reg()))
                    self.tt(ALU.mult, dst.full, (pso.ap[0:64, :], pso.reg()), reps.full)
                if DBG and b == 0 and h == 0 and ic == 0:
                    self.dbg("l2_osl00", osl)
                    self.dbg("l2_owl00", owl)
                self.ts(repw.full, (ocb[t2].ap[o:o + 64, i0:i0 + 512],
                                    ocb[t2].reg(o, o + 64, i0, i0 + 512)),
                        (gcols.ap[:, 3 * h:3 * h + 1], gcols.reg(0, 64, 3 * h, 3 * h + 1)),
                        None, op0=ALU.mult)
                self.stt(repw, osl.full, (gcols.ap[:, 3 * h + 1:3 * h + 2],
                                          gcols.reg(0, 64, 3 * h + 1, 3 * h + 2)),
                         repw.full, op0=ALU.mult, op1=ALU.add)
                self.stt((ofin[t2].ap[o:o + 64, i0:i0 + 512],
                          ofin[t2].reg(o, o + 64, i0, i0 + 512)),
                         owl.full, (gcols.ap[:, 3 * h + 2:3 * h + 3],
                                    gcols.reg(0, 64, 3 * h + 2, 3 * h + 3)),
                         repw.full, op0=ALU.mult, op1=ALU.add)
        if DBG and b == 0:
            self.dbg("l2_ofin0", ofin[0])
        for dt in range(DT):
            Bd.dma("sp", xres, self.dsl(self.X, "Xs", b, dt))
            for c in range(4):
                c0 = 512 * c
                ps = Bd.psum(6)
                P.mm(ps.full, [(wo8[:, 1024 * k + 128 * dt:1024 * (k) + 128 * (dt + 1)],
                                ofin[k][:, c0:c0 + 512]) for k in range(DT)])
                self.tt(ALU.add, xres[:, c0:c0 + 512], xres[:, c0:c0 + 512], ps.full)
            Bd.dma("sp", self.dsl(self.X, "Xs", b, dt), xres)

    def l2_ffn(self, b):
        P, Bd = self, self.B
        Bd.ph_reset()
        xnb = [Bd.ph(128, T, BF16) for _ in range(DT)]
        gw = Bd.ph(128, 64, BF16)
        lg = Bd.ph(8, T, F32)
        Wsb = Bd.ph(8, T, F32)
        oh8 = Bd.ph(8, 1024, F32)
        Lt = Bd.ph(128, 8, F32)
        m1 = Bd.ph(128, 1, F32)
        nm1 = Bd.ph(128, 1, F32)
        m2 = Bd.ph(128, 1, F32)
        rd = Bd.ph(128, 1, F32)
        msk = Bd.ph(128, 8, F32)
        W8 = Bd.ph(128, 8, F32)
        acc = [Bd.ph(128, 1024, F32) for _ in range(DT)]
        zb = [Bd.ph(128, 1024, BF16) for _ in range(16)]
        sb = [Bd.ph(128, 1024, BF16) for _ in range(16)]
        w1b = [Bd.ph(128, 1024, BF16) for _ in range(2)]
        w2b = [Bd.ph(128, 1024, BF16) for _ in range(2)]
        wgb = [Bd.ph(128, 2048, BF16) for _ in range(2)]
        w3b = [Bd.ph(128, 2048, BF16) for _ in range(2)]
        wrp = [Bd.ph(128, 1024, F32) for _ in range(2)]
        t1f = [Bd.ph(128, 512, F32) for _ in range(2)]
        ysb = [Bd.ph(128, 512, F32) for _ in range(2)]
        yt = [Bd.ph(128, 512, F32) for _ in range(2)]
        xres = [Bd.ph(128, 1024, F32) for _ in range(2)]
        for dt in range(DT):
            Bd.dma("sp", xnb[dt], self.dsl(self.XNB, "XNBs", b, dt, esz=2))
        Bd.dma("pool", (gw.ap.rearrange("p (k n) -> p k n", k=8), gw.reg()),
               (self.inp["l2f_gw"].rearrange("(k p) n -> p k n", p=128),
                dreg("l2f_gw")))
        for c in range(4):
            c0 = 512 * c
            psl = Bd.psum(4)
            P.mm((psl.ap[0:8, :], psl.reg()),
                 [((gw.ap[:, 8 * k:8 * (k + 1)], gw.reg(0, 128, 8 * k, 8 * (k + 1))),
                   xnb[k][:, c0:c0 + 512]) for k in range(DT)])
            self.ts((lg.ap[0:8, c0:c0 + 512], lg.reg(0, 8, c0, c0 + 512)),
                    (psl.ap[0:8, :], psl.reg()),
                    self.vcol("l2f_gateb", 0, P=8), op0=ALU.add)
        for mt in range(16):
            psf = Bd.psum(5)
            self.transpose((psf.ap[0:128, 0:8], psf.reg()),
                           (lg.ap[0:8, 128 * mt:128 * (mt + 1)],
                            lg.reg(0, 8, 128 * mt, 128 * (mt + 1))), bf16=False, kp=8)
            self.copy(Lt.full, (psf.ap[0:128, 0:8], psf.reg()))
            self.red(m1, Lt, op=ALU.max)
            self.ts(nm1, m1, -1.0, None, op0=ALU.mult)
            self.ts(msk, Lt, m1.full, -1e38, op0=ALU.is_ge, op1=ALU.mult)
            self.tt(ALU.add, msk, msk, Lt)
            self.red(m2, msk, op=ALU.max)
            self.tt(ALU.subtract, rd, m2, m1)
            self.act(rd, rd, AF.Exp)
            self.ts(rd, rd, 1.0, None, op0=ALU.add)
            self.B.t("dve", lambda e: e.reciprocal(rd.ap, rd.ap),
                     r=[rd.full], w=[rd.full])
            self.act(W8, Lt, AF.Exp, bias=nm1.full)
            self.ts(msk, Lt, m2.full, 1.0, op0=ALU.is_ge, op1=ALU.mult)
            self.tt(ALU.mult, W8, W8, msk)
            self.ts(W8, W8, rd.full, 0.0, op0=ALU.mult, op1=ALU.add)
            psf2 = Bd.psum(6)
            self.transpose((psf2.ap[0:8, 0:128], psf2.reg()), W8.full,
                           bf16=False, kp=128)
            self.copy((Wsb.ap[0:8, 128 * mt:128 * (mt + 1)],
                       Wsb.reg(0, 8, 128 * mt, 128 * (mt + 1))),
                      (psf2.ap[0:8, 0:128], psf2.reg()))
        oh8np = np.zeros((8, 1024), np.float32)
        for e in range(8):
            oh8np[e, 128 * e:128 * (e + 1)] = 1.0
        Bd.dma("sp", oh8, (self.lazy_const("c_oh8", lambda: oh8np), dreg("c_oh8")))
        for c2 in range(2):
            q0 = 1024 * c2
            for dt in range(DT):
                self.memset(acc[dt], 0.0)
            for e in range(8):
                psw = Bd.psum(5, 2)
                for hh in range(2):
                    P.mm((psw.ap[:, 512 * hh:512 * (hh + 1)],
                          psw.reg(0, 128, 512 * hh, 512 * (hh + 1))),
                         [((oh8.ap[:, 128 * e:128 * (e + 1)],
                            oh8.reg(0, 8, 128 * e, 128 * (e + 1))),
                           (Wsb.ap[0:8, q0 + 512 * hh:q0 + 512 * (hh + 1)],
                            Wsb.reg(0, 8, q0 + 512 * hh, q0 + 512 * (hh + 1))))])
                self.copy(wrp[e % 2], psw)
                for ft in range(16):
                    wa_, wb_ = w1b[ft % 2], w2b[ft % 2]
                    Bd.dma("pool", (wa_.ap, wa_.reg()),
                           (self.inp["l2f_w1"].rearrange("e (kt p) m -> e p kt m", p=128)
                            [e, :, :, 128 * ft:128 * (ft + 1)], dreg("l2f_w1")))
                    Bd.dma("pool", (wb_.ap, wb_.reg()),
                           (self.inp["l2f_w2"].rearrange("e (kt p) m -> e p kt m", p=128)
                            [e, :, :, 128 * ft:128 * (ft + 1)], dreg("l2f_w2")))
                    for hh in range(2):
                        h0 = q0 + 512 * hh
                        psa = Bd.psum(hh % 2)
                        psb_ = Bd.psum(2 + hh % 2)
                        P.mm(psa.full, [((wa_.ap[:, 128 * k:128 * (k + 1)],
                                          wa_.reg(0, 128, 128 * k, 128 * (k + 1))),
                                         xnb[k][:, h0:h0 + 512]) for k in range(DT)])
                        P.mm(psb_.full, [((wb_.ap[:, 128 * k:128 * (k + 1)],
                                           wb_.reg(0, 128, 128 * k, 128 * (k + 1))),
                                          xnb[k][:, h0:h0 + 512]) for k in range(DT)])
                        self.ts(t1f[hh], psa.full, self.vcol(f"l2f_b1_{e}", ft),
                                op0=ALU.add)
                        self.stt((zb[ft].ap[:, 512 * hh:512 * (hh + 1)],
                                  zb[ft].reg(0, 128, 1024 * hh // 2,
                                             1024 * (hh + 1) // 2)),
                                 psb_.full, self.vcol(f"l2f_b2_{e}", ft),
                                 t1f[hh].full, op0=ALU.add, op1=ALU.mult)
                for gt in range(16):
                    wg_ = wgb[gt % 2]
                    Bd.dma("pool", (wg_.ap, wg_.reg()),
                           (self.inp["l2f_wg"].rearrange("e (kt p) m -> e p kt m", p=128)
                            [e, :, :, 128 * gt:128 * (gt + 1)], dreg("l2f_wg")))
                    for hh in range(2):
                        psg = Bd.psum(4 + hh % 2)
                        P.mm(psg.full, [((wg_.ap[:, 128 * k:128 * (k + 1)],
                                          wg_.reg(0, 128, 128 * k, 128 * (k + 1))),
                                         (zb[k].ap[:, 512 * hh:512 * (hh + 1)],
                                          zb[k].reg(0, 128, 1024 * hh // 2,
                                                    1024 * (hh + 1) // 2)))
                                        for k in range(16)])
                        self.act((sb[gt].ap[:, 512 * hh:512 * (hh + 1)],
                                  sb[gt].reg(0, 128, 1024 * hh // 2,
                                             1024 * (hh + 1) // 2)),
                                 psg.full, AF.Silu, bias=self.vcol(f"l2f_bg_{e}", gt))
                for dt in range(DT):
                    w3_ = w3b[dt % 2]
                    Bd.dma("pool", (w3_.ap, w3_.reg()),
                           (self.inp["l2f_w3"].rearrange("e (kt p) m -> e p kt m", p=128)
                            [e, :, :, 128 * dt:128 * (dt + 1)], dreg("l2f_w3")))
                    for hh in range(2):
                        psy = Bd.psum(hh % 2)
                        P.mm(psy.full, [((w3_.ap[:, 128 * k:128 * (k + 1)],
                                          w3_.reg(0, 128, 128 * k, 128 * (k + 1))),
                                         (sb[k].ap[:, 512 * hh:512 * (hh + 1)],
                                          sb[k].reg(0, 128, 1024 * hh // 2,
                                                    1024 * (hh + 1) // 2)))
                                        for k in range(16)])
                        self.ts(ysb[hh], psy.full, self.vcol(f"l2f_b3_{e}", dt),
                                op0=ALU.add)
                        self.tt(ALU.mult, yt[hh], ysb[hh],
                                (wrp[e % 2].ap[:, 512 * hh:512 * (hh + 1)],
                                 wrp[e % 2].reg(0, 128, 512 * hh, 512 * (hh + 1))),
                                eng="pool")
                        self.tt(ALU.add, (acc[dt].ap[:, 512 * hh:512 * (hh + 1)],
                                          acc[dt].reg(0, 128, 512 * hh, 512 * (hh + 1))),
                                (acc[dt].ap[:, 512 * hh:512 * (hh + 1)],
                                 acc[dt].reg(0, 128, 512 * hh, 512 * (hh + 1))),
                                yt[hh].full, eng="pool")
            for dt in range(DT):
                xrb = xres[dt % 2]
                Bd.dma("sp", xrb, self.dsl(self.X, "Xs", b, dt, q0, q0 + 1024))
                self.tt(ALU.add, xrb.full, xrb.full, acc[dt].full)
                Bd.dma("sp", self.dsl(self.X, "Xs", b, dt, q0, q0 + 1024), xrb)

    def l0_ffn(self, b):
        P, Bd = self, self.B
        Bd.ph_reset()
        xnb = [Bd.ph(128, T, BF16) for _ in range(DT)]
        w2 = Bd.ph(128, 32 * 1024, BF16)
        w1b = [Bd.ph(128, 1024, BF16) for _ in range(2)]
        hsb = Bd.ph(128, 32 * 512, BF16)
        ssb = [Bd.ph(128, 512, F32) for _ in range(DT)]
        sb16 = [Bd.ph(128, 512, BF16) for _ in range(DT)]
        sq16 = [Bd.ph(128, 512, BF16) for _ in range(DT)]
        mean = Bd.ph(1, 512, F32)
        e2 = Bd.ph(1, 512, F32)
        inv = Bd.ph(1, 512, F32)
        msr = Bd.ph(1, 512, F32)
        t1 = Bd.ph(128, 512, F32)
        xres = Bd.ph(128, 512, F32)
        xn32c = Bd.ph(128, 512, F32)
        for dt in range(DT):
            Bd.dma("sp", xnb[dt], self.dsl(self.XNB, "XNBs", b, dt, esz=2))
        for f in range(32):
            Bd.dma("pool", w2[:, 1024 * f:1024 * (f + 1)],
                   (self.inp["l0f_w2"][128 * f:128 * (f + 1), :], dreg("l0f_w2")))
        onesb = self.ones_col_b.full
        for c in range(4):
            c0 = 512 * c
            for f in range(32):
                wb = w1b[f % 2]
                Bd.dma("pool", wb,
                       (self.inp["l0f_w1"].rearrange("(kt p) m -> p kt m", p=128)[:, :, 128 * f:128 * (f + 1)],
                        dreg("l0f_w1")))
                ps_h = Bd.psum(f % 2)
                P.mm(ps_h.full, [(wb[:, 128 * k:128 * (k + 1)],
                                  xnb[k][:, c0:c0 + 512]) for k in range(DT)])
                self.act(hsb[:, 512 * f:512 * (f + 1)], ps_h.full, AF.Relu,
                         bias=self.vcol("l0f_b1", f))
            for dt in range(DT):
                ps_o = Bd.psum(2 + dt % 2)
                P.mm(ps_o.full, [(w2[:, 1024 * f + 128 * dt:1024 * f + 128 * (dt + 1)],
                                  hsb[:, 512 * f:512 * (f + 1)])
                                 for f in range(32)])
                Bd.dma("sp", xn32c, self.dsl(self.XN32, "XN32s", b, dt, c0, c0 + 512))
                self.stt(ssb[dt].full, ps_o.full, self.vcol("l0f_b2", dt),
                         (xn32c.ap, xn32c.reg()), op0=ALU.add, op1=ALU.add)
                self.copy(sb16[dt].full, ssb[dt].full)
                self.tt(ALU.mult, sq16[dt].full, sb16[dt].full, sb16[dt].full)
            ps_s, ps_q = Bd.psum(4), Bd.psum(5)
            P.mm((ps_s.ap[0:1, :], ps_s.reg()), [(onesb, sb16[dt].full) for dt in range(DT)])
            P.mm((ps_q.ap[0:1, :], ps_q.reg()), [(onesb, sq16[dt].full) for dt in range(DT)])
            self.ts(mean.full, (ps_s.ap[0:1, :], ps_s.reg()), 1.0 / D, op0=ALU.mult)
            self.ts(e2.full, (ps_q.ap[0:1, :], ps_q.reg()), 1.0 / D, op0=ALU.mult)
            self.tt(ALU.mult, inv.full, mean.full, mean.full)
            self.tt(ALU.subtract, inv.full, e2.full, inv.full)
            self.act(inv.full, inv.full, AF.Ln, bias=1e-5)
            self.act(inv.full, inv.full, AF.Exp, scale=-0.5)
            self.stt(msr.full, mean.full, -1.0, inv.full, op0=ALU.mult, op1=ALU.mult)
            psA_, psB_ = Bd.psum(4), Bd.psum(5)
            P.mm(psA_.full, [((self.ones_row_f.ap, self.ones_row_f.reg()), inv.full)])
            P.mm(psB_.full, [((self.ones_row_f.ap, self.ones_row_f.reg()), msr.full)])
            for dt in range(DT):
                self.tt(ALU.mult, t1.full, ssb[dt].full, psA_.full)
                self.tt(ALU.add, t1.full, t1.full, psB_.full)
                self.ts(t1.full, t1.full, self.vcol("l0f_gamma", dt),
                        self.vcol("l0f_beta", dt), op0=ALU.mult, op1=ALU.add)
                if DBG and b == 0 and c == 0 and dt == 0:
                    self.dbg("ffn_out00", t1)
                Bd.dma("sp", xres, self.dsl(self.X, "Xs", b, dt, c0, c0 + 512))
                self.tt(ALU.add, xres.full, xres.full, t1.full)
                Bd.dma("sp", self.dsl(self.X, "Xs", b, dt, c0, c0 + 512), xres)
            if DBG and b == 0 and c == 0:
                self.dbg("ffn_h0", Tile(hsb.ap[:, 0:512], "S", 0, hsb.byte0, BF16, 128, 512))
                self.dbg("ffn_s0", ssb[0])
                self.dbg("ffn_mean", mean)
                self.dbg("ffn_inv", inv)


DBG = []


def build_custom(stage_fns):
    nc = bass.Bass("TRN2", target_bir_lowering=False, debug=False)
    with ExitStack() as S:
        blk = S.enter_context(nc.Block())
        P = Prog(nc, S)
        P.load_consts()
        for fn in stage_fns:
            fn(P)
        P.B.finish(blk)
    return nc


def build_program(n_blocks=4, dump=None):
    nc = bass.Bass("TRN2", target_bir_lowering=False, debug=False)
    with ExitStack() as S:
        blk = S.enter_context(nc.Block())
        P = Prog(nc, S, n_blocks)
        P.load_consts()
        if n_blocks >= 4:
            P.l3_prep_weights()
        for i in range(n_blocks):
            for b in range(B):
                P.ln_entry(b)
                if i == 0:
                    P.l0_attn(b)
                    P.l0_ffn(b)
                elif i == 1:
                    P.l1_attn(b)
                    P.l1_ffn(b)
                elif i == 2:
                    P.l2_attn(b)
                    P.l2_ffn(b)
                elif i == 3:
                    P.l3_attn(b)
                    P.l3_ffn(b)
        for b in range(B):
            for dt in range(DT):
                P.B.dma("sp", (P.xout[b, 128 * dt:128 * (dt + 1), :], dreg("xout")),
                        P.dsl(P.X, "Xs", b, dt))
        if dump:
            for j, (dr, name, shape_esz) in enumerate(dump):
                pass
        P.B.finish(blk)
    return nc


_PROG_CACHE = {}


def kernel(**inputs):
    nc = build_program(4)
    in_maps = [stage_inputs(inputs, c) for c in range(NCORES)]
    res = run_bass_kernel_spmd(nc, in_maps, core_ids=list(range(NCORES)))
    outs = []
    for c in range(NCORES):
        o = np.asarray(res.results[c]["xout"], np.float32)
        outs.append(o.transpose(0, 2, 1))
    return np.ascontiguousarray(np.concatenate(outs, axis=0))
